# version 02
# baseline (speedup 1.0000x reference)
import numpy as np
from contextlib import ExitStack
import concourse.bass as bass
import concourse.mybir as mybir
from concourse.bass_utils import run_bass_kernel_spmd

F32 = mybir.dt.float32
BF16 = mybir.dt.bfloat16
AF = mybir.ActivationFunctionType
ALU = mybir.AluOpType
AX = mybir.AxisListType.X

L, D, DI, NS, RK, OUTF = 4096, 1024, 2048, 16, 64, 4192
NW, WIN, NJ, NK, NTT = 8, 512, 16, 8, 32
EPS = 1e-5
ALL_PHASES = (0, 1, 2, 3, 4, 5, 6, 7)


def build_nc(phases=ALL_PHASES, debug=()):
    nc = bass.Bass("TRN2", target_bir_lowering=False, debug=False)

    def dram(name, shape, dtype):
        k = "ExternalOutput" if name in debug else "Internal"
        return nc.dram_tensor(name, shape, dtype, kind=k).ap()

    x = nc.dram_tensor("x", [L, D], F32, kind="ExternalInput").ap()
    WinT = nc.dram_tensor("WinT", [D, OUTF], F32, kind="ExternalInput").ap()
    WdtT = nc.dram_tensor("WdtT", [RK, DI], F32, kind="ExternalInput").ap()
    WoutT = nc.dram_tensor("WoutT", [DI, D], F32, kind="ExternalInput").ap()
    CBT = nc.dram_tensor("CBT", [D, D], F32, kind="ExternalInput").ap()
    A_logr = nc.dram_tensor("A_logr", [DI, NS], F32, kind="ExternalInput").ap()
    bdt_c = nc.dram_tensor("bdt_c", [128, NJ], F32, kind="ExternalInput").ap()
    convw_c = nc.dram_tensor("convw_c", [128, NJ * 4], F32, kind="ExternalInput").ap()
    convb_c = nc.dram_tensor("convb_c", [128, NJ], F32, kind="ExternalInput").ap()
    D_c = nc.dram_tensor("D_c", [128, NJ], F32, kind="ExternalInput").ap()
    selPC = nc.dram_tensor("selPC", [NS, NS * 128], F32, kind="ExternalInput").ap()
    selNC = nc.dram_tensor("selNC", [NS, NS * 128], F32, kind="ExternalInput").ap()
    ident = nc.dram_tensor("ident", [128, 128], F32, kind="ExternalInput").ap()
    ones1 = nc.dram_tensor("ones1", [1, 128], F32, kind="ExternalInput").ap()
    lnrow = nc.dram_tensor("lnrow", [1, 4 * D], F32, kind="ExternalInput").ap()
    out = nc.dram_tensor("out", [L, D], F32, kind="ExternalOutput").ap()

    h_WinTb = dram("h_WinTb", [D, OUTF], BF16)
    h_WdtTb = dram("h_WdtTb", [RK, DI], BF16)
    h_WoutTb = dram("h_WoutTb", [DI, D], BF16)
    h_CBTb = dram("h_CBTb", [D, D], BF16)
    h_xn = dram("h_xn", [L, D], F32)
    h_xnb = dram("h_xnb", [L, D], BF16)
    h_xs = dram("h_xs", [DI, L], F32)
    h_sz = dram("h_sz", [DI, L], BF16)
    h_u = dram("h_u", [DI, L], BF16)
    h_BT = dram("h_BT", [NS, L], BF16)
    h_CT = dram("h_CT", [NS, L], BF16)
    h_dtpT = dram("h_dtpT", [RK, L], BF16)
    h_dtneg = dram("h_dtneg", [DI, L], F32)
    h_dtu = dram("h_dtu", [DI, L], BF16)
    h_Bbc = dram("h_Bbc", [NS * 128, L], BF16)
    h_Cbc = dram("h_Cbc", [NS * 128, L], BF16)
    h_yg = dram("h_yg", [DI, L], BF16)


    S = {"qS": 0, "qG": 0, "qT": 0, "sv": 0, "sa": 0, "sp": 0, "so": 0}
    sems = {}

    def inc(ins, name, n=1):
        S[name] += n
        ins.then_inc(sems[name], n)
        return S[name]

    def dma(eng, qname, out_, in_, **kw):
        return inc(eng.dma_start(out=out_, in_=in_, **kw), qname, 16)

    def dmat(eng, out_, in_):
        return inc(eng.dma_start_transpose(out=out_, in_=in_), "qT", 16)

    barriers = {}

    def barrier(tag):
        barriers[tag] = dict(S)

    def wait_barrier(eng, tag, names=("qS", "qG", "qT", "sv", "sa", "sp", "so")):
        bc = barriers[tag]
        for n in names:
            if bc[n] > 0:
                eng.wait_ge(sems[n], bc[n])

    with ExitStack() as _stk:
        ent = _stk.enter_context
        for _nm in ("qS", "qG", "qT", "sv", "sa", "sp", "so"):
            sems[_nm] = ent(nc.semaphore(_nm))
        A_pos = ent(nc.sbuf_tensor("A_pos", [128, NJ * NS], F32))
        negb = ent(nc.sbuf_tensor("negb", [128, NJ], F32))
        convw = ent(nc.sbuf_tensor("convw", [128, NJ * 4], F32))
        convb = ent(nc.sbuf_tensor("convb", [128, NJ], F32))
        D_col = ent(nc.sbuf_tensor("D_col", [128, NJ], F32))
        identb = ent(nc.sbuf_tensor("identb", [128, 128], BF16))
        identf = ent(nc.sbuf_tensor("identf", [128, 128], F32))
        onesf = ent(nc.sbuf_tensor("onesf", [128, 128], F32))
        cns = ent(nc.sbuf_tensor("cns", [128, 4], F32))
        jnk = ent(nc.sbuf_tensor("jnk", [128, 512], F32))
        pY1 = ent(nc.psum_tensor("pY1", [128, 2048], F32))
        pY2 = ent(nc.psum_tensor("pY2", [128, 2048], F32))
        block = ent(nc.Block())

        def psl(i):
            return (pY1 if i < 4 else pY2)[:, (i % 4) * 512:(i % 4 + 1) * 512]

        if 0 in phases:
            _ps = ExitStack()
            p0_sg = _ps.enter_context(nc.sbuf_tensor("p0_sg", [128, 2 * OUTF], F32))
            p0_ot = _ps.enter_context(nc.sbuf_tensor("p0_ot", [128, 2 * OUTF], BF16))
            if True:
                b0 = dict(S)
                pieces = []
                for k in range(NK):
                    pieces.append((WinT[k * 128:(k + 1) * 128, :], h_WinTb[k * 128:(k + 1) * 128, :], OUTF))
                for j in range(NJ):
                    pieces.append((WoutT[j * 128:(j + 1) * 128, :], h_WoutTb[j * 128:(j + 1) * 128, :], D))
                for k in range(NK):
                    pieces.append((CBT[k * 128:(k + 1) * 128, :], h_CBTb[k * 128:(k + 1) * 128, :], D))

                @block.sync
                def _(sync):
                    dma(sync, "qS", A_pos[:].rearrange("p (j n) -> p j n", n=NS), A_logr.rearrange("(j p) n -> p j n", p=128))
                    dma(sync, "qS", negb[:], bdt_c)
                    dma(sync, "qS", convw[:], convw_c)
                    dma(sync, "qS", convb[:], convb_c)
                    dma(sync, "qS", D_col[:], D_c)
                    dma(sync, "qS", identf[:], ident)
                    dma(sync, "qS", onesf[0:1, :], ones1)
                    dma(sync, "qS", p0_sg[0:RK, 0:DI], WdtT)
                    for i, (src, dst, cols) in enumerate(pieces):
                        if i == 0:
                            sync.wait_ge(sems["sv"], b0["sv"] + 1)
                        if i >= 2:
                            sync.wait_ge(sems["sv"], b0["sv"] + 1 + (i - 1))
                        off = (i % 2) * OUTF
                        dma(sync, "qS", p0_sg[:, off:off + cols], src)

                @block.vector
                def _(vector):
                    vector.memset(cns[:, 0:1], float(EPS))
                    vector.memset(cns[:, 1:2], 1e-38)
                    vector.memset(cns[:, 2:3], -0.5)
                    vector.memset(cns[:, 3:4], 1.0 / D)
                    vector.memset(jnk[:], 0.0)
                    vector.wait_ge(sems["qS"], b0["qS"] + 8 * 16)
                    vector.tensor_copy(out=identb[:], in_=identf[:])
                    vector.tensor_scalar(out=negb[:], in0=negb[:], scalar1=-1.0,
                                         scalar2=None, op0=ALU.mult)
                    ins = vector.tensor_copy(out=p0_ot[0:RK, 0:DI], in_=p0_sg[0:RK, 0:DI])
                    inc(ins, "sv", 1)
                    for i, (src, dst, cols) in enumerate(pieces):
                        vector.wait_ge(sems["qS"], b0["qS"] + (9 + i) * 16)
                        if i >= 2:
                            vector.wait_ge(sems["qG"], b0["qG"] + i * 16)
                        off = (i % 2) * OUTF
                        ins = vector.tensor_copy(out=p0_ot[:, off:off + cols],
                                                 in_=p0_sg[:, off:off + cols])
                        inc(ins, "sv", 1)

                @block.scalar
                def _(scalar):
                    scalar.wait_ge(sems["qS"], b0["qS"] + 16)
                    ins = scalar.activation(out=A_pos[:], in_=A_pos[:], func=AF.Exp)
                    inc(ins, "sa", 1)

                @block.gpsimd
                def _(gpsimd):
                    gpsimd.wait_ge(sems["sv"], b0["sv"] + 1)
                    dma(gpsimd, "qG", h_WdtTb, p0_ot[0:RK, 0:DI])
                    for i, (src, dst, cols) in enumerate(pieces):
                        gpsimd.wait_ge(sems["sv"], b0["sv"] + i + 2)
                        off = (i % 2) * OUTF
                        dma(gpsimd, "qG", dst, p0_ot[:, off:off + cols])
            _ps.close()
        barrier("p0")

        if 1 in phases:
            _ps = ExitStack()
            p1_x = _ps.enter_context(nc.sbuf_tensor("p1_x", [128, 2 * D], F32))
            p1_sq = _ps.enter_context(nc.sbuf_tensor("p1_sq", [128, 2 * D], F32))
            p1_xn = _ps.enter_context(nc.sbuf_tensor("p1_xn", [128, 2 * D], F32))
            p1_xnb = _ps.enter_context(nc.sbuf_tensor("p1_xnb", [128, 2 * D], BF16))
            p1_st = _ps.enter_context(nc.sbuf_tensor("p1_st", [128, 16], F32))
            if True:
                b0 = dict(S)

                @block.sync
                def _(sync):
                    wait_barrier(sync, "p0")
                    for tt in range(NTT):
                        if tt >= 2:
                            sync.wait_ge(sems["sv"], b0["sv"] + (tt - 1) * 2)
                        dma(sync, "qS", p1_x[:, (tt % 2) * D:(tt % 2 + 1) * D],
                            x[tt * 128:(tt + 1) * 128, :])

                @block.scalar
                def _(scalar):
                    wait_barrier(scalar, "p0", ("qG",))
                    for tt in range(NTT):
                        o8 = (tt % 2) * 8
                        scalar.wait_ge(sems["qS"], b0["qS"] + (tt + 1) * 16)
                        if tt >= 2:
                            scalar.wait_ge(sems["sv"], b0["sv"] + (tt - 2) * 2 + 1)
                        ins = scalar.activation(out=p1_sq[:, (tt % 2) * D:(tt % 2 + 1) * D],
                                                in_=p1_x[:, (tt % 2) * D:(tt % 2 + 1) * D],
                                                func=AF.Square)
                        inc(ins, "sa", 1)
                        scalar.wait_ge(sems["sv"], b0["sv"] + tt * 2 + 1)
                        scalar.activation(out=p1_st[:, o8 + 5:o8 + 6],
                                          in_=p1_st[:, o8 + 3:o8 + 4],
                                          func=AF.Ln, scale=cns[:, 3:4], bias=cns[:, 0:1])
                        scalar.activation(out=p1_st[:, o8 + 6:o8 + 7],
                                          in_=p1_st[:, o8 + 5:o8 + 6],
                                          func=AF.Exp, scale=cns[:, 2:3])
                        ins = scalar.copy(out=jnk[:], in_=p1_sq[:, (tt % 2) * D:(tt % 2) * D + 512])
                        inc(ins, "sa", 1)

                @block.vector
                def _(vector):
                    wait_barrier(vector, "p0", ("qG",))
                    for tt in range(NTT):
                        o8 = (tt % 2) * 8
                        xs_ = p1_x[:, (tt % 2) * D:(tt % 2 + 1) * D]
                        xn_ = p1_xn[:, (tt % 2) * D:(tt % 2 + 1) * D]
                        xnb_ = p1_xnb[:, (tt % 2) * D:(tt % 2 + 1) * D]
                        sl = lambda c: p1_st[:, o8 + c:o8 + c + 1]
                        vector.wait_ge(sems["qS"], b0["qS"] + (tt + 1) * 16)
                        vector.tensor_reduce(out=sl(0), in_=xs_, axis=AX, op=ALU.add)
                        vector.wait_ge(sems["sa"], b0["sa"] + tt * 2 + 1)
                        vector.tensor_reduce(out=sl(1), in_=p1_sq[:, (tt % 2) * D:(tt % 2 + 1) * D],
                                             axis=AX, op=ALU.add)
                        vector.tensor_scalar(out=sl(4), in0=sl(0), scalar1=-1.0 / D,
                                             scalar2=None, op0=ALU.mult)
                        vector.tensor_copy(out=jnk[:], in_=xs_[:, 0:512])
                        vector.scalar_tensor_tensor(out=sl(3), in0=sl(0), scalar=sl(4),
                                                    in1=sl(1), op0=ALU.mult, op1=ALU.add)
                        ins = vector.tensor_copy(out=jnk[:], in_=xs_[:, 0:512])
                        inc(ins, "sv", 1)
                        vector.wait_ge(sems["sa"], b0["sa"] + tt * 2 + 2)
                        if tt >= 2:
                            vector.wait_ge(sems["qG"], b0["qG"] + (tt - 1) * 32)
                        vector.tensor_scalar(out=xn_, in0=xs_, scalar1=sl(4), scalar2=sl(6),
                                             op0=ALU.add, op1=ALU.mult)
                        ins = vector.tensor_copy(out=xnb_, in_=xn_)
                        inc(ins, "sv", 1)

                @block.gpsimd
                def _(gpsimd):
                    for tt in range(NTT):
                        gpsimd.wait_ge(sems["sv"], b0["sv"] + tt * 2 + 2)
                        dma(gpsimd, "qG", h_xn[tt * 128:(tt + 1) * 128, :],
                            p1_xn[:, (tt % 2) * D:(tt % 2 + 1) * D])
                        dma(gpsimd, "qG", h_xnb[tt * 128:(tt + 1) * 128, :],
                            p1_xnb[:, (tt % 2) * D:(tt % 2 + 1) * D])
            _ps.close()
        barrier("p1")

        if 2 in phases:
            NGRP = 35

            def gspec(g):
                if g < 32:
                    return g * 128, 128
                return {32: (4096, 16), 33: (4112, 16), 34: (4128, 64)}[g]

            _ps = ExitStack()
            p2_w = _ps.enter_context(nc.sbuf_tensor("p2_w", [128, NK * OUTF], BF16))
            p2_xt = _ps.enter_context(nc.sbuf_tensor("p2_xt", [128, 2 * NK * WIN], BF16))
            p2_ev = _ps.enter_context(nc.sbuf_tensor("p2_ev", [128, 2 * WIN], F32))
            p2_evb = _ps.enter_context(nc.sbuf_tensor("p2_evb", [128, 2 * WIN], BF16))
            if True:
                b0 = dict(S)

                @block.sync
                def _(sync):
                    wait_barrier(sync, "p1")
                    for k in range(NK):
                        dma(sync, "qS", p2_w[:, k * OUTF:(k + 1) * OUTF],
                            h_WinTb[k * 128:(k + 1) * 128, :])
                    for w in range(NW):
                        if w >= 2:
                            sync.wait_ge(sems["sp"], b0["sp"] + (w - 1) * NGRP)
                        off = (w % 2) * NK * WIN
                        for k in range(NK):
                            dmat(sync, p2_xt[:, off + k * WIN:off + (k + 1) * WIN],
                                 h_xnb[w * WIN:(w + 1) * WIN, k * 128:(k + 1) * 128])

                @block.tensor
                def _(tensor):
                    wait_barrier(tensor, "p1", ("qG",))
                    tensor.wait_ge(sems["qS"], b0["qS"] + 8 * 16)
                    for w in range(NW):
                        tensor.wait_ge(sems["qT"], b0["qT"] + (w + 1) * NK * 16)
                        off = (w % 2) * NK * WIN
                        for g in range(NGRP):
                            G = w * NGRP + g
                            if G >= 4:
                                tensor.wait_ge(sems["sa"], b0["sa"] + G - 3)
                            c0, M = gspec(g)
                            ps = psl(G % 4)
                            for k in range(NK):
                                ins = tensor.matmul(
                                    ps[0:M, :],
                                    p2_w[:, k * OUTF + c0:k * OUTF + c0 + M],
                                    p2_xt[:, off + k * WIN:off + (k + 1) * WIN],
                                    start=(k == 0), stop=(k == NK - 1))
                            inc(ins, "sp", 1)

                @block.scalar
                def _(scalar):
                    for w in range(NW):
                        for g in range(NGRP):
                            G = w * NGRP + g
                            scalar.wait_ge(sems["sp"], b0["sp"] + G + 1)
                            if G >= 2:
                                scalar.wait_ge(sems["qG"], b0["qG"] + (G - 1) * 16)
                            ps = psl(G % 4)
                            eo = (G % 2) * WIN
                            if g < 16:
                                ins = scalar.copy(out=p2_ev[:, eo:eo + WIN], in_=ps)
                            elif g < 32:
                                ins = scalar.activation(out=p2_evb[:, eo:eo + WIN],
                                                        in_=ps, func=AF.Silu)
                            elif g in (32, 33):
                                ins = scalar.copy(out=p2_evb[0:16, eo:eo + WIN], in_=ps[0:16, :])
                            else:
                                ins = scalar.copy(out=p2_evb[0:64, eo:eo + WIN], in_=ps[0:64, :])
                            inc(ins, "sa", 1)

                @block.gpsimd
                def _(gpsimd):
                    for w in range(NW):
                        tsl = slice(w * WIN, (w + 1) * WIN)
                        for g in range(NGRP):
                            G = w * NGRP + g
                            gpsimd.wait_ge(sems["sa"], b0["sa"] + G + 1)
                            eo = (G % 2) * WIN
                            if g < 16:
                                dma(gpsimd, "qG", h_xs[g * 128:(g + 1) * 128, tsl],
                                    p2_ev[:, eo:eo + WIN])
                            elif g < 32:
                                dma(gpsimd, "qG", h_sz[(g - 16) * 128:(g - 15) * 128, tsl],
                                    p2_evb[:, eo:eo + WIN])
                            elif g == 32:
                                dma(gpsimd, "qG", h_BT[:, tsl], p2_evb[0:16, eo:eo + WIN])
                            elif g == 33:
                                dma(gpsimd, "qG", h_CT[:, tsl], p2_evb[0:16, eo:eo + WIN])
                            else:
                                dma(gpsimd, "qG", h_dtpT[:, tsl], p2_evb[0:64, eo:eo + WIN])
            _ps.close()
        barrier("p2")

        if 3 in phases:
            _ps = ExitStack()
            p3_xs = _ps.enter_context(nc.sbuf_tensor("p3_xs", [128, 2 * 520], F32))
            p3_acc = _ps.enter_context(nc.sbuf_tensor("p3_acc", [128, 2 * WIN], F32))
            p3_u = _ps.enter_context(nc.sbuf_tensor("p3_u", [128, 2 * WIN], BF16))
            if True:
                b0 = dict(S)
                iters = [(w, j) for w in range(NW) for j in range(NJ)]
                qs_at = []
                q = b0["qS"]
                for (w, j) in iters:
                    q += 16 if w == 0 else 32
                    qs_at.append(q)

                @block.sync
                def _(sync):
                    wait_barrier(sync, "p2")
                    for i, (w, j) in enumerate(iters):
                        if i >= 2:
                            sync.wait_ge(sems["sv"], b0["sv"] + (i - 1))
                        off = (i % 2) * 520
                        r = slice(j * 128, (j + 1) * 128)
                        if w == 0:
                            dma(sync, "qS", p3_xs[:, off + 3:off + 3 + WIN], h_xs[r, 0:WIN])
                        else:
                            dma(sync, "qS", p3_xs[:, off:off + 3],
                                h_xs[r, w * WIN - 3:w * WIN])
                            dma(sync, "qS", p3_xs[:, off + 3:off + 3 + WIN],
                                h_xs[r, w * WIN:(w + 1) * WIN])

                @block.vector
                def _(vector):
                    wait_barrier(vector, "p2", ("qG",))
                    for i, (w, j) in enumerate(iters):
                        off = (i % 2) * 520
                        ao = (i % 2) * WIN
                        vector.wait_ge(sems["qS"], qs_at[i])
                        if w == 0:
                            vector.memset(p3_xs[:, off:off + 3], 0.0)
                        if i >= 2:
                            vector.wait_ge(sems["sa"], b0["sa"] + i - 1)
                        vector.tensor_scalar(out=p3_acc[:, ao:ao + WIN],
                                             in0=p3_xs[:, off + 3:off + 3 + WIN],
                                             scalar1=convw[:, j * 4 + 3:j * 4 + 4],
                                             scalar2=None, op0=ALU.mult)
                        for k in (2, 1, 0):
                            ins = vector.scalar_tensor_tensor(
                                out=p3_acc[:, ao:ao + WIN],
                                in0=p3_xs[:, off + k:off + k + WIN],
                                scalar=convw[:, j * 4 + k:j * 4 + k + 1],
                                in1=p3_acc[:, ao:ao + WIN],
                                op0=ALU.mult, op1=ALU.add)
                        inc(ins, "sv", 1)

                @block.scalar
                def _(scalar):
                    for i, (w, j) in enumerate(iters):
                        ao = (i % 2) * WIN
                        scalar.wait_ge(sems["sv"], b0["sv"] + i + 1)
                        if i >= 2:
                            scalar.wait_ge(sems["qG"], b0["qG"] + (i - 1) * 16)
                        ins = scalar.activation(out=p3_u[:, ao:ao + WIN],
                                                in_=p3_acc[:, ao:ao + WIN],
                                                func=AF.Silu, bias=convb[:, j:j + 1])
                        inc(ins, "sa", 1)

                @block.gpsimd
                def _(gpsimd):
                    for i, (w, j) in enumerate(iters):
                        ao = (i % 2) * WIN
                        gpsimd.wait_ge(sems["sa"], b0["sa"] + i + 1)
                        dma(gpsimd, "qG", h_u[j * 128:(j + 1) * 128, w * WIN:(w + 1) * WIN],
                            p3_u[:, ao:ao + WIN])
            _ps.close()
        barrier("p3")

        if 4 in phases:
            _ps = ExitStack()
            p4_wdt = _ps.enter_context(nc.sbuf_tensor("p4_wdt", [128, DI], BF16))
            p4_dtp = _ps.enter_context(nc.sbuf_tensor("p4_dtp", [128, 2 * WIN], BF16))
            p4_sig = _ps.enter_context(nc.sbuf_tensor("p4_sig", [128, NJ * WIN], F32))
            p4_dtn = _ps.enter_context(nc.sbuf_tensor("p4_dtn", [128, 2 * WIN], F32))
            p4_u = _ps.enter_context(nc.sbuf_tensor("p4_u", [128, 2 * WIN], BF16))
            p4_dtu = _ps.enter_context(nc.sbuf_tensor("p4_dtu", [128, 2 * WIN], BF16))
            if True:
                b0 = dict(S)

                def sig_sa(G):
                    w_, j_ = divmod(G, NJ)
                    return b0["sa"] + w_ * 32 + j_ + 1

                @block.sync
                def _(sync):
                    wait_barrier(sync, "p3")
                    dma(sync, "qS", p4_wdt[0:RK, :], h_WdtTb)
                    for w in range(NW):
                        if w >= 2:
                            sync.wait_ge(sems["sp"], b0["sp"] + (w - 1) * NJ)
                        dma(sync, "qS", p4_dtp[0:RK, (w % 2) * WIN:(w % 2 + 1) * WIN],
                            h_dtpT[:, w * WIN:(w + 1) * WIN])
                        for j in range(NJ):
                            it = w * NJ + j
                            if it >= 2:
                                sync.wait_ge(sems["sv"], b0["sv"] + it - 1)
                            dma(sync, "qS", p4_u[:, (j % 2) * WIN:(j % 2 + 1) * WIN],
                                h_u[j * 128:(j + 1) * 128, w * WIN:(w + 1) * WIN])

                @block.tensor
                def _(tensor):
                    wait_barrier(tensor, "p3", ("qG",))
                    for w in range(NW):
                        tensor.wait_ge(sems["qS"], b0["qS"] + 16 + w * 17 * 16 + 16)
                        for j in range(NJ):
                            G = w * NJ + j
                            if G >= 4:
                                tensor.wait_ge(sems["sa"], sig_sa(G - 4))
                            ins = tensor.matmul(psl(G % 4),
                                                p4_wdt[0:RK, j * 128:(j + 1) * 128],
                                                p4_dtp[0:RK, (w % 2) * WIN:(w % 2 + 1) * WIN],
                                                start=True, stop=True)
                            inc(ins, "sp", 1)

                @block.scalar
                def _(scalar):
                    for w in range(NW):
                        for j in range(NJ):
                            G = w * NJ + j
                            scalar.wait_ge(sems["sp"], b0["sp"] + G + 1)
                            ins = scalar.activation(out=p4_sig[:, j * WIN:(j + 1) * WIN],
                                                    in_=psl(G % 4), func=AF.Sigmoid,
                                                    scale=-1.0, bias=negb[:, j:j + 1])
                            inc(ins, "sa", 1)
                        for j in range(NJ):
                            G = w * NJ + j
                            if G >= 2:
                                scalar.wait_ge(sems["qG"], b0["qG"] + (G - 1) * 32)
                            ins = scalar.activation(out=p4_dtn[:, (j % 2) * WIN:(j % 2 + 1) * WIN],
                                                    in_=p4_sig[:, j * WIN:(j + 1) * WIN],
                                                    func=AF.Ln, bias=cns[:, 1:2])
                            inc(ins, "sa", 1)

                @block.vector
                def _(vector):
                    wait_barrier(vector, "p3", ())
                    for w in range(NW):
                        for j in range(NJ):
                            G = w * NJ + j
                            vector.wait_ge(sems["sa"], b0["sa"] + w * 32 + 16 + j + 1)
                            vector.wait_ge(sems["qS"], b0["qS"] + 16 + w * 17 * 16 + 16 + (j + 1) * 16)
                            ins = vector.tensor_tensor(
                                out=p4_dtu[:, (j % 2) * WIN:(j % 2 + 1) * WIN],
                                in0=p4_dtn[:, (j % 2) * WIN:(j % 2 + 1) * WIN],
                                in1=p4_u[:, (j % 2) * WIN:(j % 2 + 1) * WIN],
                                op=ALU.mult)
                            inc(ins, "sv", 1)

                @block.gpsimd
                def _(gpsimd):
                    for w in range(NW):
                        for j in range(NJ):
                            G = w * NJ + j
                            gpsimd.wait_ge(sems["sv"], b0["sv"] + G + 1)
                            dma(gpsimd, "qG", h_dtneg[j * 128:(j + 1) * 128, w * WIN:(w + 1) * WIN],
                                p4_dtn[:, (j % 2) * WIN:(j % 2 + 1) * WIN])
                            dma(gpsimd, "qG", h_dtu[j * 128:(j + 1) * 128, w * WIN:(w + 1) * WIN],
                                p4_dtu[:, (j % 2) * WIN:(j % 2 + 1) * WIN])
            _ps.close()
        barrier("p4")

        if 5 in phases:
            SEL = NS * 128
            _ps = ExitStack()
            p5_sel = _ps.enter_context(nc.sbuf_tensor("p5_sel", [128, 2 * SEL], F32))
            p5_selb = _ps.enter_context(nc.sbuf_tensor("p5_selb", [128, 2 * SEL], BF16))
            p5_bt = _ps.enter_context(nc.sbuf_tensor("p5_bt", [128, 2 * WIN], BF16))
            p5_bc = _ps.enter_context(nc.sbuf_tensor("p5_bc", [128, 2 * WIN], BF16))
            if True:
                b0 = dict(S)

                @block.sync
                def _(sync):
                    wait_barrier(sync, "p4")
                    dma(sync, "qS", p5_sel[0:NS, 0:SEL], selNC)
                    dma(sync, "qS", p5_sel[0:NS, SEL:2 * SEL], selPC)
                    for w in range(NW):
                        if w >= 1:
                            sync.wait_ge(sems["sp"], b0["sp"] + w * 32)
                        dma(sync, "qS", p5_bt[0:NS, 0:WIN], h_BT[:, w * WIN:(w + 1) * WIN])
                        dma(sync, "qS", p5_bt[0:NS, WIN:2 * WIN], h_CT[:, w * WIN:(w + 1) * WIN])

                @block.vector
                def _(vector):
                    wait_barrier(vector, "p4", ())
                    vector.wait_ge(sems["qS"], b0["qS"] + 32)
                    ins = vector.tensor_copy(out=p5_selb[0:NS, :], in_=p5_sel[0:NS, :])
                    inc(ins, "sv", 1)

                @block.tensor
                def _(tensor):
                    tensor.wait_ge(sems["sv"], b0["sv"] + 1)
                    for w in range(NW):
                        tensor.wait_ge(sems["qS"], b0["qS"] + 32 + (w + 1) * 32)
                        for g in range(32):
                            G = w * 32 + g
                            n = g % 16
                            isB = g < 16
                            if G >= 4:
                                tensor.wait_ge(sems["sa"], b0["sa"] + G - 3)
                            so_ = 0 if isB else SEL
                            bo = 0 if isB else WIN
                            ins = tensor.matmul(psl(G % 4),
                                                p5_selb[0:NS, so_ + n * 128:so_ + (n + 1) * 128],
                                                p5_bt[0:NS, bo:bo + WIN],
                                                start=True, stop=True)
                            inc(ins, "sp", 1)

                @block.scalar
                def _(scalar):
                    for w in range(NW):
                        for g in range(32):
                            G = w * 32 + g
                            scalar.wait_ge(sems["sp"], b0["sp"] + G + 1)
                            if G >= 2:
                                scalar.wait_ge(sems["qG"], b0["qG"] + (G - 1) * 16)
                            ins = scalar.copy(out=p5_bc[:, (G % 2) * WIN:(G % 2 + 1) * WIN],
                                              in_=psl(G % 4))
                            inc(ins, "sa", 1)

                @block.gpsimd
                def _(gpsimd):
                    for w in range(NW):
                        for g in range(32):
                            G = w * 32 + g
                            n = g % 16
                            gpsimd.wait_ge(sems["sa"], b0["sa"] + G + 1)
                            dst = h_Bbc if g < 16 else h_Cbc
                            dma(gpsimd, "qG", dst[n * 128:(n + 1) * 128, w * WIN:(w + 1) * WIN],
                                p5_bc[:, (G % 2) * WIN:(G % 2 + 1) * WIN])
            _ps.close()
        barrier("p5")

        if 6 in phases:
            _ps = ExitStack()
            p6_dtn = _ps.enter_context(nc.sbuf_tensor("p6_dtn", [128, L], F32))
            p6_dA = _ps.enter_context(nc.sbuf_tensor("p6_dA", [128, 2 * L], F32))
            p6_dtu = _ps.enter_context(nc.sbuf_tensor("p6_dtu", [128, L], BF16))
            p6_u = _ps.enter_context(nc.sbuf_tensor("p6_u", [128, L], BF16))
            p6_sz = _ps.enter_context(nc.sbuf_tensor("p6_sz", [128, L], BF16))
            p6_bcB = _ps.enter_context(nc.sbuf_tensor("p6_bcB", [128, 2 * L], BF16))
            p6_bcC = _ps.enter_context(nc.sbuf_tensor("p6_bcC", [128, 2 * L], BF16))
            p6_X = _ps.enter_context(nc.sbuf_tensor("p6_X", [128, 2 * L], BF16))
            p6_h = _ps.enter_context(nc.sbuf_tensor("p6_h", [128, 2 * L], BF16))
            p6_hC = _ps.enter_context(nc.sbuf_tensor("p6_hC", [128, 2 * L], BF16))
            p6_t1 = _ps.enter_context(nc.sbuf_tensor("p6_t1", [128, L], F32))
            p6_yg = _ps.enter_context(nc.sbuf_tensor("p6_yg", [128, L], BF16))
            if True:
                b0 = dict(S)
                PJ = {"qS": 36 * 16, "sa": 16, "sv": 51, "sp": 16, "qG": 16}

                def jb(j, name):
                    return b0[name] + j * PJ[name]

                @block.sync
                def _(sync):
                    wait_barrier(sync, "p5")
                    for j in range(NJ):
                        r = slice(j * 128, (j + 1) * 128)
                        if j >= 1:
                            sync.wait_ge(sems["sv"], jb(j, "sv"))
                            sync.wait_ge(sems["sa"], jb(j, "sa"))
                        dma(sync, "qS", p6_dtn[:], h_dtneg[r, :])
                        dma(sync, "qS", p6_dtu[:], h_dtu[r, :])
                        dma(sync, "qS", p6_u[:], h_u[r, :])
                        dma(sync, "qS", p6_sz[:], h_sz[r, :])
                        for n in range(NS):
                            if n >= 2:
                                sync.wait_ge(sems["sv"], jb(j, "sv") + 3 * n - 3)
                            nb = (n % 2) * L
                            dma(sync, "qS", p6_bcB[:, nb:nb + L], h_Bbc[n * 128:(n + 1) * 128, :])
                            dma(sync, "qS", p6_bcC[:, nb:nb + L], h_Cbc[n * 128:(n + 1) * 128, :])

                @block.scalar
                def _(scalar):
                    wait_barrier(scalar, "p5", ())
                    for j in range(NJ):
                        scalar.wait_ge(sems["qS"], jb(j, "qS") + 4 * 16)
                        for n in range(NS):
                            if n >= 2:
                                scalar.wait_ge(sems["sv"], jb(j, "sv") + 3 * n - 4)
                            ins = scalar.activation(out=p6_dA[:, (n % 2) * L:(n % 2 + 1) * L],
                                                    in_=p6_dtn[:], func=AF.Exp,
                                                    scale=A_pos[:, j * NS + n:j * NS + n + 1])
                            inc(ins, "sa", 1)

                @block.vector
                def _(vector):
                    wait_barrier(vector, "p5", ())
                    for j in range(NJ):
                        for n in range(NS):
                            nb = (n % 2) * L
                            vector.wait_ge(sems["qS"], jb(j, "qS") + 64 + (n + 1) * 32)
                            ins = vector.tensor_tensor(out=p6_X[:, nb:nb + L],
                                                       in0=p6_dtu[:],
                                                       in1=p6_bcB[:, nb:nb + L],
                                                       op=ALU.mult)
                            inc(ins, "sv", 1)
                            vector.wait_ge(sems["sa"], jb(j, "sa") + n + 1)
                            ins = vector.tensor_tensor_scan(out=p6_h[:, nb:nb + L],
                                                            data0=p6_dA[:, nb:nb + L],
                                                            data1=p6_X[:, nb:nb + L],
                                                            initial=0.0,
                                                            op0=ALU.mult, op1=ALU.add)
                            inc(ins, "sv", 1)
                            if n >= 2:
                                vector.wait_ge(sems["sp"], jb(j, "sp") + n - 1)
                            ins = vector.tensor_tensor(out=p6_hC[:, nb:nb + L],
                                                       in0=p6_h[:, nb:nb + L],
                                                       in1=p6_bcC[:, nb:nb + L],
                                                       op=ALU.mult)
                            inc(ins, "sv", 1)
                        vector.wait_ge(sems["sp"], jb(j, "sp") + 16)
                        ins = vector.scalar_tensor_tensor(out=p6_t1[:, 0:2048],
                                                          in0=p6_u[:, 0:2048],
                                                          scalar=D_col[:, j:j + 1],
                                                          in1=pY1[:],
                                                          op0=ALU.mult, op1=ALU.add)
                        inc(ins, "sv", 1)
                        ins = vector.scalar_tensor_tensor(out=p6_t1[:, 2048:4096],
                                                          in0=p6_u[:, 2048:4096],
                                                          scalar=D_col[:, j:j + 1],
                                                          in1=pY2[:],
                                                          op0=ALU.mult, op1=ALU.add)
                        inc(ins, "sv", 1)
                        if j >= 1:
                            vector.wait_ge(sems["qG"], jb(j, "qG"))
                        ins = vector.tensor_tensor(out=p6_yg[:], in0=p6_t1[:],
                                                   in1=p6_sz[:], op=ALU.mult)
                        inc(ins, "sv", 1)

                @block.gpsimd
                def _(gpsimd):
                    wait_barrier(gpsimd, "p5", ())
                    for j in range(NJ):
                        gpsimd.wait_ge(sems["sv"], jb(j, "sv") + PJ["sv"])
                        dma(gpsimd, "qG", h_yg[j * 128:(j + 1) * 128, :], p6_yg[:])

                @block.tensor
                def _(tensor):
                    wait_barrier(tensor, "p5", ())
                    for j in range(NJ):
                        for n in range(NS):
                            nb = (n % 2) * L
                            tensor.wait_ge(sems["sv"], jb(j, "sv") + 3 * n + 3)
                            if n == 0 and j >= 1:
                                tensor.wait_ge(sems["sv"], jb(j, "sv") - 1)
                            for s in range(8):
                                ins = tensor.matmul(psl(s), identb[:],
                                                    p6_hC[:, nb + s * 512:nb + (s + 1) * 512],
                                                    start=(n == 0), stop=(n == NS - 1))
                            inc(ins, "sp", 1)
            _ps.close()
        barrier("p6")

        if 7 in phases:
            _ps = ExitStack()
            p7_wout = _ps.enter_context(nc.sbuf_tensor("p7_wout", [128, NJ * D], BF16))
            p7_cbt = _ps.enter_context(nc.sbuf_tensor("p7_cbt", [128, NK * D], BF16))
            p7_yg = _ps.enter_context(nc.sbuf_tensor("p7_yg", [128, 2 * NJ * 128], BF16))
            p7_xt = _ps.enter_context(nc.sbuf_tensor("p7_xt", [128, 2 * NK * 128], BF16))
            p7_xn = _ps.enter_context(nc.sbuf_tensor("p7_xn", [128, 2 * D], F32))
            p7_x = _ps.enter_context(nc.sbuf_tensor("p7_x", [128, 2 * D], F32))
            p7_fa = _ps.enter_context(nc.sbuf_tensor("p7_fa", [128, D], F32))
            p7_sq = _ps.enter_context(nc.sbuf_tensor("p7_sq", [128, D], F32))
            p7_fb = _ps.enter_context(nc.sbuf_tensor("p7_fb", [128, D], F32))
            p7_out = _ps.enter_context(nc.sbuf_tensor("p7_out", [128, 2 * D], F32))
            p7_st = _ps.enter_context(nc.sbuf_tensor("p7_st", [128, 16], F32))
            p7_row = _ps.enter_context(nc.sbuf_tensor("p7_row", [128, 4 * D], F32))
            p7_gbc = _ps.enter_context(nc.sbuf_tensor("p7_gbc", [128, D], F32))
            p7_sbc = _ps.enter_context(nc.sbuf_tensor("p7_sbc", [128, D], F32))
            p7_bbc = _ps.enter_context(nc.sbuf_tensor("p7_bbc", [128, D], F32))
            p7_tbc = _ps.enter_context(nc.sbuf_tensor("p7_tbc", [128, D], F32))
            if True:
                b0 = dict(S)

                @block.sync
                def _(sync):
                    wait_barrier(sync, "p6")
                    dma(sync, "qS", p7_row[0:1, :], lnrow)
                    for j in range(NJ):
                        dma(sync, "qS", p7_wout[:, j * D:(j + 1) * D],
                            h_WoutTb[j * 128:(j + 1) * 128, :])
                    for k in range(NK):
                        dma(sync, "qS", p7_cbt[:, k * D:(k + 1) * D],
                            h_CBTb[k * 128:(k + 1) * 128, :])
                    for tt in range(NTT):
                        if tt >= 2:
                            sync.wait_ge(sems["sv"], b0["sv"] + 1 + (tt - 1) * 9)
                        off = tt % 2
                        r = slice(tt * 128, (tt + 1) * 128)
                        for j in range(NJ):
                            dma(sync, "qS",
                                p7_yg[:, off * NJ * 128 + j * 128:off * NJ * 128 + (j + 1) * 128],
                                h_yg[j * 128:(j + 1) * 128, r])
                        for k in range(NK):
                            dmat(sync,
                                 p7_xt[:, off * NK * 128 + k * 128:off * NK * 128 + (k + 1) * 128],
                                 h_xnb[r, k * 128:(k + 1) * 128])
                        dma(sync, "qS", p7_xn[:, off * D:(off + 1) * D], h_xn[r, :])
                        dma(sync, "qS", p7_x[:, off * D:(off + 1) * D], x[r, :])

                @block.tensor
                def _(tensor):
                    wait_barrier(tensor, "p6")
                    tensor.wait_ge(sems["qS"], b0["qS"] + 16)
                    for i in range(4):
                        for half in range(2):
                            ins = tensor.matmul(psl(i * 2 + half), onesf[0:1, :],
                                                p7_row[0:1, i * D + half * 512:i * D + half * 512 + 512],
                                                start=True, stop=True)
                        inc(ins, "sp", 1)

                @block.scalar
                def _(scalar):
                    wait_barrier(scalar, "p6", ())
                    scalar.wait_ge(sems["sp"], b0["sp"] + 4)
                    for half in range(2):
                        scalar.copy(out=p7_gbc[:, half * 512:(half + 1) * 512], in_=psl(0 + half))
                        scalar.copy(out=p7_bbc[:, half * 512:(half + 1) * 512], in_=psl(2 + half))
                        scalar.copy(out=p7_sbc[:, half * 512:(half + 1) * 512], in_=psl(4 + half))
                        ins = scalar.copy(out=p7_tbc[:, half * 512:(half + 1) * 512], in_=psl(6 + half))
                    inc(ins, "sa", 1)

                @block.vector
                def _(vector):
                    wait_barrier(vector, "p6", ())
                    vector.wait_ge(sems["sa"], b0["sa"] + 1)
                    vector.tensor_scalar(out=p7_sbc[:], in0=p7_sbc[:],
                                         scalar1=1.0 / 127.0, scalar2=None, op0=ALU.mult)
                    ins = vector.tensor_tensor(out=p7_bbc[:], in0=p7_bbc[:],
                                               in1=p7_tbc[:], op=ALU.add)
                    inc(ins, "sv", 1)

                bp = dict(S)
                q0 = b0["qS"] + 25 * 16
                t0 = b0["qT"]

                @block.tensor
                def _(tensor):
                    tensor.wait_ge(sems["sv"], bp["sv"])
                    for tt in range(NTT):
                        off = tt % 2
                        tensor.wait_ge(sems["qS"], q0 + (tt + 1) * 18 * 16)
                        tensor.wait_ge(sems["qT"], t0 + (tt + 1) * 8 * 16)
                        if tt >= 2:
                            tensor.wait_ge(sems["sv"], bp["sv"] + (tt - 1) * 9)
                        for half in range(2):
                            for j in range(NJ):
                                ins = tensor.matmul(
                                    psl(half),
                                    p7_yg[:, off * NJ * 128 + j * 128:off * NJ * 128 + (j + 1) * 128],
                                    p7_wout[:, j * D + half * 512:j * D + half * 512 + 512],
                                    start=(j == 0), stop=(j == NJ - 1))
                            inc(ins, "sp", 1)
                        for half in range(2):
                            for k in range(NK):
                                ins = tensor.matmul(
                                    psl(4 + half),
                                    p7_xt[:, off * NK * 128 + k * 128:off * NK * 128 + (k + 1) * 128],
                                    p7_cbt[:, k * D + half * 512:k * D + half * 512 + 512],
                                    start=(k == 0), stop=(k == NK - 1))
                            inc(ins, "sp", 1)

                @block.scalar
                def _(scalar):
                    for tt in range(NTT):
                        scalar.wait_ge(sems["sv"], bp["sv"] + tt * 9 + 1)
                        ins = scalar.activation(out=p7_sq[:], in_=p7_fa[:], func=AF.Square)
                        inc(ins, "sa", 1)
                        scalar.wait_ge(sems["sv"], bp["sv"] + tt * 9 + 2)
                        scalar.activation(out=p7_st[:, 5:6], in_=p7_st[:, 3:4],
                                          func=AF.Ln, scale=cns[:, 3:4], bias=cns[:, 0:1])
                        scalar.activation(out=p7_st[:, 6:7], in_=p7_st[:, 5:6],
                                          func=AF.Exp, scale=cns[:, 2:3])
                        ins = scalar.copy(out=jnk[:], in_=p7_sq[:, 0:512])
                        inc(ins, "sa", 1)

                @block.vector
                def _(vector):
                    for tt in range(NTT):
                        off = tt % 2
                        sl = lambda c: p7_st[:, c:c + 1]
                        vector.wait_ge(sems["sp"], bp["sp"] + tt * 4 + 2)
                        vector.wait_ge(sems["qS"], q0 + (tt + 1) * 18 * 16)
                        vector.tensor_tensor(out=p7_fa[:, 0:512], in0=pY1[:, 0:512],
                                             in1=p7_xn[:, off * D:off * D + 512], op=ALU.add)
                        ins = vector.tensor_tensor(out=p7_fa[:, 512:1024], in0=pY1[:, 512:1024],
                                                   in1=p7_xn[:, off * D + 512:off * D + 1024],
                                                   op=ALU.add)
                        inc(ins, "sv", 1)
                        vector.tensor_reduce(out=sl(0), in_=p7_fa[:], axis=AX, op=ALU.add)
                        vector.wait_ge(sems["sa"], bp["sa"] + tt * 2 + 1)
                        vector.tensor_reduce(out=sl(1), in_=p7_sq[:], axis=AX, op=ALU.add)
                        vector.tensor_scalar(out=sl(4), in0=sl(0), scalar1=-1.0 / D,
                                             scalar2=None, op0=ALU.mult)
                        vector.tensor_copy(out=jnk[:], in_=p7_fa[:, 0:512])
                        vector.scalar_tensor_tensor(out=sl(3), in0=sl(0), scalar=sl(4),
                                                    in1=sl(1), op0=ALU.mult, op1=ALU.add)
                        ins = vector.tensor_copy(out=jnk[:], in_=p7_fa[:, 0:512])
                        inc(ins, "sv", 1)
                        vector.wait_ge(sems["sa"], bp["sa"] + tt * 2 + 2)
                        ins = vector.tensor_scalar(out=p7_fb[:], in0=p7_fa[:], scalar1=sl(4),
                                                   scalar2=sl(6), op0=ALU.add, op1=ALU.mult)
                        inc(ins, "sv", 1)
                        ins = vector.tensor_tensor(out=p7_fb[:], in0=p7_fb[:], in1=p7_gbc[:],
                                                   op=ALU.mult)
                        inc(ins, "sv", 1)
                        vector.wait_ge(sems["sp"], bp["sp"] + tt * 4 + 4)
                        vector.tensor_tensor(out=p7_fa[:, 0:512], in0=pY2[:, 0:512],
                                             in1=p7_sbc[:, 0:512], op=ALU.mult)
                        ins = vector.tensor_tensor(out=p7_fa[:, 512:1024], in0=pY2[:, 512:1024],
                                                   in1=p7_sbc[:, 512:1024], op=ALU.mult)
                        inc(ins, "sv", 1)
                        ins = vector.tensor_tensor(out=p7_fb[:], in0=p7_fb[:], in1=p7_fa[:],
                                                   op=ALU.add)
                        inc(ins, "sv", 1)
                        ins = vector.tensor_tensor(out=p7_fb[:], in0=p7_fb[:], in1=p7_bbc[:],
                                                   op=ALU.add)
                        inc(ins, "sv", 1)
                        if tt >= 2:
                            vector.wait_ge(sems["qG"], bp["qG"] + (tt - 1) * 16)
                        ins = vector.tensor_tensor(out=p7_out[:, off * D:(off + 1) * D],
                                                   in0=p7_fb[:],
                                                   in1=p7_x[:, off * D:(off + 1) * D],
                                                   op=ALU.add)
                        inc(ins, "sv", 2)

                @block.gpsimd
                def _(gpsimd):
                    for tt in range(NTT):
                        off = tt % 2
                        gpsimd.wait_ge(sems["sv"], bp["sv"] + (tt + 1) * 9)
                        dma(gpsimd, "qG", out[tt * 128:(tt + 1) * 128, :],
                            p7_out[:, off * D:(off + 1) * D])
            _ps.close()
        barrier("p7")

        @block.gpsimd
        def _(gpsimd):
            if S["qG"] > 0:
                gpsimd.wait_ge(sems["qG"], S["qG"])
            if S["qS"] > 0:
                gpsimd.wait_ge(sems["qS"], S["qS"])
            if S["qT"] > 0:
                gpsimd.wait_ge(sems["qT"], S["qT"])

    import os
    if os.environ.get("MK_PRINT_S"):
        print("FINAL S:", S)
        print("n instructions:", sum(1 for _ in nc.all_instructions()))
    return nc


def _prep(inputs):
    return {
        "WinT": np.ascontiguousarray(inputs["L0_attn_W_in"].T),
        "WdtT": np.ascontiguousarray(inputs["L0_attn_W_dt"].T),
        "WoutT": np.ascontiguousarray(inputs["L0_attn_W_out"].T),
        "CBT": np.ascontiguousarray(inputs["L0_ffn_CB"].T),
        "A_logr": np.ascontiguousarray(inputs["L0_attn_A_log"]),
        "bdt_c": np.ascontiguousarray(inputs["L0_attn_b_dt"].reshape(NJ, 128).T),
        "convw_c": np.ascontiguousarray(
            inputs["L0_attn_conv_w"][:, 0, :].reshape(NJ, 128, 4).transpose(1, 0, 2).reshape(128, NJ * 4)),
        "convb_c": np.ascontiguousarray(inputs["L0_attn_conv_b"].reshape(NJ, 128).T),
        "D_c": np.ascontiguousarray(inputs["L0_attn_D"].reshape(NJ, 128).T),
        "selPC": np.kron(np.eye(NS, dtype=np.float32), np.ones((1, 128), np.float32)),
        "selNC": -np.kron(np.eye(NS, dtype=np.float32), np.ones((1, 128), np.float32)),
        "ident": np.eye(128, dtype=np.float32),
        "ones1": np.ones((1, 128), np.float32),
        "lnrow": np.concatenate([inputs["L0_attn_ln_g"], inputs["L0_attn_ln_b"],
                                 inputs["L0_ffn_SCB"], inputs["L0_ffn_bias"]])[None, :].astype(np.float32),
    }


def kernel(**inputs):
    shared = _prep(inputs)
    xfull = np.ascontiguousarray(np.asarray(inputs["L0_attn_x"], dtype=np.float32))
    nc = build_nc()
    in_maps = []
    for b in range(8):
        m = dict(shared)
        m["x"] = np.ascontiguousarray(xfull[b])
        in_maps.append(m)
    res = run_bass_kernel_spmd(nc, in_maps, core_ids=list(range(8)))
    return np.stack([r["out"] for r in res.results], axis=0)


# version 04
# speedup vs baseline: 1.0787x; 1.0787x over previous
import numpy as np
from contextlib import ExitStack
import concourse.bass as bass
import concourse.mybir as mybir
from concourse.bass_utils import run_bass_kernel_spmd

F32 = mybir.dt.float32
BF16 = mybir.dt.bfloat16
AF = mybir.ActivationFunctionType
ALU = mybir.AluOpType
AX = mybir.AxisListType.X

L, D, DI, NS, RK, OUTF = 4096, 1024, 2048, 16, 64, 4192
NW, WIN, NJ, NK, NTT = 8, 512, 16, 8, 32
EPS = 1e-5
ALL_PHASES = (0, 1, 2, 3, 4, 5, 6, 7)


def build_nc(phases=ALL_PHASES, debug=()):
    nc = bass.Bass("TRN2", target_bir_lowering=False, debug=False)

    def dram(name, shape, dtype):
        k = "ExternalOutput" if name in debug else "Internal"
        return nc.dram_tensor(name, shape, dtype, kind=k).ap()

    x = nc.dram_tensor("x", [L, D], F32, kind="ExternalInput").ap()
    WinT = nc.dram_tensor("WinT", [D, OUTF], F32, kind="ExternalInput").ap()
    WdtT = nc.dram_tensor("WdtT", [RK, DI], F32, kind="ExternalInput").ap()
    WoutT = nc.dram_tensor("WoutT", [DI, D], F32, kind="ExternalInput").ap()
    CBT = nc.dram_tensor("CBT", [D, D], F32, kind="ExternalInput").ap()
    A_logr = nc.dram_tensor("A_logr", [DI, NS], F32, kind="ExternalInput").ap()
    bdt_c = nc.dram_tensor("bdt_c", [128, NJ], F32, kind="ExternalInput").ap()
    convw_c = nc.dram_tensor("convw_c", [128, NJ * 4], F32, kind="ExternalInput").ap()
    convb_c = nc.dram_tensor("convb_c", [128, NJ], F32, kind="ExternalInput").ap()
    D_c = nc.dram_tensor("D_c", [128, NJ], F32, kind="ExternalInput").ap()
    selPC = nc.dram_tensor("selPC", [NS, NS * 128], F32, kind="ExternalInput").ap()
    selNC = nc.dram_tensor("selNC", [NS, NS * 128], F32, kind="ExternalInput").ap()
    ident = nc.dram_tensor("ident", [128, 128], F32, kind="ExternalInput").ap()
    ones1 = nc.dram_tensor("ones1", [1, 128], F32, kind="ExternalInput").ap()
    lnrow = nc.dram_tensor("lnrow", [1, 4 * D], F32, kind="ExternalInput").ap()
    out = nc.dram_tensor("out", [L, D], F32, kind="ExternalOutput").ap()

    h_WinTb = dram("h_WinTb", [D, OUTF], BF16)
    h_WdtTb = dram("h_WdtTb", [RK, DI], BF16)
    h_WoutTb = dram("h_WoutTb", [DI, D], BF16)
    h_CBTb = dram("h_CBTb", [D, D], BF16)
    h_xn = dram("h_xn", [L, D], F32)
    h_xnb = dram("h_xnb", [L, D], BF16)
    h_xs = dram("h_xs", [DI, L], F32)
    h_sz = dram("h_sz", [DI, L], BF16)
    h_u = dram("h_u", [DI, L], BF16)
    h_BT = dram("h_BT", [NS, L], BF16)
    h_CT = dram("h_CT", [NS, L], BF16)
    h_dtpT = dram("h_dtpT", [RK, L], BF16)
    h_dtneg = dram("h_dtneg", [DI, L], F32)
    h_dtu = dram("h_dtu", [DI, L], BF16)
    h_Bbc = dram("h_Bbc", [NS * 128, L], BF16)
    h_Cbc = dram("h_Cbc", [NS * 128, L], BF16)
    h_yg = dram("h_yg", [DI, L], BF16)


    DMA_SEMS = ["qH", "qP0", "qP1", "qP2", "qP3", "qP4", "qP5",
                "qG0", "qG1", "qG2", "qG3", "qG4", "qT0", "qT1"]
    S = {n: 0 for n in ["sv", "sa", "sp", "so"] + DMA_SEMS}
    sems = {}
    rec = {}

    def inc(ins, name, n=1):
        S[name] += n
        ins.then_inc(sems[name], n)
        return S[name]

    def dma(eng, qname, out_, in_, tag=None, **kw):
        v = inc(eng.dma_start(out=out_, in_=in_, **kw), qname, 16)
        if tag is not None:
            rec[tag] = (qname, v)
        return v

    def dmat(eng, qname, out_, in_, tag=None):
        v = inc(eng.dma_start_transpose(out=out_, in_=in_), qname, 16)
        if tag is not None:
            rec[tag] = (qname, v)
        return v

    def wtag(eng, tag):
        qn, v = rec[tag]
        eng.wait_ge(sems[qn], v)

    barriers = {}

    def barrier(tag):
        barriers[tag] = dict(S)

    def wait_barrier(eng, tag, names=None):
        bc = barriers[tag]
        if names is None:
            names = list(bc.keys())
        for n in names:
            if bc[n] > 0:
                eng.wait_ge(sems[n], bc[n])

    with ExitStack() as _stk:
        ent = _stk.enter_context
        for _nm in list(S.keys()):
            sems[_nm] = ent(nc.semaphore(_nm))
        A_pos = ent(nc.sbuf_tensor("A_pos", [128, NJ * NS], F32))
        negb = ent(nc.sbuf_tensor("negb", [128, NJ], F32))
        convw = ent(nc.sbuf_tensor("convw", [128, NJ * 4], F32))
        convb = ent(nc.sbuf_tensor("convb", [128, NJ], F32))
        D_col = ent(nc.sbuf_tensor("D_col", [128, NJ], F32))
        identb = ent(nc.sbuf_tensor("identb", [128, 128], BF16))
        identf = ent(nc.sbuf_tensor("identf", [128, 128], F32))
        onesf = ent(nc.sbuf_tensor("onesf", [128, 128], F32))
        cns = ent(nc.sbuf_tensor("cns", [128, 4], F32))
        jnk = ent(nc.sbuf_tensor("jnk", [128, 512], F32))
        pY1 = ent(nc.psum_tensor("pY1", [128, 2048], F32))
        pY2 = ent(nc.psum_tensor("pY2", [128, 2048], F32))
        block = ent(nc.Block())

        def psl(i):
            return (pY1 if i < 4 else pY2)[:, (i % 4) * 512:(i % 4 + 1) * 512]

        if 0 in phases:
            _ps = ExitStack()
            p0_sg = _ps.enter_context(nc.sbuf_tensor("p0_sg", [128, 2 * OUTF], F32))
            p0_ot = _ps.enter_context(nc.sbuf_tensor("p0_ot", [128, 2 * OUTF], BF16))
            if True:
                b0 = dict(S)
                pieces = []
                for k in range(NK):
                    pieces.append((WinT[k * 128:(k + 1) * 128, :], h_WinTb[k * 128:(k + 1) * 128, :], OUTF))
                for j in range(NJ):
                    pieces.append((WoutT[j * 128:(j + 1) * 128, :], h_WoutTb[j * 128:(j + 1) * 128, :], D))
                for k in range(NK):
                    pieces.append((CBT[k * 128:(k + 1) * 128, :], h_CBTb[k * 128:(k + 1) * 128, :], D))

                @block.sync
                def _(sync):
                    dma(sync, "qH", A_pos[:].rearrange("p (j n) -> p j n", n=NS), A_logr.rearrange("(j p) n -> p j n", p=128), tag="p0a")
                    dma(sync, "qH", negb[:], bdt_c)
                    dma(sync, "qH", convw[:], convw_c)
                    dma(sync, "qH", convb[:], convb_c)
                    dma(sync, "qH", D_col[:], D_c)
                    dma(sync, "qH", identf[:], ident)
                    dma(sync, "qH", onesf[0:1, :], ones1)
                    dma(sync, "qH", p0_sg[0:RK, 0:DI], WdtT, tag="p0h")
                    for i, (src, dst, cols) in enumerate(pieces):
                        if i == 0:
                            sync.wait_ge(sems["sv"], b0["sv"] + 1)
                        if i >= 2:
                            sync.wait_ge(sems["sv"], b0["sv"] + 1 + (i - 1))
                        off = (i % 2) * OUTF
                        dma(sync, "qP0" if i % 2 == 0 else "qP1", p0_sg[:, off:off + cols], src, tag=("p0p", i))

                @block.gpsimd
                def _(gpsimd):
                    gpsimd.wait_ge(sems["sv"], b0["sv"] + 1)
                    dma(gpsimd, "qG4", h_WdtTb, p0_ot[0:RK, 0:DI])
                    for i, (src, dst, cols) in enumerate(pieces):
                        gpsimd.wait_ge(sems["sv"], b0["sv"] + i + 2)
                        off = (i % 2) * OUTF
                        dma(gpsimd, "qG0" if i % 2 == 0 else "qG1", dst, p0_ot[:, off:off + cols], tag=("p0s", i))
                @block.vector
                def _(vector):
                    vector.memset(cns[:, 0:1], float(EPS))
                    vector.memset(cns[:, 1:2], 1e-38)
                    vector.memset(cns[:, 2:3], -0.5)
                    vector.memset(cns[:, 3:4], 1.0 / D)
                    vector.memset(jnk[:], 0.0)
                    wtag(vector, "p0h")
                    vector.tensor_copy(out=identb[:], in_=identf[:])
                    vector.tensor_scalar(out=negb[:], in0=negb[:], scalar1=-1.0,
                                         scalar2=None, op0=ALU.mult)
                    ins = vector.tensor_copy(out=p0_ot[0:RK, 0:DI], in_=p0_sg[0:RK, 0:DI])
                    inc(ins, "sv", 1)
                    for i, (src, dst, cols) in enumerate(pieces):
                        wtag(vector, ("p0p", i))
                        if i >= 2:
                            wtag(vector, ("p0s", i - 2))
                        off = (i % 2) * OUTF
                        ins = vector.tensor_copy(out=p0_ot[:, off:off + cols],
                                                 in_=p0_sg[:, off:off + cols])
                        inc(ins, "sv", 1)

                @block.scalar
                def _(scalar):
                    wtag(scalar, "p0a")
                    ins = scalar.activation(out=A_pos[:], in_=A_pos[:], func=AF.Exp)
                    inc(ins, "sa", 1)

            _ps.close()
        barrier("p0")

        if 1 in phases:
            _ps = ExitStack()
            p1_x = _ps.enter_context(nc.sbuf_tensor("p1_x", [128, 2 * D], F32))
            p1_sq = _ps.enter_context(nc.sbuf_tensor("p1_sq", [128, 2 * D], F32))
            p1_xn = _ps.enter_context(nc.sbuf_tensor("p1_xn", [128, 2 * D], F32))
            p1_xnb = _ps.enter_context(nc.sbuf_tensor("p1_xnb", [128, 2 * D], BF16))
            p1_st = _ps.enter_context(nc.sbuf_tensor("p1_st", [128, 16], F32))
            if True:
                b0 = dict(S)

                @block.sync
                def _(sync):
                    wait_barrier(sync, "p0")
                    for tt in range(NTT):
                        if tt >= 2:
                            sync.wait_ge(sems["sv"], b0["sv"] + (tt - 1) * 2)
                        dma(sync, "qP0" if tt % 2 == 0 else "qP1",
                            p1_x[:, (tt % 2) * D:(tt % 2 + 1) * D],
                            x[tt * 128:(tt + 1) * 128, :], tag=("p1x", tt))

                @block.gpsimd
                def _(gpsimd):
                    for tt in range(NTT):
                        gpsimd.wait_ge(sems["sv"], b0["sv"] + tt * 2 + 2)
                        qq = "qG0" if tt % 2 == 0 else "qG1"
                        dma(gpsimd, qq, h_xn[tt * 128:(tt + 1) * 128, :],
                            p1_xn[:, (tt % 2) * D:(tt % 2 + 1) * D])
                        dma(gpsimd, qq, h_xnb[tt * 128:(tt + 1) * 128, :],
                            p1_xnb[:, (tt % 2) * D:(tt % 2 + 1) * D], tag=("p1s", tt))
                @block.scalar
                def _(scalar):
                    wait_barrier(scalar, "p0", ("qG0", "qG1", "qG2", "qG3", "qG4"))
                    for tt in range(NTT):
                        o8 = (tt % 2) * 8
                        wtag(scalar, ("p1x", tt))
                        if tt >= 2:
                            scalar.wait_ge(sems["sv"], b0["sv"] + (tt - 2) * 2 + 1)
                        ins = scalar.activation(out=p1_sq[:, (tt % 2) * D:(tt % 2 + 1) * D],
                                                in_=p1_x[:, (tt % 2) * D:(tt % 2 + 1) * D],
                                                func=AF.Square)
                        inc(ins, "sa", 1)
                        scalar.wait_ge(sems["sv"], b0["sv"] + tt * 2 + 1)
                        scalar.activation(out=p1_st[:, o8 + 5:o8 + 6],
                                          in_=p1_st[:, o8 + 3:o8 + 4],
                                          func=AF.Ln, scale=cns[:, 3:4], bias=cns[:, 0:1])
                        scalar.activation(out=p1_st[:, o8 + 6:o8 + 7],
                                          in_=p1_st[:, o8 + 5:o8 + 6],
                                          func=AF.Exp, scale=cns[:, 2:3])
                        ins = scalar.copy(out=jnk[:], in_=p1_sq[:, (tt % 2) * D:(tt % 2) * D + 512])
                        inc(ins, "sa", 1)

                @block.vector
                def _(vector):
                    wait_barrier(vector, "p0", ("qG0", "qG1", "qG2", "qG3", "qG4"))
                    for tt in range(NTT):
                        o8 = (tt % 2) * 8
                        xs_ = p1_x[:, (tt % 2) * D:(tt % 2 + 1) * D]
                        xn_ = p1_xn[:, (tt % 2) * D:(tt % 2 + 1) * D]
                        xnb_ = p1_xnb[:, (tt % 2) * D:(tt % 2 + 1) * D]
                        sl = lambda c: p1_st[:, o8 + c:o8 + c + 1]
                        wtag(vector, ("p1x", tt))
                        vector.tensor_reduce(out=sl(0), in_=xs_, axis=AX, op=ALU.add)
                        vector.wait_ge(sems["sa"], b0["sa"] + tt * 2 + 1)
                        vector.tensor_reduce(out=sl(1), in_=p1_sq[:, (tt % 2) * D:(tt % 2 + 1) * D],
                                             axis=AX, op=ALU.add)
                        vector.tensor_scalar(out=sl(4), in0=sl(0), scalar1=-1.0 / D,
                                             scalar2=None, op0=ALU.mult)
                        vector.tensor_copy(out=jnk[:], in_=xs_[:, 0:512])
                        vector.scalar_tensor_tensor(out=sl(3), in0=sl(0), scalar=sl(4),
                                                    in1=sl(1), op0=ALU.mult, op1=ALU.add)
                        ins = vector.tensor_copy(out=jnk[:], in_=xs_[:, 0:512])
                        inc(ins, "sv", 1)
                        vector.wait_ge(sems["sa"], b0["sa"] + tt * 2 + 2)
                        if tt >= 2:
                            wtag(vector, ("p1s", tt - 2))
                        vector.tensor_scalar(out=xn_, in0=xs_, scalar1=sl(4), scalar2=sl(6),
                                             op0=ALU.add, op1=ALU.mult)
                        ins = vector.tensor_copy(out=xnb_, in_=xn_)
                        inc(ins, "sv", 1)

            _ps.close()
        barrier("p1")

        if 2 in phases:
            NGRP = 35

            def gspec(g):
                if g < 32:
                    return g * 128, 128
                return {32: (4096, 16), 33: (4112, 16), 34: (4128, 64)}[g]

            _ps = ExitStack()
            p2_w = _ps.enter_context(nc.sbuf_tensor("p2_w", [128, NK * OUTF], BF16))
            p2_xt = _ps.enter_context(nc.sbuf_tensor("p2_xt", [128, 2 * NK * WIN], BF16))
            p2_ev = _ps.enter_context(nc.sbuf_tensor("p2_ev", [128, 2 * WIN], F32))
            p2_evb = _ps.enter_context(nc.sbuf_tensor("p2_evb", [128, 2 * WIN], BF16))
            if True:
                b0 = dict(S)

                @block.sync
                def _(sync):
                    wait_barrier(sync, "p1")
                    for k in range(NK):
                        dma(sync, "qH", p2_w[:, k * OUTF:(k + 1) * OUTF],
                            h_WinTb[k * 128:(k + 1) * 128, :], tag="p2w")
                    for w in range(NW):
                        if w >= 2:
                            sync.wait_ge(sems["sp"], b0["sp"] + (w - 1) * NGRP)
                        off = (w % 2) * NK * WIN
                        for k in range(NK):
                            dmat(sync, "qT0" if w % 2 == 0 else "qT1",
                                 p2_xt[:, off + k * WIN:off + (k + 1) * WIN],
                                 h_xnb[w * WIN:(w + 1) * WIN, k * 128:(k + 1) * 128],
                                 tag=("p2xt", w))

                @block.gpsimd
                def _(gpsimd):
                    for w in range(NW):
                        tsl = slice(w * WIN, (w + 1) * WIN)
                        for g in range(NGRP):
                            G = w * NGRP + g
                            gpsimd.wait_ge(sems["sa"], b0["sa"] + G + 1)
                            eo = (G % 2) * WIN
                            qq = "qG0" if G % 2 == 0 else "qG1"
                            tg = ("p2s", G)
                            if g < 16:
                                dma(gpsimd, qq, h_xs[g * 128:(g + 1) * 128, tsl],
                                    p2_ev[:, eo:eo + WIN], tag=tg)
                            elif g < 32:
                                dma(gpsimd, qq, h_sz[(g - 16) * 128:(g - 15) * 128, tsl],
                                    p2_evb[:, eo:eo + WIN], tag=tg)
                            elif g == 32:
                                dma(gpsimd, qq, h_BT[:, tsl], p2_evb[0:16, eo:eo + WIN], tag=tg)
                            elif g == 33:
                                dma(gpsimd, qq, h_CT[:, tsl], p2_evb[0:16, eo:eo + WIN], tag=tg)
                            else:
                                dma(gpsimd, qq, h_dtpT[:, tsl], p2_evb[0:64, eo:eo + WIN], tag=tg)
                @block.tensor
                def _(tensor):
                    wait_barrier(tensor, "p1", ("qG0", "qG1"))
                    wtag(tensor, "p2w")
                    for w in range(NW):
                        wtag(tensor, ("p2xt", w))
                        off = (w % 2) * NK * WIN
                        for g in range(NGRP):
                            G = w * NGRP + g
                            if G >= 4:
                                tensor.wait_ge(sems["sa"], b0["sa"] + G - 3)
                            c0, M = gspec(g)
                            ps = psl(G % 4)
                            for k in range(NK):
                                ins = tensor.matmul(
                                    ps[0:M, :],
                                    p2_w[:, k * OUTF + c0:k * OUTF + c0 + M],
                                    p2_xt[:, off + k * WIN:off + (k + 1) * WIN],
                                    start=(k == 0), stop=(k == NK - 1))
                            inc(ins, "sp", 1)

                @block.scalar
                def _(scalar):
                    for w in range(NW):
                        for g in range(NGRP):
                            G = w * NGRP + g
                            scalar.wait_ge(sems["sp"], b0["sp"] + G + 1)
                            if G >= 2:
                                wtag(scalar, ("p2s", G - 2))
                            ps = psl(G % 4)
                            eo = (G % 2) * WIN
                            if g < 16:
                                ins = scalar.copy(out=p2_ev[:, eo:eo + WIN], in_=ps)
                            elif g < 32:
                                ins = scalar.activation(out=p2_evb[:, eo:eo + WIN],
                                                        in_=ps, func=AF.Silu)
                            elif g in (32, 33):
                                ins = scalar.copy(out=p2_evb[0:16, eo:eo + WIN], in_=ps[0:16, :])
                            else:
                                ins = scalar.copy(out=p2_evb[0:64, eo:eo + WIN], in_=ps[0:64, :])
                            inc(ins, "sa", 1)

            _ps.close()
        barrier("p2")

        if 3 in phases:
            _ps = ExitStack()
            p3_xs = _ps.enter_context(nc.sbuf_tensor("p3_xs", [128, 4 * 520], F32))
            p3_acc = _ps.enter_context(nc.sbuf_tensor("p3_acc", [128, 2 * WIN], F32))
            p3_u = _ps.enter_context(nc.sbuf_tensor("p3_u", [128, 4 * WIN], BF16))
            if True:
                b0 = dict(S)
                iters = [(w, j) for w in range(NW) for j in range(NJ)]

                @block.sync
                def _(sync):
                    wait_barrier(sync, "p2")
                    for i, (w, j) in enumerate(iters):
                        if i >= 4:
                            sync.wait_ge(sems["sv"], b0["sv"] + (i - 3))
                        off = (i % 4) * 520
                        qq = "qP%d" % (i % 4)
                        r = slice(j * 128, (j + 1) * 128)
                        if w == 0:
                            dma(sync, qq, p3_xs[:, off + 3:off + 3 + WIN], h_xs[r, 0:WIN],
                                tag=("p3x", i))
                        else:
                            dma(sync, qq, p3_xs[:, off:off + 3],
                                h_xs[r, w * WIN - 3:w * WIN])
                            dma(sync, qq, p3_xs[:, off + 3:off + 3 + WIN],
                                h_xs[r, w * WIN:(w + 1) * WIN], tag=("p3x", i))

                @block.gpsimd
                def _(gpsimd):
                    for i, (w, j) in enumerate(iters):
                        uo = (i % 4) * WIN
                        gpsimd.wait_ge(sems["sa"], b0["sa"] + i + 1)
                        dma(gpsimd, "qG%d" % (i % 4), h_u[j * 128:(j + 1) * 128, w * WIN:(w + 1) * WIN],
                            p3_u[:, uo:uo + WIN], tag=("p3s", i))
                @block.vector
                def _(vector):
                    wait_barrier(vector, "p2", ("qG0", "qG1", "qG2", "qG3", "qG4"))
                    for i, (w, j) in enumerate(iters):
                        off = (i % 4) * 520
                        ao = (i % 2) * WIN
                        wtag(vector, ("p3x", i))
                        if w == 0:
                            vector.memset(p3_xs[:, off:off + 3], 0.0)
                        if i >= 2:
                            vector.wait_ge(sems["sa"], b0["sa"] + i - 1)
                        vector.tensor_scalar(out=p3_acc[:, ao:ao + WIN],
                                             in0=p3_xs[:, off + 3:off + 3 + WIN],
                                             scalar1=convw[:, j * 4 + 3:j * 4 + 4],
                                             scalar2=None, op0=ALU.mult)
                        for k in (2, 1, 0):
                            ins = vector.scalar_tensor_tensor(
                                out=p3_acc[:, ao:ao + WIN],
                                in0=p3_xs[:, off + k:off + k + WIN],
                                scalar=convw[:, j * 4 + k:j * 4 + k + 1],
                                in1=p3_acc[:, ao:ao + WIN],
                                op0=ALU.mult, op1=ALU.add)
                        inc(ins, "sv", 1)

                @block.scalar
                def _(scalar):
                    for i, (w, j) in enumerate(iters):
                        ao = (i % 2) * WIN
                        uo = (i % 4) * WIN
                        scalar.wait_ge(sems["sv"], b0["sv"] + i + 1)
                        if i >= 4:
                            wtag(scalar, ("p3s", i - 4))
                        ins = scalar.activation(out=p3_u[:, uo:uo + WIN],
                                                in_=p3_acc[:, ao:ao + WIN],
                                                func=AF.Silu, bias=convb[:, j:j + 1])
                        inc(ins, "sa", 1)

            _ps.close()
        barrier("p3")

        if 4 in phases:
            _ps = ExitStack()
            p4_wdt = _ps.enter_context(nc.sbuf_tensor("p4_wdt", [128, DI], BF16))
            p4_dtp = _ps.enter_context(nc.sbuf_tensor("p4_dtp", [128, 2 * WIN], BF16))
            p4_sig = _ps.enter_context(nc.sbuf_tensor("p4_sig", [128, NJ * WIN], F32))
            p4_dtn = _ps.enter_context(nc.sbuf_tensor("p4_dtn", [128, 4 * WIN], F32))
            p4_u = _ps.enter_context(nc.sbuf_tensor("p4_u", [128, 4 * WIN], BF16))
            p4_dtu = _ps.enter_context(nc.sbuf_tensor("p4_dtu", [128, 4 * WIN], BF16))
            if True:
                b0 = dict(S)

                def sig_sa(G):
                    w_, j_ = divmod(G, NJ)
                    return b0["sa"] + w_ * 32 + j_ + 1

                @block.sync
                def _(sync):
                    wait_barrier(sync, "p3")
                    dma(sync, "qH", p4_wdt[0:RK, :], h_WdtTb, tag="p4w")
                    for w in range(NW):
                        if w >= 2:
                            sync.wait_ge(sems["sp"], b0["sp"] + (w - 1) * NJ)
                        dma(sync, "qP4" if w % 2 == 0 else "qP5",
                            p4_dtp[0:RK, (w % 2) * WIN:(w % 2 + 1) * WIN],
                            h_dtpT[:, w * WIN:(w + 1) * WIN], tag=("p4d", w))
                        for j in range(NJ):
                            it = w * NJ + j
                            if it >= 4:
                                sync.wait_ge(sems["sv"], b0["sv"] + it - 3)
                            dma(sync, "qP%d" % (it % 4),
                                p4_u[:, (it % 4) * WIN:(it % 4 + 1) * WIN],
                                h_u[j * 128:(j + 1) * 128, w * WIN:(w + 1) * WIN],
                                tag=("p4u", it))

                @block.gpsimd
                def _(gpsimd):
                    for w in range(NW):
                        for j in range(NJ):
                            G = w * NJ + j
                            gpsimd.wait_ge(sems["sv"], b0["sv"] + G + 1)
                            qq = "qG%d" % (G % 4)
                            dma(gpsimd, qq, h_dtneg[j * 128:(j + 1) * 128, w * WIN:(w + 1) * WIN],
                                p4_dtn[:, (G % 4) * WIN:(G % 4 + 1) * WIN])
                            dma(gpsimd, qq, h_dtu[j * 128:(j + 1) * 128, w * WIN:(w + 1) * WIN],
                                p4_dtu[:, (G % 4) * WIN:(G % 4 + 1) * WIN], tag=("p4s", G))
                @block.tensor
                def _(tensor):
                    wait_barrier(tensor, "p3", ())
                    wtag(tensor, "p4w")
                    for w in range(NW):
                        wtag(tensor, ("p4d", w))
                        for j in range(NJ):
                            G = w * NJ + j
                            if G >= 4:
                                tensor.wait_ge(sems["sa"], sig_sa(G - 4))
                            ins = tensor.matmul(psl(G % 4),
                                                p4_wdt[0:RK, j * 128:(j + 1) * 128],
                                                p4_dtp[0:RK, (w % 2) * WIN:(w % 2 + 1) * WIN],
                                                start=True, stop=True)
                            inc(ins, "sp", 1)

                @block.scalar
                def _(scalar):
                    for w in range(NW):
                        for j in range(NJ):
                            G = w * NJ + j
                            scalar.wait_ge(sems["sp"], b0["sp"] + G + 1)
                            ins = scalar.activation(out=p4_sig[:, j * WIN:(j + 1) * WIN],
                                                    in_=psl(G % 4), func=AF.Sigmoid,
                                                    scale=-1.0, bias=negb[:, j:j + 1])
                            inc(ins, "sa", 1)
                        for j in range(NJ):
                            G = w * NJ + j
                            if G >= 4:
                                wtag(scalar, ("p4s", G - 4))
                            ins = scalar.activation(out=p4_dtn[:, (G % 4) * WIN:(G % 4 + 1) * WIN],
                                                    in_=p4_sig[:, j * WIN:(j + 1) * WIN],
                                                    func=AF.Ln, bias=cns[:, 1:2])
                            inc(ins, "sa", 1)

                @block.vector
                def _(vector):
                    wait_barrier(vector, "p3", ())
                    for w in range(NW):
                        for j in range(NJ):
                            G = w * NJ + j
                            vector.wait_ge(sems["sa"], b0["sa"] + w * 32 + 16 + j + 1)
                            wtag(vector, ("p4u", G))
                            ins = vector.tensor_tensor(
                                out=p4_dtu[:, (G % 4) * WIN:(G % 4 + 1) * WIN],
                                in0=p4_dtn[:, (G % 4) * WIN:(G % 4 + 1) * WIN],
                                in1=p4_u[:, (G % 4) * WIN:(G % 4 + 1) * WIN],
                                op=ALU.mult)
                            inc(ins, "sv", 1)

            _ps.close()
        barrier("p4")

        if 5 in phases:
            SEL = NS * 128
            _ps = ExitStack()
            p5_sel = _ps.enter_context(nc.sbuf_tensor("p5_sel", [128, 2 * SEL], F32))
            p5_selb = _ps.enter_context(nc.sbuf_tensor("p5_selb", [128, 2 * SEL], BF16))
            p5_bt = _ps.enter_context(nc.sbuf_tensor("p5_bt", [128, 2 * WIN], BF16))
            p5_bc = _ps.enter_context(nc.sbuf_tensor("p5_bc", [128, 4 * WIN], BF16))
            if True:
                b0 = dict(S)

                @block.sync
                def _(sync):
                    wait_barrier(sync, "p4")
                    dma(sync, "qH", p5_sel[0:NS, 0:SEL], selNC)
                    dma(sync, "qH", p5_sel[0:NS, SEL:2 * SEL], selPC, tag="p5sel")
                    for w in range(NW):
                        if w >= 1:
                            sync.wait_ge(sems["sp"], b0["sp"] + w * 32)
                        qq = "qP4" if w % 2 == 0 else "qP5"
                        dma(sync, qq, p5_bt[0:NS, 0:WIN], h_BT[:, w * WIN:(w + 1) * WIN])
                        dma(sync, qq, p5_bt[0:NS, WIN:2 * WIN], h_CT[:, w * WIN:(w + 1) * WIN],
                            tag=("p5b", w))

                @block.gpsimd
                def _(gpsimd):
                    for w in range(NW):
                        for g in range(32):
                            G = w * 32 + g
                            n = g % 16
                            gpsimd.wait_ge(sems["sa"], b0["sa"] + G + 1)
                            dst = h_Bbc if g < 16 else h_Cbc
                            dma(gpsimd, "qG%d" % (G % 4), dst[n * 128:(n + 1) * 128, w * WIN:(w + 1) * WIN],
                                p5_bc[:, (G % 4) * WIN:(G % 4 + 1) * WIN], tag=("p5s", G))
                @block.vector
                def _(vector):
                    wait_barrier(vector, "p4", ())
                    wtag(vector, "p5sel")
                    ins = vector.tensor_copy(out=p5_selb[0:NS, :], in_=p5_sel[0:NS, :])
                    inc(ins, "sv", 1)

                @block.tensor
                def _(tensor):
                    tensor.wait_ge(sems["sv"], b0["sv"] + 1)
                    for w in range(NW):
                        wtag(tensor, ("p5b", w))
                        for g in range(32):
                            G = w * 32 + g
                            n = g % 16
                            isB = g < 16
                            if G >= 4:
                                tensor.wait_ge(sems["sa"], b0["sa"] + G - 3)
                            so_ = 0 if isB else SEL
                            bo = 0 if isB else WIN
                            ins = tensor.matmul(psl(G % 4),
                                                p5_selb[0:NS, so_ + n * 128:so_ + (n + 1) * 128],
                                                p5_bt[0:NS, bo:bo + WIN],
                                                start=True, stop=True)
                            inc(ins, "sp", 1)

                @block.scalar
                def _(scalar):
                    for w in range(NW):
                        for g in range(32):
                            G = w * 32 + g
                            scalar.wait_ge(sems["sp"], b0["sp"] + G + 1)
                            if G >= 4:
                                wtag(scalar, ("p5s", G - 4))
                            ins = scalar.copy(out=p5_bc[:, (G % 4) * WIN:(G % 4 + 1) * WIN],
                                              in_=psl(G % 4))
                            inc(ins, "sa", 1)

            _ps.close()
        barrier("p5")

        if 6 in phases:
            _ps = ExitStack()
            p6_dtn = _ps.enter_context(nc.sbuf_tensor("p6_dtn", [128, L], F32))
            p6_dA = _ps.enter_context(nc.sbuf_tensor("p6_dA", [128, 2 * L], F32))
            p6_dtu = _ps.enter_context(nc.sbuf_tensor("p6_dtu", [128, L], BF16))
            p6_u = _ps.enter_context(nc.sbuf_tensor("p6_u", [128, L], BF16))
            p6_sz = _ps.enter_context(nc.sbuf_tensor("p6_sz", [128, L], BF16))
            p6_bcB = _ps.enter_context(nc.sbuf_tensor("p6_bcB", [128, 2 * L], BF16))
            p6_bcC = _ps.enter_context(nc.sbuf_tensor("p6_bcC", [128, 2 * L], BF16))
            p6_X = _ps.enter_context(nc.sbuf_tensor("p6_X", [128, 2 * L], BF16))
            p6_h = _ps.enter_context(nc.sbuf_tensor("p6_h", [128, 2 * L], BF16))
            p6_hC = _ps.enter_context(nc.sbuf_tensor("p6_hC", [128, 2 * L], BF16))
            p6_t1 = _ps.enter_context(nc.sbuf_tensor("p6_t1", [128, L], F32))
            p6_yg = _ps.enter_context(nc.sbuf_tensor("p6_yg", [128, L], BF16))
            if True:
                b0 = dict(S)
                PJ = {"sa": 16, "sv": 51, "sp": 16}

                def jb(j, name):
                    return b0[name] + j * PJ[name]

                @block.sync
                def _(sync):
                    wait_barrier(sync, "p5")
                    for j in range(NJ):
                        r = slice(j * 128, (j + 1) * 128)
                        if j >= 1:
                            sync.wait_ge(sems["sv"], jb(j, "sv"))
                            sync.wait_ge(sems["sa"], jb(j, "sa"))
                        dma(sync, "qH", p6_dtn[:], h_dtneg[r, :])
                        dma(sync, "qH", p6_dtu[:], h_dtu[r, :])
                        dma(sync, "qH", p6_u[:], h_u[r, :])
                        dma(sync, "qH", p6_sz[:], h_sz[r, :], tag=("p6j", j))
                        for n in range(NS):
                            if n >= 2:
                                sync.wait_ge(sems["sv"], jb(j, "sv") + 3 * n - 3)
                            nb = (n % 2) * L
                            qq = "qP0" if n % 2 == 0 else "qP1"
                            dma(sync, qq, p6_bcB[:, nb:nb + L], h_Bbc[n * 128:(n + 1) * 128, :])
                            dma(sync, qq, p6_bcC[:, nb:nb + L], h_Cbc[n * 128:(n + 1) * 128, :],
                                tag=("p6bc", j, n))

                @block.gpsimd
                def _(gpsimd):
                    wait_barrier(gpsimd, "p5", ())
                    for j in range(NJ):
                        gpsimd.wait_ge(sems["sv"], jb(j, "sv") + PJ["sv"])
                        dma(gpsimd, "qG4", h_yg[j * 128:(j + 1) * 128, :], p6_yg[:],
                            tag=("p6s", j))

                @block.scalar
                def _(scalar):
                    wait_barrier(scalar, "p5", ())
                    for j in range(NJ):
                        wtag(scalar, ("p6j", j))
                        for n in range(NS):
                            if n >= 2:
                                scalar.wait_ge(sems["sv"], jb(j, "sv") + 3 * n - 4)
                            ins = scalar.activation(out=p6_dA[:, (n % 2) * L:(n % 2 + 1) * L],
                                                    in_=p6_dtn[:], func=AF.Exp,
                                                    scale=A_pos[:, j * NS + n:j * NS + n + 1])
                            inc(ins, "sa", 1)

                @block.vector
                def _(vector):
                    wait_barrier(vector, "p5", ())
                    for j in range(NJ):
                        for n in range(NS):
                            nb = (n % 2) * L
                            if n == 0:
                                wtag(vector, ("p6j", j))
                            wtag(vector, ("p6bc", j, n))
                            ins = vector.tensor_tensor(out=p6_X[:, nb:nb + L],
                                                       in0=p6_dtu[:],
                                                       in1=p6_bcB[:, nb:nb + L],
                                                       op=ALU.mult)
                            inc(ins, "sv", 1)
                            vector.wait_ge(sems["sa"], jb(j, "sa") + n + 1)
                            ins = vector.tensor_tensor_scan(out=p6_h[:, nb:nb + L],
                                                            data0=p6_dA[:, nb:nb + L],
                                                            data1=p6_X[:, nb:nb + L],
                                                            initial=0.0,
                                                            op0=ALU.mult, op1=ALU.add)
                            inc(ins, "sv", 1)
                            if n >= 2:
                                vector.wait_ge(sems["sp"], jb(j, "sp") + n - 1)
                            ins = vector.tensor_tensor(out=p6_hC[:, nb:nb + L],
                                                       in0=p6_h[:, nb:nb + L],
                                                       in1=p6_bcC[:, nb:nb + L],
                                                       op=ALU.mult)
                            inc(ins, "sv", 1)
                        vector.wait_ge(sems["sp"], jb(j, "sp") + 16)
                        ins = vector.scalar_tensor_tensor(out=p6_t1[:, 0:2048],
                                                          in0=p6_u[:, 0:2048],
                                                          scalar=D_col[:, j:j + 1],
                                                          in1=pY1[:],
                                                          op0=ALU.mult, op1=ALU.add)
                        inc(ins, "sv", 1)
                        ins = vector.scalar_tensor_tensor(out=p6_t1[:, 2048:4096],
                                                          in0=p6_u[:, 2048:4096],
                                                          scalar=D_col[:, j:j + 1],
                                                          in1=pY2[:],
                                                          op0=ALU.mult, op1=ALU.add)
                        inc(ins, "sv", 1)
                        if j >= 1:
                            wtag(vector, ("p6s", j - 1))
                        ins = vector.tensor_tensor(out=p6_yg[:], in0=p6_t1[:],
                                                   in1=p6_sz[:], op=ALU.mult)
                        inc(ins, "sv", 1)

                @block.tensor
                def _(tensor):
                    wait_barrier(tensor, "p5", ())
                    for j in range(NJ):
                        for n in range(NS):
                            nb = (n % 2) * L
                            tensor.wait_ge(sems["sv"], jb(j, "sv") + 3 * n + 3)
                            if n == 0 and j >= 1:
                                tensor.wait_ge(sems["sv"], jb(j, "sv") - 1)
                            for s in range(8):
                                ins = tensor.matmul(psl(s), identb[:],
                                                    p6_hC[:, nb + s * 512:nb + (s + 1) * 512],
                                                    start=(n == 0), stop=(n == NS - 1))
                            inc(ins, "sp", 1)
            _ps.close()
        barrier("p6")

        if 7 in phases:
            _ps = ExitStack()
            p7_wout = _ps.enter_context(nc.sbuf_tensor("p7_wout", [128, NJ * D], BF16))
            p7_cbt = _ps.enter_context(nc.sbuf_tensor("p7_cbt", [128, NK * D], BF16))
            p7_yg = _ps.enter_context(nc.sbuf_tensor("p7_yg", [128, 2 * NJ * 128], BF16))
            p7_xt = _ps.enter_context(nc.sbuf_tensor("p7_xt", [128, 2 * NK * 128], BF16))
            p7_xn = _ps.enter_context(nc.sbuf_tensor("p7_xn", [128, 2 * D], F32))
            p7_x = _ps.enter_context(nc.sbuf_tensor("p7_x", [128, 2 * D], F32))
            p7_fa = _ps.enter_context(nc.sbuf_tensor("p7_fa", [128, D], F32))
            p7_sq = _ps.enter_context(nc.sbuf_tensor("p7_sq", [128, D], F32))
            p7_fb = _ps.enter_context(nc.sbuf_tensor("p7_fb", [128, D], F32))
            p7_out = _ps.enter_context(nc.sbuf_tensor("p7_out", [128, 2 * D], F32))
            p7_st = _ps.enter_context(nc.sbuf_tensor("p7_st", [128, 16], F32))
            p7_row = _ps.enter_context(nc.sbuf_tensor("p7_row", [128, 4 * D], F32))
            p7_gbc = _ps.enter_context(nc.sbuf_tensor("p7_gbc", [128, D], F32))
            p7_sbc = _ps.enter_context(nc.sbuf_tensor("p7_sbc", [128, D], F32))
            p7_bbc = _ps.enter_context(nc.sbuf_tensor("p7_bbc", [128, D], F32))
            p7_tbc = _ps.enter_context(nc.sbuf_tensor("p7_tbc", [128, D], F32))
            if True:
                b0 = dict(S)

                @block.sync
                def _(sync):
                    wait_barrier(sync, "p6")
                    dma(sync, "qH", p7_row[0:1, :], lnrow, tag="p7row")
                    for j in range(NJ):
                        dma(sync, "qH", p7_wout[:, j * D:(j + 1) * D],
                            h_WoutTb[j * 128:(j + 1) * 128, :])
                    for k in range(NK):
                        dma(sync, "qH", p7_cbt[:, k * D:(k + 1) * D],
                            h_CBTb[k * 128:(k + 1) * 128, :], tag="p7h")
                    for tt in range(NTT):
                        if tt >= 2:
                            sync.wait_ge(sems["sv"], b0["sv"] + 1 + (tt - 1) * 9)
                        off = tt % 2
                        qq = "qP0" if tt % 2 == 0 else "qP1"
                        r = slice(tt * 128, (tt + 1) * 128)
                        for j in range(NJ):
                            dma(sync, qq,
                                p7_yg[:, off * NJ * 128 + j * 128:off * NJ * 128 + (j + 1) * 128],
                                h_yg[j * 128:(j + 1) * 128, r])
                        for k in range(NK):
                            dmat(sync, "qT0" if tt % 2 == 0 else "qT1",
                                 p7_xt[:, off * NK * 128 + k * 128:off * NK * 128 + (k + 1) * 128],
                                 h_xnb[r, k * 128:(k + 1) * 128], tag=("p7xt", tt))
                        dma(sync, qq, p7_xn[:, off * D:(off + 1) * D], h_xn[r, :])
                        dma(sync, qq, p7_x[:, off * D:(off + 1) * D], x[r, :], tag=("p7l", tt))

                @block.tensor
                def _(tensor):
                    wait_barrier(tensor, "p6")
                    wtag(tensor, "p7row")
                    for i in range(4):
                        for half in range(2):
                            ins = tensor.matmul(psl(i * 2 + half), onesf[0:1, :],
                                                p7_row[0:1, i * D + half * 512:i * D + half * 512 + 512],
                                                start=True, stop=True)
                        inc(ins, "sp", 1)

                @block.scalar
                def _(scalar):
                    wait_barrier(scalar, "p6", ())
                    scalar.wait_ge(sems["sp"], b0["sp"] + 4)
                    for half in range(2):
                        scalar.copy(out=p7_gbc[:, half * 512:(half + 1) * 512], in_=psl(0 + half))
                        scalar.copy(out=p7_bbc[:, half * 512:(half + 1) * 512], in_=psl(2 + half))
                        scalar.copy(out=p7_sbc[:, half * 512:(half + 1) * 512], in_=psl(4 + half))
                        ins = scalar.copy(out=p7_tbc[:, half * 512:(half + 1) * 512], in_=psl(6 + half))
                    inc(ins, "sa", 1)

                @block.vector
                def _(vector):
                    wait_barrier(vector, "p6", ())
                    vector.wait_ge(sems["sa"], b0["sa"] + 1)
                    vector.tensor_scalar(out=p7_sbc[:], in0=p7_sbc[:],
                                         scalar1=1.0 / 127.0, scalar2=None, op0=ALU.mult)
                    ins = vector.tensor_tensor(out=p7_bbc[:], in0=p7_bbc[:],
                                               in1=p7_tbc[:], op=ALU.add)
                    inc(ins, "sv", 1)

                bp = dict(S)

                @block.gpsimd
                def _(gpsimd):
                    for tt in range(NTT):
                        off = tt % 2
                        gpsimd.wait_ge(sems["sv"], bp["sv"] + (tt + 1) * 9)
                        dma(gpsimd, "qG0" if tt % 2 == 0 else "qG1",
                            out[tt * 128:(tt + 1) * 128, :],
                            p7_out[:, off * D:(off + 1) * D], tag=("p7o", tt))

                @block.tensor
                def _(tensor):
                    tensor.wait_ge(sems["sv"], bp["sv"])
                    for tt in range(NTT):
                        off = tt % 2
                        wtag(tensor, ("p7l", tt))
                        wtag(tensor, ("p7xt", tt))
                        if tt >= 2:
                            tensor.wait_ge(sems["sv"], bp["sv"] + (tt - 1) * 9)
                        for half in range(2):
                            for j in range(NJ):
                                ins = tensor.matmul(
                                    psl(half),
                                    p7_yg[:, off * NJ * 128 + j * 128:off * NJ * 128 + (j + 1) * 128],
                                    p7_wout[:, j * D + half * 512:j * D + half * 512 + 512],
                                    start=(j == 0), stop=(j == NJ - 1))
                            inc(ins, "sp", 1)
                        for half in range(2):
                            for k in range(NK):
                                ins = tensor.matmul(
                                    psl(4 + half),
                                    p7_xt[:, off * NK * 128 + k * 128:off * NK * 128 + (k + 1) * 128],
                                    p7_cbt[:, k * D + half * 512:k * D + half * 512 + 512],
                                    start=(k == 0), stop=(k == NK - 1))
                            inc(ins, "sp", 1)

                @block.scalar
                def _(scalar):
                    for tt in range(NTT):
                        scalar.wait_ge(sems["sv"], bp["sv"] + tt * 9 + 1)
                        ins = scalar.activation(out=p7_sq[:], in_=p7_fa[:], func=AF.Square)
                        inc(ins, "sa", 1)
                        scalar.wait_ge(sems["sv"], bp["sv"] + tt * 9 + 2)
                        scalar.activation(out=p7_st[:, 5:6], in_=p7_st[:, 3:4],
                                          func=AF.Ln, scale=cns[:, 3:4], bias=cns[:, 0:1])
                        scalar.activation(out=p7_st[:, 6:7], in_=p7_st[:, 5:6],
                                          func=AF.Exp, scale=cns[:, 2:3])
                        ins = scalar.copy(out=jnk[:], in_=p7_sq[:, 0:512])
                        inc(ins, "sa", 1)

                @block.vector
                def _(vector):
                    for tt in range(NTT):
                        off = tt % 2
                        sl = lambda c: p7_st[:, c:c + 1]
                        vector.wait_ge(sems["sp"], bp["sp"] + tt * 4 + 2)
                        wtag(vector, ("p7l", tt))
                        vector.tensor_tensor(out=p7_fa[:, 0:512], in0=pY1[:, 0:512],
                                             in1=p7_xn[:, off * D:off * D + 512], op=ALU.add)
                        ins = vector.tensor_tensor(out=p7_fa[:, 512:1024], in0=pY1[:, 512:1024],
                                                   in1=p7_xn[:, off * D + 512:off * D + 1024],
                                                   op=ALU.add)
                        inc(ins, "sv", 1)
                        vector.tensor_reduce(out=sl(0), in_=p7_fa[:], axis=AX, op=ALU.add)
                        vector.wait_ge(sems["sa"], bp["sa"] + tt * 2 + 1)
                        vector.tensor_reduce(out=sl(1), in_=p7_sq[:], axis=AX, op=ALU.add)
                        vector.tensor_scalar(out=sl(4), in0=sl(0), scalar1=-1.0 / D,
                                             scalar2=None, op0=ALU.mult)
                        vector.tensor_copy(out=jnk[:], in_=p7_fa[:, 0:512])
                        vector.scalar_tensor_tensor(out=sl(3), in0=sl(0), scalar=sl(4),
                                                    in1=sl(1), op0=ALU.mult, op1=ALU.add)
                        ins = vector.tensor_copy(out=jnk[:], in_=p7_fa[:, 0:512])
                        inc(ins, "sv", 1)
                        vector.wait_ge(sems["sa"], bp["sa"] + tt * 2 + 2)
                        ins = vector.tensor_scalar(out=p7_fb[:], in0=p7_fa[:], scalar1=sl(4),
                                                   scalar2=sl(6), op0=ALU.add, op1=ALU.mult)
                        inc(ins, "sv", 1)
                        ins = vector.tensor_tensor(out=p7_fb[:], in0=p7_fb[:], in1=p7_gbc[:],
                                                   op=ALU.mult)
                        inc(ins, "sv", 1)
                        vector.wait_ge(sems["sp"], bp["sp"] + tt * 4 + 4)
                        vector.tensor_tensor(out=p7_fa[:, 0:512], in0=pY2[:, 0:512],
                                             in1=p7_sbc[:, 0:512], op=ALU.mult)
                        ins = vector.tensor_tensor(out=p7_fa[:, 512:1024], in0=pY2[:, 512:1024],
                                                   in1=p7_sbc[:, 512:1024], op=ALU.mult)
                        inc(ins, "sv", 1)
                        ins = vector.tensor_tensor(out=p7_fb[:], in0=p7_fb[:], in1=p7_fa[:],
                                                   op=ALU.add)
                        inc(ins, "sv", 1)
                        ins = vector.tensor_tensor(out=p7_fb[:], in0=p7_fb[:], in1=p7_bbc[:],
                                                   op=ALU.add)
                        inc(ins, "sv", 1)
                        if tt >= 2:
                            wtag(vector, ("p7o", tt - 2))
                        ins = vector.tensor_tensor(out=p7_out[:, off * D:(off + 1) * D],
                                                   in0=p7_fb[:],
                                                   in1=p7_x[:, off * D:(off + 1) * D],
                                                   op=ALU.add)
                        inc(ins, "sv", 2)

            _ps.close()
        barrier("p7")

        @block.gpsimd
        def _(gpsimd):
            for n in DMA_SEMS:
                if S[n] > 0:
                    gpsimd.wait_ge(sems[n], S[n])

    import os
    if os.environ.get("MK_PRINT_S"):
        print("FINAL S:", S)
        print("n instructions:", sum(1 for _ in nc.all_instructions()))
    return nc


def _prep(inputs):
    return {
        "WinT": np.ascontiguousarray(inputs["L0_attn_W_in"].T),
        "WdtT": np.ascontiguousarray(inputs["L0_attn_W_dt"].T),
        "WoutT": np.ascontiguousarray(inputs["L0_attn_W_out"].T),
        "CBT": np.ascontiguousarray(inputs["L0_ffn_CB"].T),
        "A_logr": np.ascontiguousarray(inputs["L0_attn_A_log"]),
        "bdt_c": np.ascontiguousarray(inputs["L0_attn_b_dt"].reshape(NJ, 128).T),
        "convw_c": np.ascontiguousarray(
            inputs["L0_attn_conv_w"][:, 0, :].reshape(NJ, 128, 4).transpose(1, 0, 2).reshape(128, NJ * 4)),
        "convb_c": np.ascontiguousarray(inputs["L0_attn_conv_b"].reshape(NJ, 128).T),
        "D_c": np.ascontiguousarray(inputs["L0_attn_D"].reshape(NJ, 128).T),
        "selPC": np.kron(np.eye(NS, dtype=np.float32), np.ones((1, 128), np.float32)),
        "selNC": -np.kron(np.eye(NS, dtype=np.float32), np.ones((1, 128), np.float32)),
        "ident": np.eye(128, dtype=np.float32),
        "ones1": np.ones((1, 128), np.float32),
        "lnrow": np.concatenate([inputs["L0_attn_ln_g"], inputs["L0_attn_ln_b"],
                                 inputs["L0_ffn_SCB"], inputs["L0_ffn_bias"]])[None, :].astype(np.float32),
    }


def kernel(**inputs):
    shared = _prep(inputs)
    xfull = np.ascontiguousarray(np.asarray(inputs["L0_attn_x"], dtype=np.float32))
    nc = build_nc()
    in_maps = []
    for b in range(8):
        m = dict(shared)
        m["x"] = np.ascontiguousarray(xfull[b])
        in_maps.append(m)
    res = run_bass_kernel_spmd(nc, in_maps, core_ids=list(range(8)))
    return np.stack([r["out"] for r in res.results], axis=0)


# version 05
# speedup vs baseline: 1.1466x; 1.0629x over previous
import numpy as np
from contextlib import ExitStack
import concourse.bass as bass
import concourse.mybir as mybir
from concourse.bass_utils import run_bass_kernel_spmd

F32 = mybir.dt.float32
BF16 = mybir.dt.bfloat16
AF = mybir.ActivationFunctionType
ALU = mybir.AluOpType
AX = mybir.AxisListType.X

L, D, DI, NS, RK, OUTF = 4096, 1024, 2048, 16, 64, 4192
NW, WIN, NJ, NK, NTT = 8, 512, 16, 8, 32
EPS = 1e-5
ALL_PHASES = (0, 1, 2, 3, 4, 5, 6, 7)


def build_nc(phases=ALL_PHASES, debug=()):
    nc = bass.Bass("TRN2", target_bir_lowering=False, debug=False)

    def dram(name, shape, dtype):
        k = "ExternalOutput" if name in debug else "Internal"
        return nc.dram_tensor(name, shape, dtype, kind=k).ap()

    x = nc.dram_tensor("x", [L, D], F32, kind="ExternalInput").ap()
    WinT = nc.dram_tensor("WinT", [D, OUTF], F32, kind="ExternalInput").ap()
    WdtT = nc.dram_tensor("WdtT", [RK, DI], F32, kind="ExternalInput").ap()
    WoutT = nc.dram_tensor("WoutT", [DI, D], F32, kind="ExternalInput").ap()
    CBT = nc.dram_tensor("CBT", [D, D], F32, kind="ExternalInput").ap()
    A_logr = nc.dram_tensor("A_logr", [DI, NS], F32, kind="ExternalInput").ap()
    bdt_c = nc.dram_tensor("bdt_c", [128, NJ], F32, kind="ExternalInput").ap()
    convw_c = nc.dram_tensor("convw_c", [128, NJ * 4], F32, kind="ExternalInput").ap()
    convb_c = nc.dram_tensor("convb_c", [128, NJ], F32, kind="ExternalInput").ap()
    D_c = nc.dram_tensor("D_c", [128, NJ], F32, kind="ExternalInput").ap()
    selPC = nc.dram_tensor("selPC", [NS, NS * 128], F32, kind="ExternalInput").ap()
    selNC = nc.dram_tensor("selNC", [NS, NS * 128], F32, kind="ExternalInput").ap()
    ident = nc.dram_tensor("ident", [128, 128], F32, kind="ExternalInput").ap()
    ones1 = nc.dram_tensor("ones1", [1, 128], F32, kind="ExternalInput").ap()
    lnrow = nc.dram_tensor("lnrow", [1, 4 * D], F32, kind="ExternalInput").ap()
    out = nc.dram_tensor("out", [L, D], F32, kind="ExternalOutput").ap()

    h_WinTb = dram("h_WinTb", [D, OUTF], BF16)
    h_WdtTb = dram("h_WdtTb", [RK, DI], BF16)
    h_WoutTb = dram("h_WoutTb", [DI, D], BF16)
    h_CBTb = dram("h_CBTb", [D, D], BF16)
    h_xn = dram("h_xn", [L, D], F32)
    h_xnb = dram("h_xnb", [L, D], BF16)
    h_xs = dram("h_xs", [DI, L], F32)
    h_sz = dram("h_sz", [DI, L], BF16)
    h_u = dram("h_u", [DI, L], BF16)
    h_BT = dram("h_BT", [NS, L], BF16)
    h_CT = dram("h_CT", [NS, L], BF16)
    h_dtpT = dram("h_dtpT", [RK, L], BF16)
    h_dtneg = dram("h_dtneg", [DI, L], BF16)
    h_dtu = dram("h_dtu", [DI, L], BF16)
    h_Bbc = dram("h_Bbc", [NS * 128, L], BF16)
    h_Cbc = dram("h_Cbc", [NS * 128, L], BF16)
    h_yg = dram("h_yg", [DI, L], BF16)


    DMA_SEMS = ["qH", "qP0", "qP1", "qP2", "qP3", "qP4", "qP5",
                "qG0", "qG1", "qG2", "qG3", "qG4", "qT0", "qT1"]
    S = {n: 0 for n in ["sv", "sa", "sp", "so"] + DMA_SEMS}
    sems = {}
    rec = {}

    def inc(ins, name, n=1):
        S[name] += n
        ins.then_inc(sems[name], n)
        return S[name]

    def dma(eng, qname, out_, in_, tag=None, **kw):
        v = inc(eng.dma_start(out=out_, in_=in_, **kw), qname, 16)
        if tag is not None:
            rec[tag] = (qname, v)
        return v

    def dmat(eng, qname, out_, in_, tag=None):
        v = inc(eng.dma_start_transpose(out=out_, in_=in_), qname, 16)
        if tag is not None:
            rec[tag] = (qname, v)
        return v

    def wtag(eng, tag):
        qn, v = rec[tag]
        eng.wait_ge(sems[qn], v)

    barriers = {}

    def barrier(tag):
        barriers[tag] = dict(S)

    def wait_barrier(eng, tag, names=None):
        bc = barriers[tag]
        if names is None:
            names = list(bc.keys())
        for n in names:
            if bc[n] > 0:
                eng.wait_ge(sems[n], bc[n])

    with ExitStack() as _stk:
        ent = _stk.enter_context
        for _nm in list(S.keys()):
            sems[_nm] = ent(nc.semaphore(_nm))
        A_pos = ent(nc.sbuf_tensor("A_pos", [128, NJ * NS], F32))
        negb = ent(nc.sbuf_tensor("negb", [128, NJ], F32))
        convw = ent(nc.sbuf_tensor("convw", [128, NJ * 4], F32))
        convb = ent(nc.sbuf_tensor("convb", [128, NJ], F32))
        D_col = ent(nc.sbuf_tensor("D_col", [128, NJ], F32))
        identb = ent(nc.sbuf_tensor("identb", [128, 128], BF16))
        identf = ent(nc.sbuf_tensor("identf", [128, 128], F32))
        onesf = ent(nc.sbuf_tensor("onesf", [128, 128], F32))
        cns = ent(nc.sbuf_tensor("cns", [128, 4], F32))
        jnk = ent(nc.sbuf_tensor("jnk", [128, 512], F32))
        pY1 = ent(nc.psum_tensor("pY1", [128, 2048], F32))
        pY2 = ent(nc.psum_tensor("pY2", [128, 2048], F32))
        block = ent(nc.Block())

        def psl(i):
            return (pY1 if i < 4 else pY2)[:, (i % 4) * 512:(i % 4 + 1) * 512]

        if 0 in phases:
            _ps = ExitStack()
            p0_sg = _ps.enter_context(nc.sbuf_tensor("p0_sg", [128, 2 * OUTF], F32))
            p0_ot = _ps.enter_context(nc.sbuf_tensor("p0_ot", [128, 2 * OUTF], BF16))
            if True:
                b0 = dict(S)
                pieces = []
                for k in range(NK):
                    pieces.append((WinT[k * 128:(k + 1) * 128, :], h_WinTb[k * 128:(k + 1) * 128, :], OUTF))
                for j in range(NJ):
                    pieces.append((WoutT[j * 128:(j + 1) * 128, :], h_WoutTb[j * 128:(j + 1) * 128, :], D))
                for k in range(NK):
                    pieces.append((CBT[k * 128:(k + 1) * 128, :], h_CBTb[k * 128:(k + 1) * 128, :], D))

                @block.sync
                def _(sync):
                    dma(sync, "qH", A_pos[:].rearrange("p (j n) -> p j n", n=NS), A_logr.rearrange("(j p) n -> p j n", p=128), tag="p0a")
                    dma(sync, "qH", negb[:], bdt_c)
                    dma(sync, "qH", convw[:], convw_c)
                    dma(sync, "qH", convb[:], convb_c)
                    dma(sync, "qH", D_col[:], D_c)
                    dma(sync, "qH", identf[:], ident)
                    dma(sync, "qH", onesf[0:1, :], ones1)
                    dma(sync, "qH", p0_sg[0:RK, 0:DI], WdtT, tag="p0h")
                    for i, (src, dst, cols) in enumerate(pieces):
                        if i == 0:
                            sync.wait_ge(sems["sv"], b0["sv"] + 1)
                        if i >= 2:
                            sync.wait_ge(sems["sv"], b0["sv"] + 1 + (i - 1))
                        off = (i % 2) * OUTF
                        dma(sync, "qP0" if i % 2 == 0 else "qP1", p0_sg[:, off:off + cols], src, tag=("p0p", i))

                @block.gpsimd
                def _(gpsimd):
                    gpsimd.wait_ge(sems["sv"], b0["sv"] + 1)
                    dma(gpsimd, "qG4", h_WdtTb, p0_ot[0:RK, 0:DI])
                    for i, (src, dst, cols) in enumerate(pieces):
                        gpsimd.wait_ge(sems["sv"], b0["sv"] + i + 2)
                        off = (i % 2) * OUTF
                        dma(gpsimd, "qG0" if i % 2 == 0 else "qG1", dst, p0_ot[:, off:off + cols], tag=("p0s", i))
                @block.vector
                def _(vector):
                    vector.memset(cns[:, 0:1], float(EPS))
                    vector.memset(cns[:, 1:2], 1e-38)
                    vector.memset(cns[:, 2:3], -0.5)
                    vector.memset(cns[:, 3:4], 1.0 / D)
                    vector.memset(jnk[:], 0.0)
                    wtag(vector, "p0h")
                    vector.tensor_copy(out=identb[:], in_=identf[:])
                    vector.tensor_scalar(out=negb[:], in0=negb[:], scalar1=-1.0,
                                         scalar2=None, op0=ALU.mult)
                    ins = vector.tensor_copy(out=p0_ot[0:RK, 0:DI], in_=p0_sg[0:RK, 0:DI])
                    inc(ins, "sv", 1)
                    for i, (src, dst, cols) in enumerate(pieces):
                        wtag(vector, ("p0p", i))
                        if i >= 2:
                            wtag(vector, ("p0s", i - 2))
                        off = (i % 2) * OUTF
                        ins = vector.tensor_copy(out=p0_ot[:, off:off + cols],
                                                 in_=p0_sg[:, off:off + cols])
                        inc(ins, "sv", 1)

                @block.scalar
                def _(scalar):
                    wtag(scalar, "p0a")
                    ins = scalar.activation(out=A_pos[:], in_=A_pos[:], func=AF.Exp)
                    inc(ins, "sa", 1)

            _ps.close()
        barrier("p0")

        if 1 in phases:
            _ps = ExitStack()
            p1_x = _ps.enter_context(nc.sbuf_tensor("p1_x", [128, 2 * D], F32))
            p1_sq = _ps.enter_context(nc.sbuf_tensor("p1_sq", [128, 2 * D], F32))
            p1_xn = _ps.enter_context(nc.sbuf_tensor("p1_xn", [128, 2 * D], F32))
            p1_xnb = _ps.enter_context(nc.sbuf_tensor("p1_xnb", [128, 2 * D], BF16))
            p1_st = _ps.enter_context(nc.sbuf_tensor("p1_st", [128, 16], F32))
            if True:
                b0 = dict(S)

                @block.sync
                def _(sync):
                    wait_barrier(sync, "p0")
                    for tt in range(NTT):
                        if tt >= 2:
                            sync.wait_ge(sems["sv"], b0["sv"] + (tt - 1) * 2)
                        dma(sync, "qP0" if tt % 2 == 0 else "qP1",
                            p1_x[:, (tt % 2) * D:(tt % 2 + 1) * D],
                            x[tt * 128:(tt + 1) * 128, :], tag=("p1x", tt))

                @block.gpsimd
                def _(gpsimd):
                    for tt in range(NTT):
                        gpsimd.wait_ge(sems["sv"], b0["sv"] + tt * 2 + 2)
                        qq = "qG0" if tt % 2 == 0 else "qG1"
                        dma(gpsimd, qq, h_xn[tt * 128:(tt + 1) * 128, :],
                            p1_xn[:, (tt % 2) * D:(tt % 2 + 1) * D])
                        dma(gpsimd, qq, h_xnb[tt * 128:(tt + 1) * 128, :],
                            p1_xnb[:, (tt % 2) * D:(tt % 2 + 1) * D], tag=("p1s", tt))
                @block.scalar
                def _(scalar):
                    wait_barrier(scalar, "p0", ("qG0", "qG1", "qG2", "qG3", "qG4"))
                    for tt in range(NTT):
                        o8 = (tt % 2) * 8
                        wtag(scalar, ("p1x", tt))
                        if tt >= 2:
                            scalar.wait_ge(sems["sv"], b0["sv"] + (tt - 2) * 2 + 1)
                        ins = scalar.activation(out=p1_sq[:, (tt % 2) * D:(tt % 2 + 1) * D],
                                                in_=p1_x[:, (tt % 2) * D:(tt % 2 + 1) * D],
                                                func=AF.Square)
                        inc(ins, "sa", 1)
                        scalar.wait_ge(sems["sv"], b0["sv"] + tt * 2 + 1)
                        scalar.activation(out=p1_st[:, o8 + 5:o8 + 6],
                                          in_=p1_st[:, o8 + 3:o8 + 4],
                                          func=AF.Ln, scale=cns[:, 3:4], bias=cns[:, 0:1])
                        scalar.activation(out=p1_st[:, o8 + 6:o8 + 7],
                                          in_=p1_st[:, o8 + 5:o8 + 6],
                                          func=AF.Exp, scale=cns[:, 2:3])
                        ins = scalar.copy(out=jnk[:], in_=p1_sq[:, (tt % 2) * D:(tt % 2) * D + 512])
                        inc(ins, "sa", 1)

                @block.vector
                def _(vector):
                    wait_barrier(vector, "p0", ("qG0", "qG1", "qG2", "qG3", "qG4"))
                    for tt in range(NTT):
                        o8 = (tt % 2) * 8
                        xs_ = p1_x[:, (tt % 2) * D:(tt % 2 + 1) * D]
                        xn_ = p1_xn[:, (tt % 2) * D:(tt % 2 + 1) * D]
                        xnb_ = p1_xnb[:, (tt % 2) * D:(tt % 2 + 1) * D]
                        sl = lambda c: p1_st[:, o8 + c:o8 + c + 1]
                        wtag(vector, ("p1x", tt))
                        vector.tensor_reduce(out=sl(0), in_=xs_, axis=AX, op=ALU.add)
                        vector.wait_ge(sems["sa"], b0["sa"] + tt * 2 + 1)
                        vector.tensor_reduce(out=sl(1), in_=p1_sq[:, (tt % 2) * D:(tt % 2 + 1) * D],
                                             axis=AX, op=ALU.add)
                        vector.tensor_scalar(out=sl(4), in0=sl(0), scalar1=-1.0 / D,
                                             scalar2=None, op0=ALU.mult)
                        vector.tensor_copy(out=jnk[:], in_=xs_[:, 0:512])
                        vector.scalar_tensor_tensor(out=sl(3), in0=sl(0), scalar=sl(4),
                                                    in1=sl(1), op0=ALU.mult, op1=ALU.add)
                        ins = vector.tensor_copy(out=jnk[:], in_=xs_[:, 0:512])
                        inc(ins, "sv", 1)
                        vector.wait_ge(sems["sa"], b0["sa"] + tt * 2 + 2)
                        if tt >= 2:
                            wtag(vector, ("p1s", tt - 2))
                        vector.tensor_scalar(out=xn_, in0=xs_, scalar1=sl(4), scalar2=sl(6),
                                             op0=ALU.add, op1=ALU.mult)
                        ins = vector.tensor_copy(out=xnb_, in_=xn_)
                        inc(ins, "sv", 1)

            _ps.close()
        barrier("p1")

        if 2 in phases:
            NGRP = 35

            def gspec(g):
                if g < 32:
                    return g * 128, 128
                return {32: (4096, 16), 33: (4112, 16), 34: (4128, 64)}[g]

            _ps = ExitStack()
            p2_w = _ps.enter_context(nc.sbuf_tensor("p2_w", [128, NK * OUTF], BF16))
            p2_xt = _ps.enter_context(nc.sbuf_tensor("p2_xt", [128, 2 * NK * WIN], BF16))
            p2_ev = _ps.enter_context(nc.sbuf_tensor("p2_ev", [128, 2 * WIN], F32))
            p2_evb = _ps.enter_context(nc.sbuf_tensor("p2_evb", [128, 2 * WIN], BF16))
            if True:
                b0 = dict(S)

                @block.sync
                def _(sync):
                    wait_barrier(sync, "p1")
                    for k in range(NK):
                        dma(sync, "qH", p2_w[:, k * OUTF:(k + 1) * OUTF],
                            h_WinTb[k * 128:(k + 1) * 128, :], tag="p2w")
                    for w in range(NW):
                        if w >= 2:
                            sync.wait_ge(sems["sp"], b0["sp"] + (w - 1) * NGRP)
                        off = (w % 2) * NK * WIN
                        for k in range(NK):
                            dmat(sync, "qT0" if w % 2 == 0 else "qT1",
                                 p2_xt[:, off + k * WIN:off + (k + 1) * WIN],
                                 h_xnb[w * WIN:(w + 1) * WIN, k * 128:(k + 1) * 128],
                                 tag=("p2xt", w))

                @block.gpsimd
                def _(gpsimd):
                    for w in range(NW):
                        tsl = slice(w * WIN, (w + 1) * WIN)
                        for g in range(NGRP):
                            G = w * NGRP + g
                            gpsimd.wait_ge(sems["sa"], b0["sa"] + G + 1)
                            eo = (G % 2) * WIN
                            qq = "qG0" if G % 2 == 0 else "qG1"
                            tg = ("p2s", G)
                            if g < 16:
                                dma(gpsimd, qq, h_xs[g * 128:(g + 1) * 128, tsl],
                                    p2_ev[:, eo:eo + WIN], tag=tg)
                            elif g < 32:
                                dma(gpsimd, qq, h_sz[(g - 16) * 128:(g - 15) * 128, tsl],
                                    p2_evb[:, eo:eo + WIN], tag=tg)
                            elif g == 32:
                                dma(gpsimd, qq, h_BT[:, tsl], p2_evb[0:16, eo:eo + WIN], tag=tg)
                            elif g == 33:
                                dma(gpsimd, qq, h_CT[:, tsl], p2_evb[0:16, eo:eo + WIN], tag=tg)
                            else:
                                dma(gpsimd, qq, h_dtpT[:, tsl], p2_evb[0:64, eo:eo + WIN], tag=tg)
                @block.tensor
                def _(tensor):
                    wait_barrier(tensor, "p1", ("qG0", "qG1"))
                    wtag(tensor, "p2w")
                    for w in range(NW):
                        wtag(tensor, ("p2xt", w))
                        off = (w % 2) * NK * WIN
                        for g in range(NGRP):
                            G = w * NGRP + g
                            if G >= 4:
                                tensor.wait_ge(sems["sa"], b0["sa"] + G - 3)
                            c0, M = gspec(g)
                            ps = psl(G % 4)
                            for k in range(NK):
                                ins = tensor.matmul(
                                    ps[0:M, :],
                                    p2_w[:, k * OUTF + c0:k * OUTF + c0 + M],
                                    p2_xt[:, off + k * WIN:off + (k + 1) * WIN],
                                    start=(k == 0), stop=(k == NK - 1))
                            inc(ins, "sp", 1)

                @block.scalar
                def _(scalar):
                    for w in range(NW):
                        for g in range(NGRP):
                            G = w * NGRP + g
                            scalar.wait_ge(sems["sp"], b0["sp"] + G + 1)
                            if G >= 2:
                                wtag(scalar, ("p2s", G - 2))
                            ps = psl(G % 4)
                            eo = (G % 2) * WIN
                            if g < 16:
                                ins = scalar.copy(out=p2_ev[:, eo:eo + WIN], in_=ps)
                            elif g < 32:
                                ins = scalar.activation(out=p2_evb[:, eo:eo + WIN],
                                                        in_=ps, func=AF.Silu)
                            elif g in (32, 33):
                                ins = scalar.copy(out=p2_evb[0:16, eo:eo + WIN], in_=ps[0:16, :])
                            else:
                                ins = scalar.copy(out=p2_evb[0:64, eo:eo + WIN], in_=ps[0:64, :])
                            inc(ins, "sa", 1)

            _ps.close()
        barrier("p2")

        if 3 in phases:
            _ps = ExitStack()
            p3_xs = _ps.enter_context(nc.sbuf_tensor("p3_xs", [128, 4 * 520], F32))
            p3_acc = _ps.enter_context(nc.sbuf_tensor("p3_acc", [128, 4 * WIN], F32))
            p3_u = _ps.enter_context(nc.sbuf_tensor("p3_u", [128, 4 * WIN], BF16))
            if True:
                b0 = dict(S)
                iters = [(w, j) for w in range(NW) for j in range(NJ)]

                @block.sync
                def _(sync):
                    wait_barrier(sync, "p2")
                    for i, (w, j) in enumerate(iters):
                        if i >= 4:
                            sync.wait_ge(sems["sv"], b0["sv"] + (i - 3))
                        off = (i % 4) * 520
                        qq = "qP%d" % (i % 4)
                        r = slice(j * 128, (j + 1) * 128)
                        if w == 0:
                            dma(sync, qq, p3_xs[:, off + 3:off + 3 + WIN], h_xs[r, 0:WIN],
                                tag=("p3x", i))
                        else:
                            dma(sync, qq, p3_xs[:, off:off + 3],
                                h_xs[r, w * WIN - 3:w * WIN])
                            dma(sync, qq, p3_xs[:, off + 3:off + 3 + WIN],
                                h_xs[r, w * WIN:(w + 1) * WIN], tag=("p3x", i))

                @block.gpsimd
                def _(gpsimd):
                    for i, (w, j) in enumerate(iters):
                        uo = (i % 4) * WIN
                        gpsimd.wait_ge(sems["sa"], b0["sa"] + i + 1)
                        dma(gpsimd, "qG%d" % (i % 4), h_u[j * 128:(j + 1) * 128, w * WIN:(w + 1) * WIN],
                            p3_u[:, uo:uo + WIN], tag=("p3s", i))
                @block.vector
                def _(vector):
                    wait_barrier(vector, "p2", ("qG0", "qG1", "qG2", "qG3", "qG4"))
                    for i, (w, j) in enumerate(iters):
                        off = (i % 4) * 520
                        ao = (i % 4) * WIN
                        wtag(vector, ("p3x", i))
                        if w == 0:
                            vector.memset(p3_xs[:, off:off + 3], 0.0)
                        if i >= 4:
                            vector.wait_ge(sems["sa"], b0["sa"] + i - 3)
                        vector.tensor_scalar(out=p3_acc[:, ao:ao + WIN],
                                             in0=p3_xs[:, off + 3:off + 3 + WIN],
                                             scalar1=convw[:, j * 4 + 3:j * 4 + 4],
                                             scalar2=None, op0=ALU.mult)
                        for k in (2, 1, 0):
                            ins = vector.scalar_tensor_tensor(
                                out=p3_acc[:, ao:ao + WIN],
                                in0=p3_xs[:, off + k:off + k + WIN],
                                scalar=convw[:, j * 4 + k:j * 4 + k + 1],
                                in1=p3_acc[:, ao:ao + WIN],
                                op0=ALU.mult, op1=ALU.add)
                        inc(ins, "sv", 1)

                @block.scalar
                def _(scalar):
                    for i, (w, j) in enumerate(iters):
                        ao = (i % 4) * WIN
                        uo = (i % 4) * WIN
                        scalar.wait_ge(sems["sv"], b0["sv"] + i + 1)
                        if i >= 4:
                            wtag(scalar, ("p3s", i - 4))
                        ins = scalar.activation(out=p3_u[:, uo:uo + WIN],
                                                in_=p3_acc[:, ao:ao + WIN],
                                                func=AF.Silu, bias=convb[:, j:j + 1])
                        inc(ins, "sa", 1)

            _ps.close()
        barrier("p3")

        if 4 in phases:
            _ps = ExitStack()
            p4_wdt = _ps.enter_context(nc.sbuf_tensor("p4_wdt", [128, DI], BF16))
            p4_dtp = _ps.enter_context(nc.sbuf_tensor("p4_dtp", [128, 2 * WIN], BF16))
            p4_sig = _ps.enter_context(nc.sbuf_tensor("p4_sig", [128, NJ * WIN], F32))
            p4_dtn = _ps.enter_context(nc.sbuf_tensor("p4_dtn", [128, 4 * WIN], BF16))
            p4_u = _ps.enter_context(nc.sbuf_tensor("p4_u", [128, 4 * WIN], BF16))
            p4_dtu = _ps.enter_context(nc.sbuf_tensor("p4_dtu", [128, 4 * WIN], BF16))
            if True:
                b0 = dict(S)

                def sig_sa(G):
                    w_, j_ = divmod(G, NJ)
                    return b0["sa"] + w_ * 32 + j_ + 1

                @block.sync
                def _(sync):
                    wait_barrier(sync, "p3")
                    dma(sync, "qH", p4_wdt[0:RK, :], h_WdtTb, tag="p4w")
                    for w in range(NW):
                        if w >= 2:
                            sync.wait_ge(sems["sp"], b0["sp"] + (w - 1) * NJ)
                        dma(sync, "qP4" if w % 2 == 0 else "qP5",
                            p4_dtp[0:RK, (w % 2) * WIN:(w % 2 + 1) * WIN],
                            h_dtpT[:, w * WIN:(w + 1) * WIN], tag=("p4d", w))
                        for j in range(NJ):
                            it = w * NJ + j
                            if it >= 4:
                                sync.wait_ge(sems["sv"], b0["sv"] + it - 3)
                            dma(sync, "qP%d" % (it % 4),
                                p4_u[:, (it % 4) * WIN:(it % 4 + 1) * WIN],
                                h_u[j * 128:(j + 1) * 128, w * WIN:(w + 1) * WIN],
                                tag=("p4u", it))

                @block.gpsimd
                def _(gpsimd):
                    for w in range(NW):
                        for j in range(NJ):
                            G = w * NJ + j
                            gpsimd.wait_ge(sems["sv"], b0["sv"] + G + 1)
                            qq = "qG%d" % (G % 4)
                            dma(gpsimd, qq, h_dtneg[j * 128:(j + 1) * 128, w * WIN:(w + 1) * WIN],
                                p4_dtn[:, (G % 4) * WIN:(G % 4 + 1) * WIN])
                            dma(gpsimd, qq, h_dtu[j * 128:(j + 1) * 128, w * WIN:(w + 1) * WIN],
                                p4_dtu[:, (G % 4) * WIN:(G % 4 + 1) * WIN], tag=("p4s", G))
                @block.tensor
                def _(tensor):
                    wait_barrier(tensor, "p3", ())
                    wtag(tensor, "p4w")
                    for w in range(NW):
                        wtag(tensor, ("p4d", w))
                        for j in range(NJ):
                            G = w * NJ + j
                            if G >= 4:
                                tensor.wait_ge(sems["sa"], sig_sa(G - 4))
                            ins = tensor.matmul(psl(G % 4),
                                                p4_wdt[0:RK, j * 128:(j + 1) * 128],
                                                p4_dtp[0:RK, (w % 2) * WIN:(w % 2 + 1) * WIN],
                                                start=True, stop=True)
                            inc(ins, "sp", 1)

                @block.scalar
                def _(scalar):
                    for w in range(NW):
                        for j in range(NJ):
                            G = w * NJ + j
                            scalar.wait_ge(sems["sp"], b0["sp"] + G + 1)
                            ins = scalar.activation(out=p4_sig[:, j * WIN:(j + 1) * WIN],
                                                    in_=psl(G % 4), func=AF.Sigmoid,
                                                    scale=-1.0, bias=negb[:, j:j + 1])
                            inc(ins, "sa", 1)
                        for j in range(NJ):
                            G = w * NJ + j
                            if G >= 4:
                                wtag(scalar, ("p4s", G - 4))
                            ins = scalar.activation(out=p4_dtn[:, (G % 4) * WIN:(G % 4 + 1) * WIN],
                                                    in_=p4_sig[:, j * WIN:(j + 1) * WIN],
                                                    func=AF.Ln, bias=cns[:, 1:2])
                            inc(ins, "sa", 1)

                @block.vector
                def _(vector):
                    wait_barrier(vector, "p3", ())
                    for w in range(NW):
                        for j in range(NJ):
                            G = w * NJ + j
                            vector.wait_ge(sems["sa"], b0["sa"] + w * 32 + 16 + j + 1)
                            wtag(vector, ("p4u", G))
                            ins = vector.tensor_tensor(
                                out=p4_dtu[:, (G % 4) * WIN:(G % 4 + 1) * WIN],
                                in0=p4_dtn[:, (G % 4) * WIN:(G % 4 + 1) * WIN],
                                in1=p4_u[:, (G % 4) * WIN:(G % 4 + 1) * WIN],
                                op=ALU.mult)
                            inc(ins, "sv", 1)

            _ps.close()
        barrier("p4")

        if 5 in phases:
            SEL = NS * 128
            _ps = ExitStack()
            p5_sel = _ps.enter_context(nc.sbuf_tensor("p5_sel", [128, 2 * SEL], F32))
            p5_selb = _ps.enter_context(nc.sbuf_tensor("p5_selb", [128, 2 * SEL], BF16))
            p5_bt = _ps.enter_context(nc.sbuf_tensor("p5_bt", [128, 2 * WIN], BF16))
            p5_bc = _ps.enter_context(nc.sbuf_tensor("p5_bc", [128, 4 * WIN], BF16))
            if True:
                b0 = dict(S)

                @block.sync
                def _(sync):
                    wait_barrier(sync, "p4")
                    dma(sync, "qH", p5_sel[0:NS, 0:SEL], selNC)
                    dma(sync, "qH", p5_sel[0:NS, SEL:2 * SEL], selPC, tag="p5sel")
                    for w in range(NW):
                        if w >= 1:
                            sync.wait_ge(sems["sp"], b0["sp"] + w * 32)
                        qq = "qP4" if w % 2 == 0 else "qP5"
                        dma(sync, qq, p5_bt[0:NS, 0:WIN], h_BT[:, w * WIN:(w + 1) * WIN])
                        dma(sync, qq, p5_bt[0:NS, WIN:2 * WIN], h_CT[:, w * WIN:(w + 1) * WIN],
                            tag=("p5b", w))

                @block.gpsimd
                def _(gpsimd):
                    for w in range(NW):
                        for g in range(32):
                            G = w * 32 + g
                            n = g % 16
                            gpsimd.wait_ge(sems["sa"], b0["sa"] + G + 1)
                            dst = h_Bbc if g < 16 else h_Cbc
                            dma(gpsimd, "qG%d" % (G % 4), dst[n * 128:(n + 1) * 128, w * WIN:(w + 1) * WIN],
                                p5_bc[:, (G % 4) * WIN:(G % 4 + 1) * WIN], tag=("p5s", G))
                @block.vector
                def _(vector):
                    wait_barrier(vector, "p4", ())
                    wtag(vector, "p5sel")
                    ins = vector.tensor_copy(out=p5_selb[0:NS, :], in_=p5_sel[0:NS, :])
                    inc(ins, "sv", 1)

                @block.tensor
                def _(tensor):
                    tensor.wait_ge(sems["sv"], b0["sv"] + 1)
                    for w in range(NW):
                        wtag(tensor, ("p5b", w))
                        for g in range(32):
                            G = w * 32 + g
                            n = g % 16
                            isB = g < 16
                            if G >= 4:
                                tensor.wait_ge(sems["sa"], b0["sa"] + G - 3)
                            so_ = 0 if isB else SEL
                            bo = 0 if isB else WIN
                            ins = tensor.matmul(psl(G % 4),
                                                p5_selb[0:NS, so_ + n * 128:so_ + (n + 1) * 128],
                                                p5_bt[0:NS, bo:bo + WIN],
                                                start=True, stop=True)
                            inc(ins, "sp", 1)

                @block.scalar
                def _(scalar):
                    for w in range(NW):
                        for g in range(32):
                            G = w * 32 + g
                            scalar.wait_ge(sems["sp"], b0["sp"] + G + 1)
                            if G >= 4:
                                wtag(scalar, ("p5s", G - 4))
                            ins = scalar.copy(out=p5_bc[:, (G % 4) * WIN:(G % 4 + 1) * WIN],
                                              in_=psl(G % 4))
                            inc(ins, "sa", 1)

            _ps.close()
        barrier("p5")

        if 6 in phases:
            _ps = ExitStack()
            p6_dtn = _ps.enter_context(nc.sbuf_tensor("p6_dtn", [128, 2 * L], BF16))
            p6_dA = _ps.enter_context(nc.sbuf_tensor("p6_dA", [128, 2 * L], F32))
            p6_dtu = _ps.enter_context(nc.sbuf_tensor("p6_dtu", [128, 2 * L], BF16))
            p6_u = _ps.enter_context(nc.sbuf_tensor("p6_u", [128, 2 * L], BF16))
            p6_sz = _ps.enter_context(nc.sbuf_tensor("p6_sz", [128, 2 * L], BF16))
            p6_bcB = _ps.enter_context(nc.sbuf_tensor("p6_bcB", [128, 2 * L], BF16))
            p6_bcC = _ps.enter_context(nc.sbuf_tensor("p6_bcC", [128, 2 * L], BF16))
            p6_X = _ps.enter_context(nc.sbuf_tensor("p6_X", [128, L], BF16))
            p6_h = _ps.enter_context(nc.sbuf_tensor("p6_h", [128, L], BF16))
            p6_hC = _ps.enter_context(nc.sbuf_tensor("p6_hC", [128, 2 * L], BF16))
            p6_yg = _ps.enter_context(nc.sbuf_tensor("p6_yg", [128, L], BF16))
            if True:
                b0 = dict(S)
                PJ = {"sa": 16, "sv": 51, "sp": 16}

                def jb(j, name):
                    return b0[name] + j * PJ[name]

                @block.sync
                def _(sync):
                    wait_barrier(sync, "p5")
                    for j in range(NJ):
                        r = slice(j * 128, (j + 1) * 128)
                        jo = (j % 2) * L
                        if j >= 2:
                            sync.wait_ge(sems["sv"], jb(j - 1, "sv"))
                            sync.wait_ge(sems["sa"], jb(j - 1, "sa"))
                        dma(sync, "qH", p6_dtn[:, jo:jo + L], h_dtneg[r, :])
                        dma(sync, "qH", p6_dtu[:, jo:jo + L], h_dtu[r, :])
                        dma(sync, "qH", p6_u[:, jo:jo + L], h_u[r, :])
                        dma(sync, "qH", p6_sz[:, jo:jo + L], h_sz[r, :], tag=("p6j", j))
                        for n in range(NS):
                            if n >= 2:
                                sync.wait_ge(sems["sv"], jb(j, "sv") + 3 * n - 3)
                            elif j >= 1:
                                sync.wait_ge(sems["sv"], jb(j - 1, "sv") + 3 * (14 + n) + 3)
                            nb = (n % 2) * L
                            qq = "qP0" if n % 2 == 0 else "qP1"
                            dma(sync, qq, p6_bcB[:, nb:nb + L], h_Bbc[n * 128:(n + 1) * 128, :])
                            dma(sync, qq, p6_bcC[:, nb:nb + L], h_Cbc[n * 128:(n + 1) * 128, :],
                                tag=("p6bc", j, n))

                @block.gpsimd
                def _(gpsimd):
                    wait_barrier(gpsimd, "p5", ())
                    for j in range(NJ):
                        gpsimd.wait_ge(sems["sv"], jb(j, "sv") + PJ["sv"])
                        dma(gpsimd, "qG4", h_yg[j * 128:(j + 1) * 128, :], p6_yg[:],
                            tag=("p6s", j))

                @block.scalar
                def _(scalar):
                    wait_barrier(scalar, "p5", ())
                    for j in range(NJ):
                        jo = (j % 2) * L
                        wtag(scalar, ("p6j", j))
                        for n in range(NS):
                            if n >= 2:
                                scalar.wait_ge(sems["sv"], jb(j, "sv") + 3 * n - 4)
                            elif j >= 1:
                                if n == 0:
                                    scalar.wait_ge(sems["sv"], jb(j, "sv"))
                                else:
                                    scalar.wait_ge(sems["sv"], jb(j - 1, "sv") + 47)
                            ins = scalar.activation(out=p6_dA[:, (n % 2) * L:(n % 2 + 1) * L],
                                                    in_=p6_dtn[:, jo:jo + L], func=AF.Exp,
                                                    scale=A_pos[:, j * NS + n:j * NS + n + 1])
                            inc(ins, "sa", 1)

                @block.vector
                def _(vector):
                    wait_barrier(vector, "p5", ())
                    for j in range(NJ):
                        jo = (j % 2) * L
                        for n in range(NS):
                            nb = (n % 2) * L
                            if n == 0:
                                wtag(vector, ("p6j", j))
                            wtag(vector, ("p6bc", j, n))
                            ins = vector.tensor_tensor(out=p6_X[:],
                                                       in0=p6_dtu[:, jo:jo + L],
                                                       in1=p6_bcB[:, nb:nb + L],
                                                       op=ALU.mult)
                            inc(ins, "sv", 1)
                            vector.wait_ge(sems["sa"], jb(j, "sa") + n + 1)
                            ins = vector.tensor_tensor_scan(out=p6_h[:],
                                                            data0=p6_dA[:, nb:nb + L],
                                                            data1=p6_X[:],
                                                            initial=0.0,
                                                            op0=ALU.mult, op1=ALU.add)
                            inc(ins, "sv", 1)
                            if n >= 2:
                                vector.wait_ge(sems["sp"], jb(j, "sp") + n - 1)
                            elif j >= 1:
                                vector.wait_ge(sems["sp"], jb(j - 1, "sp") + 15 + n)
                            ins = vector.tensor_tensor(out=p6_hC[:, nb:nb + L],
                                                       in0=p6_h[:],
                                                       in1=p6_bcC[:, nb:nb + L],
                                                       op=ALU.mult)
                            inc(ins, "sv", 1)
                        vector.wait_ge(sems["sp"], jb(j, "sp") + 16)
                        ins = vector.scalar_tensor_tensor(out=p6_dA[:, 0:2048],
                                                          in0=p6_u[:, jo:jo + 2048],
                                                          scalar=D_col[:, j:j + 1],
                                                          in1=pY1[:],
                                                          op0=ALU.mult, op1=ALU.add)
                        inc(ins, "sv", 1)
                        ins = vector.scalar_tensor_tensor(out=p6_dA[:, 2048:4096],
                                                          in0=p6_u[:, jo + 2048:jo + 4096],
                                                          scalar=D_col[:, j:j + 1],
                                                          in1=pY2[:],
                                                          op0=ALU.mult, op1=ALU.add)
                        inc(ins, "sv", 1)
                        if j >= 1:
                            wtag(vector, ("p6s", j - 1))
                        ins = vector.tensor_tensor(out=p6_yg[:], in0=p6_dA[:, 0:4096],
                                                   in1=p6_sz[:, jo:jo + L], op=ALU.mult)
                        inc(ins, "sv", 1)

                @block.tensor
                def _(tensor):
                    wait_barrier(tensor, "p5", ())
                    for j in range(NJ):
                        for n in range(NS):
                            nb = (n % 2) * L
                            tensor.wait_ge(sems["sv"], jb(j, "sv") + 3 * n + 3)
                            if n == 0 and j >= 1:
                                tensor.wait_ge(sems["sv"], jb(j, "sv") - 1)
                            for s in range(8):
                                ins = tensor.matmul(psl(s), identb[:],
                                                    p6_hC[:, nb + s * 512:nb + (s + 1) * 512],
                                                    start=(n == 0), stop=(n == NS - 1))
                            inc(ins, "sp", 1)
            _ps.close()
        barrier("p6")

        if 7 in phases:
            _ps = ExitStack()
            p7_wout = _ps.enter_context(nc.sbuf_tensor("p7_wout", [128, NJ * D], BF16))
            p7_cbt = _ps.enter_context(nc.sbuf_tensor("p7_cbt", [128, NK * D], BF16))
            p7_yg = _ps.enter_context(nc.sbuf_tensor("p7_yg", [128, 2 * NJ * 128], BF16))
            p7_xt = _ps.enter_context(nc.sbuf_tensor("p7_xt", [128, 2 * NK * 128], BF16))
            p7_xn = _ps.enter_context(nc.sbuf_tensor("p7_xn", [128, 2 * D], F32))
            p7_x = _ps.enter_context(nc.sbuf_tensor("p7_x", [128, 2 * D], F32))
            p7_fa = _ps.enter_context(nc.sbuf_tensor("p7_fa", [128, D], F32))
            p7_sq = _ps.enter_context(nc.sbuf_tensor("p7_sq", [128, D], F32))
            p7_fb = _ps.enter_context(nc.sbuf_tensor("p7_fb", [128, D], F32))
            p7_out = _ps.enter_context(nc.sbuf_tensor("p7_out", [128, 2 * D], F32))
            p7_st = _ps.enter_context(nc.sbuf_tensor("p7_st", [128, 16], F32))
            p7_row = _ps.enter_context(nc.sbuf_tensor("p7_row", [128, 4 * D], F32))
            p7_gbc = _ps.enter_context(nc.sbuf_tensor("p7_gbc", [128, D], F32))
            p7_sbc = _ps.enter_context(nc.sbuf_tensor("p7_sbc", [128, D], F32))
            p7_bbc = _ps.enter_context(nc.sbuf_tensor("p7_bbc", [128, D], F32))
            p7_tbc = _ps.enter_context(nc.sbuf_tensor("p7_tbc", [128, D], F32))
            if True:
                b0 = dict(S)

                @block.sync
                def _(sync):
                    wait_barrier(sync, "p6")
                    dma(sync, "qH", p7_row[0:1, :], lnrow, tag="p7row")
                    for j in range(NJ):
                        dma(sync, "qH", p7_wout[:, j * D:(j + 1) * D],
                            h_WoutTb[j * 128:(j + 1) * 128, :])
                    for k in range(NK):
                        dma(sync, "qH", p7_cbt[:, k * D:(k + 1) * D],
                            h_CBTb[k * 128:(k + 1) * 128, :], tag="p7h")
                    for tt in range(NTT):
                        if tt >= 2:
                            sync.wait_ge(sems["sv"], b0["sv"] + 1 + (tt - 1) * 9)
                        off = tt % 2
                        qq = "qP0" if tt % 2 == 0 else "qP1"
                        r = slice(tt * 128, (tt + 1) * 128)
                        for j in range(NJ):
                            dma(sync, qq,
                                p7_yg[:, off * NJ * 128 + j * 128:off * NJ * 128 + (j + 1) * 128],
                                h_yg[j * 128:(j + 1) * 128, r])
                        for k in range(NK):
                            dmat(sync, "qT0" if tt % 2 == 0 else "qT1",
                                 p7_xt[:, off * NK * 128 + k * 128:off * NK * 128 + (k + 1) * 128],
                                 h_xnb[r, k * 128:(k + 1) * 128], tag=("p7xt", tt))
                        dma(sync, qq, p7_xn[:, off * D:(off + 1) * D], h_xn[r, :])
                        dma(sync, qq, p7_x[:, off * D:(off + 1) * D], x[r, :], tag=("p7l", tt))

                @block.tensor
                def _(tensor):
                    wait_barrier(tensor, "p6")
                    wtag(tensor, "p7row")
                    for i in range(4):
                        for half in range(2):
                            ins = tensor.matmul(psl(i * 2 + half), onesf[0:1, :],
                                                p7_row[0:1, i * D + half * 512:i * D + half * 512 + 512],
                                                start=True, stop=True)
                        inc(ins, "sp", 1)

                @block.scalar
                def _(scalar):
                    wait_barrier(scalar, "p6", ())
                    scalar.wait_ge(sems["sp"], b0["sp"] + 4)
                    for half in range(2):
                        scalar.copy(out=p7_gbc[:, half * 512:(half + 1) * 512], in_=psl(0 + half))
                        scalar.copy(out=p7_bbc[:, half * 512:(half + 1) * 512], in_=psl(2 + half))
                        scalar.copy(out=p7_sbc[:, half * 512:(half + 1) * 512], in_=psl(4 + half))
                        ins = scalar.copy(out=p7_tbc[:, half * 512:(half + 1) * 512], in_=psl(6 + half))
                    inc(ins, "sa", 1)

                @block.vector
                def _(vector):
                    wait_barrier(vector, "p6", ())
                    vector.wait_ge(sems["sa"], b0["sa"] + 1)
                    vector.tensor_scalar(out=p7_sbc[:], in0=p7_sbc[:],
                                         scalar1=1.0 / 127.0, scalar2=None, op0=ALU.mult)
                    ins = vector.tensor_tensor(out=p7_bbc[:], in0=p7_bbc[:],
                                               in1=p7_tbc[:], op=ALU.add)
                    inc(ins, "sv", 1)

                bp = dict(S)

                @block.gpsimd
                def _(gpsimd):
                    for tt in range(NTT):
                        off = tt % 2
                        gpsimd.wait_ge(sems["sv"], bp["sv"] + (tt + 1) * 9)
                        dma(gpsimd, "qG0" if tt % 2 == 0 else "qG1",
                            out[tt * 128:(tt + 1) * 128, :],
                            p7_out[:, off * D:(off + 1) * D], tag=("p7o", tt))

                @block.tensor
                def _(tensor):
                    tensor.wait_ge(sems["sv"], bp["sv"])
                    for tt in range(NTT):
                        off = tt % 2
                        wtag(tensor, ("p7l", tt))
                        wtag(tensor, ("p7xt", tt))
                        if tt >= 2:
                            tensor.wait_ge(sems["sv"], bp["sv"] + (tt - 1) * 9)
                        for half in range(2):
                            for j in range(NJ):
                                ins = tensor.matmul(
                                    psl(half),
                                    p7_yg[:, off * NJ * 128 + j * 128:off * NJ * 128 + (j + 1) * 128],
                                    p7_wout[:, j * D + half * 512:j * D + half * 512 + 512],
                                    start=(j == 0), stop=(j == NJ - 1))
                            inc(ins, "sp", 1)
                        for half in range(2):
                            for k in range(NK):
                                ins = tensor.matmul(
                                    psl(4 + half),
                                    p7_xt[:, off * NK * 128 + k * 128:off * NK * 128 + (k + 1) * 128],
                                    p7_cbt[:, k * D + half * 512:k * D + half * 512 + 512],
                                    start=(k == 0), stop=(k == NK - 1))
                            inc(ins, "sp", 1)

                @block.scalar
                def _(scalar):
                    for tt in range(NTT):
                        scalar.wait_ge(sems["sv"], bp["sv"] + tt * 9 + 1)
                        ins = scalar.activation(out=p7_sq[:], in_=p7_fa[:], func=AF.Square)
                        inc(ins, "sa", 1)
                        scalar.wait_ge(sems["sv"], bp["sv"] + tt * 9 + 2)
                        scalar.activation(out=p7_st[:, 5:6], in_=p7_st[:, 3:4],
                                          func=AF.Ln, scale=cns[:, 3:4], bias=cns[:, 0:1])
                        scalar.activation(out=p7_st[:, 6:7], in_=p7_st[:, 5:6],
                                          func=AF.Exp, scale=cns[:, 2:3])
                        ins = scalar.copy(out=jnk[:], in_=p7_sq[:, 0:512])
                        inc(ins, "sa", 1)

                @block.vector
                def _(vector):
                    for tt in range(NTT):
                        off = tt % 2
                        sl = lambda c: p7_st[:, c:c + 1]
                        vector.wait_ge(sems["sp"], bp["sp"] + tt * 4 + 2)
                        wtag(vector, ("p7l", tt))
                        vector.tensor_tensor(out=p7_fa[:, 0:512], in0=pY1[:, 0:512],
                                             in1=p7_xn[:, off * D:off * D + 512], op=ALU.add)
                        ins = vector.tensor_tensor(out=p7_fa[:, 512:1024], in0=pY1[:, 512:1024],
                                                   in1=p7_xn[:, off * D + 512:off * D + 1024],
                                                   op=ALU.add)
                        inc(ins, "sv", 1)
                        vector.tensor_reduce(out=sl(0), in_=p7_fa[:], axis=AX, op=ALU.add)
                        vector.wait_ge(sems["sa"], bp["sa"] + tt * 2 + 1)
                        vector.tensor_reduce(out=sl(1), in_=p7_sq[:], axis=AX, op=ALU.add)
                        vector.tensor_scalar(out=sl(4), in0=sl(0), scalar1=-1.0 / D,
                                             scalar2=None, op0=ALU.mult)
                        vector.tensor_copy(out=jnk[:], in_=p7_fa[:, 0:512])
                        vector.scalar_tensor_tensor(out=sl(3), in0=sl(0), scalar=sl(4),
                                                    in1=sl(1), op0=ALU.mult, op1=ALU.add)
                        ins = vector.tensor_copy(out=jnk[:], in_=p7_fa[:, 0:512])
                        inc(ins, "sv", 1)
                        vector.wait_ge(sems["sa"], bp["sa"] + tt * 2 + 2)
                        ins = vector.tensor_scalar(out=p7_fb[:], in0=p7_fa[:], scalar1=sl(4),
                                                   scalar2=sl(6), op0=ALU.add, op1=ALU.mult)
                        inc(ins, "sv", 1)
                        ins = vector.tensor_tensor(out=p7_fb[:], in0=p7_fb[:], in1=p7_gbc[:],
                                                   op=ALU.mult)
                        inc(ins, "sv", 1)
                        vector.wait_ge(sems["sp"], bp["sp"] + tt * 4 + 4)
                        vector.tensor_tensor(out=p7_fa[:, 0:512], in0=pY2[:, 0:512],
                                             in1=p7_sbc[:, 0:512], op=ALU.mult)
                        ins = vector.tensor_tensor(out=p7_fa[:, 512:1024], in0=pY2[:, 512:1024],
                                                   in1=p7_sbc[:, 512:1024], op=ALU.mult)
                        inc(ins, "sv", 1)
                        ins = vector.tensor_tensor(out=p7_fb[:], in0=p7_fb[:], in1=p7_fa[:],
                                                   op=ALU.add)
                        inc(ins, "sv", 1)
                        ins = vector.tensor_tensor(out=p7_fb[:], in0=p7_fb[:], in1=p7_bbc[:],
                                                   op=ALU.add)
                        inc(ins, "sv", 1)
                        if tt >= 2:
                            wtag(vector, ("p7o", tt - 2))
                        ins = vector.tensor_tensor(out=p7_out[:, off * D:(off + 1) * D],
                                                   in0=p7_fb[:],
                                                   in1=p7_x[:, off * D:(off + 1) * D],
                                                   op=ALU.add)
                        inc(ins, "sv", 2)

            _ps.close()
        barrier("p7")

        @block.gpsimd
        def _(gpsimd):
            for n in DMA_SEMS:
                if S[n] > 0:
                    gpsimd.wait_ge(sems[n], S[n])

    import os
    if os.environ.get("MK_PRINT_S"):
        print("FINAL S:", S)
        print("n instructions:", sum(1 for _ in nc.all_instructions()))
    return nc


def _prep(inputs):
    return {
        "WinT": np.ascontiguousarray(inputs["L0_attn_W_in"].T),
        "WdtT": np.ascontiguousarray(inputs["L0_attn_W_dt"].T),
        "WoutT": np.ascontiguousarray(inputs["L0_attn_W_out"].T),
        "CBT": np.ascontiguousarray(inputs["L0_ffn_CB"].T),
        "A_logr": np.ascontiguousarray(inputs["L0_attn_A_log"]),
        "bdt_c": np.ascontiguousarray(inputs["L0_attn_b_dt"].reshape(NJ, 128).T),
        "convw_c": np.ascontiguousarray(
            inputs["L0_attn_conv_w"][:, 0, :].reshape(NJ, 128, 4).transpose(1, 0, 2).reshape(128, NJ * 4)),
        "convb_c": np.ascontiguousarray(inputs["L0_attn_conv_b"].reshape(NJ, 128).T),
        "D_c": np.ascontiguousarray(inputs["L0_attn_D"].reshape(NJ, 128).T),
        "selPC": np.kron(np.eye(NS, dtype=np.float32), np.ones((1, 128), np.float32)),
        "selNC": -np.kron(np.eye(NS, dtype=np.float32), np.ones((1, 128), np.float32)),
        "ident": np.eye(128, dtype=np.float32),
        "ones1": np.ones((1, 128), np.float32),
        "lnrow": np.concatenate([inputs["L0_attn_ln_g"], inputs["L0_attn_ln_b"],
                                 inputs["L0_ffn_SCB"], inputs["L0_ffn_bias"]])[None, :].astype(np.float32),
    }


def kernel(**inputs):
    shared = _prep(inputs)
    xfull = np.ascontiguousarray(np.asarray(inputs["L0_attn_x"], dtype=np.float32))
    nc = build_nc()
    in_maps = []
    for b in range(8):
        m = dict(shared)
        m["x"] = np.ascontiguousarray(xfull[b])
        in_maps.append(m)
    res = run_bass_kernel_spmd(nc, in_maps, core_ids=list(range(8)))
    return np.stack([r["out"] for r in res.results], axis=0)


# version 06
# speedup vs baseline: 1.1514x; 1.0042x over previous
import numpy as np
from contextlib import ExitStack
import concourse.bass as bass
import concourse.mybir as mybir
from concourse.bass_utils import run_bass_kernel_spmd

F32 = mybir.dt.float32
BF16 = mybir.dt.bfloat16
AF = mybir.ActivationFunctionType
ALU = mybir.AluOpType
AX = mybir.AxisListType.X

L, D, DI, NS, RK, OUTF = 4096, 1024, 2048, 16, 64, 4192
NW, WIN, NJ, NK, NTT = 8, 512, 16, 8, 32
EPS = 1e-5
ALL_PHASES = (0, 1, 2, 3, 4, 5, 6, 7)


def build_nc(phases=ALL_PHASES, debug=()):
    nc = bass.Bass("TRN2", target_bir_lowering=False, debug=False)

    def dram(name, shape, dtype):
        k = "ExternalOutput" if name in debug else "Internal"
        return nc.dram_tensor(name, shape, dtype, kind=k).ap()

    x = nc.dram_tensor("x", [L, D], F32, kind="ExternalInput").ap()
    WinT = nc.dram_tensor("WinT", [D, OUTF], F32, kind="ExternalInput").ap()
    WdtT = nc.dram_tensor("WdtT", [RK, DI], F32, kind="ExternalInput").ap()
    WoutT = nc.dram_tensor("WoutT", [DI, D], F32, kind="ExternalInput").ap()
    CBT = nc.dram_tensor("CBT", [D, D], F32, kind="ExternalInput").ap()
    A_logr = nc.dram_tensor("A_logr", [DI, NS], F32, kind="ExternalInput").ap()
    bdt_c = nc.dram_tensor("bdt_c", [128, NJ], F32, kind="ExternalInput").ap()
    convw_c = nc.dram_tensor("convw_c", [128, NJ * 4], F32, kind="ExternalInput").ap()
    convb_c = nc.dram_tensor("convb_c", [128, NJ], F32, kind="ExternalInput").ap()
    D_c = nc.dram_tensor("D_c", [128, NJ], F32, kind="ExternalInput").ap()
    selPC = nc.dram_tensor("selPC", [NS, NS * 128], F32, kind="ExternalInput").ap()
    selNC = nc.dram_tensor("selNC", [NS, NS * 128], F32, kind="ExternalInput").ap()
    ident = nc.dram_tensor("ident", [128, 128], F32, kind="ExternalInput").ap()
    ones1 = nc.dram_tensor("ones1", [1, 128], F32, kind="ExternalInput").ap()
    lnrow = nc.dram_tensor("lnrow", [1, 4 * D], F32, kind="ExternalInput").ap()
    out = nc.dram_tensor("out", [L, D], F32, kind="ExternalOutput").ap()

    h_WinTb = dram("h_WinTb", [D, OUTF], BF16)
    h_WdtTb = dram("h_WdtTb", [RK, DI], BF16)
    h_WoutTb = dram("h_WoutTb", [DI, D], BF16)
    h_CBTb = dram("h_CBTb", [D, D], BF16)
    h_xn = dram("h_xn", [L, D], F32)
    h_xnb = dram("h_xnb", [L, D], BF16)
    h_xs = dram("h_xs", [DI, L], F32)
    h_sz = dram("h_sz", [DI, L], BF16)
    h_u = dram("h_u", [DI, L], BF16)
    h_BT = dram("h_BT", [NS, L], BF16)
    h_CT = dram("h_CT", [NS, L], BF16)
    h_dtpT = dram("h_dtpT", [RK, L], BF16)
    h_dtneg = dram("h_dtneg", [DI, L], BF16)
    h_dtu = dram("h_dtu", [DI, L], BF16)
    h_Bbc = dram("h_Bbc", [NS * 128, L], BF16)
    h_Cbc = dram("h_Cbc", [NS * 128, L], BF16)
    h_yg = dram("h_yg", [DI, L], BF16)


    DMA_SEMS = ["qH", "qP0", "qP1", "qP2", "qP3", "qP4", "qP5",
                "qG0", "qG1", "qG2", "qG3", "qG4", "qT0", "qT1"]
    S = {n: 0 for n in ["sv", "sa", "sp", "so"] + DMA_SEMS}
    sems = {}
    rec = {}

    def inc(ins, name, n=1):
        S[name] += n
        ins.then_inc(sems[name], n)
        return S[name]

    def dma(eng, qname, out_, in_, tag=None, **kw):
        v = inc(eng.dma_start(out=out_, in_=in_, **kw), qname, 16)
        if tag is not None:
            rec[tag] = (qname, v)
        return v

    def dmat(eng, qname, out_, in_, tag=None):
        v = inc(eng.dma_start_transpose(out=out_, in_=in_), qname, 16)
        if tag is not None:
            rec[tag] = (qname, v)
        return v

    def wtag(eng, tag):
        qn, v = rec[tag]
        eng.wait_ge(sems[qn], v)

    barriers = {}

    def barrier(tag):
        barriers[tag] = dict(S)

    def wait_barrier(eng, tag, names=None):
        bc = barriers[tag]
        if names is None:
            names = list(bc.keys())
        for n in names:
            if bc[n] > 0:
                eng.wait_ge(sems[n], bc[n])

    with ExitStack() as _stk:
        ent = _stk.enter_context
        for _nm in list(S.keys()):
            sems[_nm] = ent(nc.semaphore(_nm))
        A_pos = ent(nc.sbuf_tensor("A_pos", [128, NJ * NS], F32))
        negb = ent(nc.sbuf_tensor("negb", [128, NJ], F32))
        convw = ent(nc.sbuf_tensor("convw", [128, NJ * 4], F32))
        convb = ent(nc.sbuf_tensor("convb", [128, NJ], F32))
        D_col = ent(nc.sbuf_tensor("D_col", [128, NJ], F32))
        identb = ent(nc.sbuf_tensor("identb", [128, 128], BF16))
        identf = ent(nc.sbuf_tensor("identf", [128, 128], F32))
        onesf = ent(nc.sbuf_tensor("onesf", [128, 128], F32))
        cns = ent(nc.sbuf_tensor("cns", [128, 4], F32))
        jnk = ent(nc.sbuf_tensor("jnk", [128, 512], F32))
        pY1 = ent(nc.psum_tensor("pY1", [128, 2048], F32))
        pY2 = ent(nc.psum_tensor("pY2", [128, 2048], F32))
        block = ent(nc.Block())

        def psl(i):
            return (pY1 if i < 4 else pY2)[:, (i % 4) * 512:(i % 4 + 1) * 512]

        if 0 in phases:
            _ps = ExitStack()
            p0_sg = _ps.enter_context(nc.sbuf_tensor("p0_sg", [128, 2 * OUTF], F32))
            p0_ot = _ps.enter_context(nc.sbuf_tensor("p0_ot", [128, 2 * OUTF], BF16))
            if True:
                b0 = dict(S)
                pieces = []
                for k in range(NK):
                    pieces.append((WinT[k * 128:(k + 1) * 128, :], h_WinTb[k * 128:(k + 1) * 128, :], OUTF))
                for j in range(NJ):
                    pieces.append((WoutT[j * 128:(j + 1) * 128, :], h_WoutTb[j * 128:(j + 1) * 128, :], D))
                for k in range(NK):
                    pieces.append((CBT[k * 128:(k + 1) * 128, :], h_CBTb[k * 128:(k + 1) * 128, :], D))

                @block.sync
                def _(sync):
                    dma(sync, "qH", A_pos[:].rearrange("p (j n) -> p j n", n=NS), A_logr.rearrange("(j p) n -> p j n", p=128), tag="p0a")
                    dma(sync, "qH", negb[:], bdt_c)
                    dma(sync, "qH", convw[:], convw_c)
                    dma(sync, "qH", convb[:], convb_c)
                    dma(sync, "qH", D_col[:], D_c)
                    dma(sync, "qH", identf[:], ident)
                    dma(sync, "qH", onesf[0:1, :], ones1)
                    dma(sync, "qH", p0_sg[0:RK, 0:DI], WdtT, tag="p0h")
                    for i, (src, dst, cols) in enumerate(pieces):
                        if i == 0:
                            sync.wait_ge(sems["sv"], b0["sv"] + 1)
                        if i >= 2:
                            sync.wait_ge(sems["sv"], b0["sv"] + 1 + (i - 1))
                        off = (i % 2) * OUTF
                        dma(sync, "qP0" if i % 2 == 0 else "qP1", p0_sg[:, off:off + cols], src, tag=("p0p", i))

                @block.gpsimd
                def _(gpsimd):
                    gpsimd.wait_ge(sems["sv"], b0["sv"] + 1)
                    dma(gpsimd, "qG4", h_WdtTb, p0_ot[0:RK, 0:DI])
                    for i, (src, dst, cols) in enumerate(pieces):
                        gpsimd.wait_ge(sems["sv"], b0["sv"] + i + 2)
                        off = (i % 2) * OUTF
                        dma(gpsimd, "qG0" if i % 2 == 0 else "qG1", dst, p0_ot[:, off:off + cols], tag=("p0s", i))
                @block.vector
                def _(vector):
                    vector.memset(cns[:, 0:1], float(EPS))
                    vector.memset(cns[:, 1:2], 1e-38)
                    vector.memset(cns[:, 2:3], -0.5)
                    vector.memset(cns[:, 3:4], 1.0 / D)
                    vector.memset(jnk[:], 0.0)
                    wtag(vector, "p0h")
                    vector.tensor_copy(out=identb[:], in_=identf[:])
                    vector.tensor_scalar(out=negb[:], in0=negb[:], scalar1=-1.0,
                                         scalar2=None, op0=ALU.mult)
                    ins = vector.tensor_copy(out=p0_ot[0:RK, 0:DI], in_=p0_sg[0:RK, 0:DI])
                    inc(ins, "sv", 1)
                    for i, (src, dst, cols) in enumerate(pieces):
                        wtag(vector, ("p0p", i))
                        if i >= 2:
                            wtag(vector, ("p0s", i - 2))
                        off = (i % 2) * OUTF
                        ins = vector.tensor_copy(out=p0_ot[:, off:off + cols],
                                                 in_=p0_sg[:, off:off + cols])
                        inc(ins, "sv", 1)

                @block.scalar
                def _(scalar):
                    wtag(scalar, "p0a")
                    ins = scalar.activation(out=A_pos[:], in_=A_pos[:], func=AF.Exp)
                    inc(ins, "sa", 1)

            _ps.close()
        barrier("p0")

        if 1 in phases:
            _ps = ExitStack()
            p1_x = _ps.enter_context(nc.sbuf_tensor("p1_x", [128, 2 * D], F32))
            p1_sq = _ps.enter_context(nc.sbuf_tensor("p1_sq", [128, 2 * D], F32))
            p1_xn = _ps.enter_context(nc.sbuf_tensor("p1_xn", [128, 2 * D], F32))
            p1_xnb = _ps.enter_context(nc.sbuf_tensor("p1_xnb", [128, 2 * D], BF16))
            p1_st = _ps.enter_context(nc.sbuf_tensor("p1_st", [128, 16], F32))
            if True:
                b0 = dict(S)

                @block.sync
                def _(sync):
                    wait_barrier(sync, "p0")
                    for tt in range(NTT):
                        if tt >= 2:
                            sync.wait_ge(sems["sv"], b0["sv"] + (tt - 1) * 2)
                        dma(sync, "qP0" if tt % 2 == 0 else "qP1",
                            p1_x[:, (tt % 2) * D:(tt % 2 + 1) * D],
                            x[tt * 128:(tt + 1) * 128, :], tag=("p1x", tt))

                @block.gpsimd
                def _(gpsimd):
                    for tt in range(NTT):
                        gpsimd.wait_ge(sems["sv"], b0["sv"] + tt * 2 + 2)
                        qq = "qG0" if tt % 2 == 0 else "qG1"
                        dma(gpsimd, qq, h_xn[tt * 128:(tt + 1) * 128, :],
                            p1_xn[:, (tt % 2) * D:(tt % 2 + 1) * D])
                        dma(gpsimd, qq, h_xnb[tt * 128:(tt + 1) * 128, :],
                            p1_xnb[:, (tt % 2) * D:(tt % 2 + 1) * D], tag=("p1s", tt))
                @block.scalar
                def _(scalar):
                    wait_barrier(scalar, "p0", ("qG0", "qG1", "qG2", "qG3", "qG4"))
                    for tt in range(NTT):
                        o8 = (tt % 2) * 8
                        wtag(scalar, ("p1x", tt))
                        if tt >= 2:
                            scalar.wait_ge(sems["sv"], b0["sv"] + (tt - 2) * 2 + 1)
                        ins = scalar.activation(out=p1_sq[:, (tt % 2) * D:(tt % 2 + 1) * D],
                                                in_=p1_x[:, (tt % 2) * D:(tt % 2 + 1) * D],
                                                func=AF.Square)
                        inc(ins, "sa", 1)
                        scalar.wait_ge(sems["sv"], b0["sv"] + tt * 2 + 1)
                        scalar.activation(out=p1_st[:, o8 + 5:o8 + 6],
                                          in_=p1_st[:, o8 + 3:o8 + 4],
                                          func=AF.Ln, scale=cns[:, 3:4], bias=cns[:, 0:1])
                        scalar.activation(out=p1_st[:, o8 + 6:o8 + 7],
                                          in_=p1_st[:, o8 + 5:o8 + 6],
                                          func=AF.Exp, scale=cns[:, 2:3])
                        ins = scalar.copy(out=jnk[:], in_=p1_sq[:, (tt % 2) * D:(tt % 2) * D + 512])
                        inc(ins, "sa", 1)

                @block.vector
                def _(vector):
                    wait_barrier(vector, "p0", ("qG0", "qG1", "qG2", "qG3", "qG4"))
                    for tt in range(NTT):
                        o8 = (tt % 2) * 8
                        xs_ = p1_x[:, (tt % 2) * D:(tt % 2 + 1) * D]
                        xn_ = p1_xn[:, (tt % 2) * D:(tt % 2 + 1) * D]
                        xnb_ = p1_xnb[:, (tt % 2) * D:(tt % 2 + 1) * D]
                        sl = lambda c: p1_st[:, o8 + c:o8 + c + 1]
                        wtag(vector, ("p1x", tt))
                        vector.tensor_reduce(out=sl(0), in_=xs_, axis=AX, op=ALU.add)
                        vector.wait_ge(sems["sa"], b0["sa"] + tt * 2 + 1)
                        vector.tensor_reduce(out=sl(1), in_=p1_sq[:, (tt % 2) * D:(tt % 2 + 1) * D],
                                             axis=AX, op=ALU.add)
                        vector.tensor_scalar(out=sl(4), in0=sl(0), scalar1=-1.0 / D,
                                             scalar2=None, op0=ALU.mult)
                        vector.tensor_copy(out=jnk[:], in_=xs_[:, 0:512])
                        vector.scalar_tensor_tensor(out=sl(3), in0=sl(0), scalar=sl(4),
                                                    in1=sl(1), op0=ALU.mult, op1=ALU.add)
                        ins = vector.tensor_copy(out=jnk[:], in_=xs_[:, 0:512])
                        inc(ins, "sv", 1)
                        vector.wait_ge(sems["sa"], b0["sa"] + tt * 2 + 2)
                        if tt >= 2:
                            wtag(vector, ("p1s", tt - 2))
                        vector.tensor_scalar(out=xn_, in0=xs_, scalar1=sl(4), scalar2=sl(6),
                                             op0=ALU.add, op1=ALU.mult)
                        ins = vector.tensor_copy(out=xnb_, in_=xn_)
                        inc(ins, "sv", 1)

            _ps.close()
        barrier("p1")

        if 2 in phases:
            NGRP = 35

            def gspec(g):
                if g < 32:
                    return g * 128, 128
                return {32: (4096, 16), 33: (4112, 16), 34: (4128, 64)}[g]

            _ps = ExitStack()
            p2_w = _ps.enter_context(nc.sbuf_tensor("p2_w", [128, NK * OUTF], BF16))
            p2_xt = _ps.enter_context(nc.sbuf_tensor("p2_xt", [128, 2 * NK * WIN], BF16))
            p2_ev = _ps.enter_context(nc.sbuf_tensor("p2_ev", [128, 2 * WIN], F32))
            p2_evb = _ps.enter_context(nc.sbuf_tensor("p2_evb", [128, 2 * WIN], BF16))
            p3_xs = _ps.enter_context(nc.sbuf_tensor("p3_xs", [128, 4 * 520], F32))
            p3_acc = _ps.enter_context(nc.sbuf_tensor("p3_acc", [128, 4 * WIN], F32))
            p3_u = _ps.enter_context(nc.sbuf_tensor("p3_u", [128, 4 * WIN], BF16))
            p4_wdt = _ps.enter_context(nc.sbuf_tensor("p4_wdt", [128, DI], BF16))
            p4_dtp = _ps.enter_context(nc.sbuf_tensor("p4_dtp", [128, 2 * WIN], BF16))
            p4_sig = _ps.enter_context(nc.sbuf_tensor("p4_sig", [128, 8 * WIN], F32))
            p4_dtn = _ps.enter_context(nc.sbuf_tensor("p4_dtn", [128, 4 * WIN], BF16))
            p4_u = _ps.enter_context(nc.sbuf_tensor("p4_u", [128, 4 * WIN], BF16))
            p4_dtu = _ps.enter_context(nc.sbuf_tensor("p4_dtu", [128, 4 * WIN], BF16))
            p5_selb = _ps.enter_context(nc.sbuf_tensor("p5_selb", [128, 2 * NS * 128], BF16))
            p5_bt = _ps.enter_context(nc.sbuf_tensor("p5_bt", [128, 2 * WIN], BF16))
            p5_bc = _ps.enter_context(nc.sbuf_tensor("p5_bc", [128, 4 * WIN], BF16))
            if True:
                b0 = dict(S)
                SEL = NS * 128
                mark = {}

                class _StubIns:
                    def then_inc(self, *a, **k):
                        return self

                class _StubEng:
                    def __getattr__(self, name):
                        def f(*a, **k):
                            return _StubIns()
                        return f

                class _StubBlock:
                    def __getattr__(self, name):
                        def reg(f):
                            f(_StubEng())
                        return reg

                def emit_prod(blk, counting):
                    def mk_mark(tag):
                        if counting:
                            mark[tag] = dict(S)

                    def wmark(eng, sem_name, tag, delta=0):
                        if not counting:
                            eng.wait_ge(sems[sem_name], mark[tag][sem_name] + delta)

                    def wtag2(eng, tag):
                        if not counting:
                            wtag(eng, tag)

                    @blk.sync
                    def _(sync):
                        if not counting:
                            wait_barrier(sync, "p1")
                        for k in range(NK):
                            dma(sync, "qH", p2_w[:, k * OUTF:(k + 1) * OUTF],
                                h_WinTb[k * 128:(k + 1) * 128, :], tag="p2w")
                        for half, selsrc in ((0, selNC), (1, selPC)):
                            dma(sync, "qH", p3_acc[0:NS, 0:SEL],
                                selsrc, tag=("selst", half))
                            if not counting:
                                wmark(sync, "sv", ("selcast", half))
                        dma(sync, "qH", p4_wdt[0:RK, :], h_WdtTb, tag="p4w")
                        for W in range(NW + 2):
                            w2, w3, w4, w5 = W, W - 1, W - 2, W - 1
                            if w2 < NW:
                                if w2 >= 2:
                                    wmark(sync, "sp", ("projend", w2 - 1))
                                off = (w2 % 2) * NK * WIN
                                for k in range(NK):
                                    dmat(sync, "qT0" if w2 % 2 == 0 else "qT1",
                                         p2_xt[:, off + k * WIN:off + (k + 1) * WIN],
                                         h_xnb[w2 * WIN:(w2 + 1) * WIN, k * 128:(k + 1) * 128],
                                         tag=("p2xt", w2))
                            if 0 <= w3 < NW:
                                wtag2(sync, ("p2s", w3 * NGRP + 14))
                                wtag2(sync, ("p2s", w3 * NGRP + 15))
                                for j in range(NJ):
                                    i3 = w3 * NJ + j
                                    if i3 >= 4:
                                        wmark(sync, "sv", ("p3conv", i3 - 4))
                                    off = (i3 % 4) * 520
                                    qq = "qP%d" % (i3 % 4)
                                    r = slice(j * 128, (j + 1) * 128)
                                    if w3 == 0:
                                        dma(sync, qq, p3_xs[:, off + 3:off + 3 + WIN],
                                            h_xs[r, 0:WIN], tag=("p3x", i3))
                                    else:
                                        dma(sync, qq, p3_xs[:, off:off + 3],
                                            h_xs[r, w3 * WIN - 3:w3 * WIN])
                                        dma(sync, qq, p3_xs[:, off + 3:off + 3 + WIN],
                                            h_xs[r, w3 * WIN:(w3 + 1) * WIN], tag=("p3x", i3))
                            if 0 <= w4 < NW:
                                if w4 >= 2:
                                    wmark(sync, "sp", ("p4mmend", w4 - 1))
                                wtag2(sync, ("p2s", w4 * NGRP + 34))
                                dma(sync, "qP4" if w4 % 2 == 0 else "qP5",
                                    p4_dtp[0:RK, (w4 % 2) * WIN:(w4 % 2 + 1) * WIN],
                                    h_dtpT[:, w4 * WIN:(w4 + 1) * WIN], tag=("p4d", w4))
                                for j in range(NJ):
                                    i4 = w4 * NJ + j
                                    if i4 >= 4:
                                        wmark(sync, "sv", ("p4dtu", i4 - 4))
                                    wtag2(sync, ("p3s", i4))
                                    dma(sync, "qP%d" % (i4 % 4),
                                        p4_u[:, (i4 % 4) * WIN:(i4 % 4 + 1) * WIN],
                                        h_u[j * 128:(j + 1) * 128, w4 * WIN:(w4 + 1) * WIN],
                                        tag=("p4u", i4))
                            if 0 <= w5 < NW:
                                if w5 >= 2:
                                    wmark(sync, "sp", ("p5mmend", w5 - 1))
                                wtag2(sync, ("p2s", w5 * NGRP + 32))
                                wtag2(sync, ("p2s", w5 * NGRP + 33))
                                qq = "qP4" if w5 % 2 == 0 else "qP5"
                                dma(sync, qq, p5_bt[0:NS, 0:WIN], h_BT[:, w5 * WIN:(w5 + 1) * WIN])
                                dma(sync, qq, p5_bt[0:NS, WIN:2 * WIN],
                                    h_CT[:, w5 * WIN:(w5 + 1) * WIN], tag=("p5b", w5))
                    @blk.gpsimd
                    def _(gpsimd):
                        if not counting:
                            wait_barrier(gpsimd, "p1", ())
                        for W in range(NW + 2):
                            w2, w3, w4, w5 = W, W - 1, W - 2, W - 1
                            if w2 < NW:
                                tsl = slice(w2 * WIN, (w2 + 1) * WIN)
                                for g in range(NGRP):
                                    G = w2 * NGRP + g
                                    wmark(gpsimd, "sa", ("p2ev", G))
                                    eo = (G % 2) * WIN
                                    qq = "qG0" if G % 2 == 0 else "qG1"
                                    tg = ("p2s", G)
                                    if g < 16:
                                        dma(gpsimd, qq, h_xs[g * 128:(g + 1) * 128, tsl],
                                            p2_ev[:, eo:eo + WIN], tag=tg)
                                    elif g < 32:
                                        dma(gpsimd, qq, h_sz[(g - 16) * 128:(g - 15) * 128, tsl],
                                            p2_evb[:, eo:eo + WIN], tag=tg)
                                    elif g == 32:
                                        dma(gpsimd, qq, h_BT[:, tsl],
                                            p2_evb[0:16, eo:eo + WIN], tag=tg)
                                    elif g == 33:
                                        dma(gpsimd, qq, h_CT[:, tsl],
                                            p2_evb[0:16, eo:eo + WIN], tag=tg)
                                    else:
                                        dma(gpsimd, qq, h_dtpT[:, tsl],
                                            p2_evb[0:64, eo:eo + WIN], tag=tg)
                            if 0 <= w3 < NW:
                                for j in range(NJ):
                                    i3 = w3 * NJ + j
                                    wmark(gpsimd, "sa", ("p3silu", i3))
                                    uo = (i3 % 4) * WIN
                                    dma(gpsimd, "qG%d" % (i3 % 4),
                                        h_u[j * 128:(j + 1) * 128, w3 * WIN:(w3 + 1) * WIN],
                                        p3_u[:, uo:uo + WIN], tag=("p3s", i3))
                            if 0 <= w4 < NW:
                                for j in range(NJ):
                                    i4 = w4 * NJ + j
                                    wmark(gpsimd, "sv", ("p4dtu", i4))
                                    qq = "qG%d" % (i4 % 4)
                                    dma(gpsimd, qq,
                                        h_dtneg[j * 128:(j + 1) * 128, w4 * WIN:(w4 + 1) * WIN],
                                        p4_dtn[:, (i4 % 4) * WIN:(i4 % 4 + 1) * WIN])
                                    dma(gpsimd, qq,
                                        h_dtu[j * 128:(j + 1) * 128, w4 * WIN:(w4 + 1) * WIN],
                                        p4_dtu[:, (i4 % 4) * WIN:(i4 % 4 + 1) * WIN],
                                        tag=("p4s", i4))
                            if 0 <= w5 < NW:
                                for g in range(32):
                                    G5 = w5 * 32 + g
                                    n = g % 16
                                    wmark(gpsimd, "sa", ("p5ev", G5))
                                    dst = h_Bbc if g < 16 else h_Cbc
                                    dma(gpsimd, "qG%d" % (G5 % 4),
                                        dst[n * 128:(n + 1) * 128, w5 * WIN:(w5 + 1) * WIN],
                                        p5_bc[:, (G5 % 4) * WIN:(G5 % 4 + 1) * WIN],
                                        tag=("p5s", G5))
                    @blk.tensor
                    def _(tensor):
                        if not counting:
                            wait_barrier(tensor, "p1", ("qG0", "qG1"))
                        wtag2(tensor, "p2w")
                        wtag2(tensor, "p4w")
                        wmark(tensor, "sv", "selcast2")
                        for W in range(NW + 2):
                            w2, w3, w4, w5 = W, W - 1, W - 2, W - 1
                            if w2 < NW:
                                wtag2(tensor, ("p2xt", w2))
                                off = (w2 % 2) * NK * WIN
                                for g in range(NGRP):
                                    G = w2 * NGRP + g
                                    if G >= 4:
                                        wmark(tensor, "sa", ("p2ev", G - 4))
                                    c0, M = gspec(g)
                                    ps = psl(G % 4)
                                    for k in range(NK):
                                        ins = tensor.matmul(
                                            ps[0:M, :],
                                            p2_w[:, k * OUTF + c0:k * OUTF + c0 + M],
                                            p2_xt[:, off + k * WIN:off + (k + 1) * WIN],
                                            start=(k == 0), stop=(k == NK - 1))
                                    inc(ins, "sp", 1)
                                    mk_mark(("projg", w2, g))
                                mk_mark(("projend", w2))
                            if 0 <= w4 < NW:
                                wtag2(tensor, ("p4d", w4))
                                for j in range(NJ):
                                    i4 = w4 * NJ + j
                                    if i4 >= 2:
                                        wmark(tensor, "sa", ("p4sig", i4 - 2))
                                    ins = tensor.matmul(psl(4 + (i4 % 2)),
                                                        p4_wdt[0:RK, j * 128:(j + 1) * 128],
                                                        p4_dtp[0:RK, (w4 % 2) * WIN:(w4 % 2 + 1) * WIN],
                                                        start=True, stop=True)
                                    inc(ins, "sp", 1)
                                    mk_mark(("p4mm", i4))
                                mk_mark(("p4mmend", w4))
                            if 0 <= w5 < NW:
                                wtag2(tensor, ("p5b", w5))
                                for g in range(32):
                                    G5 = w5 * 32 + g
                                    n = g % 16
                                    isB = g < 16
                                    if G5 >= 2:
                                        wmark(tensor, "sa", ("p5ev", G5 - 2))
                                    so_ = 0 if isB else SEL
                                    ins = tensor.matmul(psl(6 + (G5 % 2)),
                                                        p5_selb[0:NS, so_ + n * 128:so_ + (n + 1) * 128],
                                                        p5_bt[0:NS, (0 if isB else WIN):(WIN if isB else 2 * WIN)],
                                                        start=True, stop=True)
                                    inc(ins, "sp", 1)
                                    mk_mark(("p5mm", G5))
                                mk_mark(("p5mmend", w5))
                    @blk.scalar
                    def _(scalar):
                        if not counting:
                            wait_barrier(scalar, "p1", ())
                        for W in range(NW + 2):
                            w2, w3, w4, w5 = W, W - 1, W - 2, W - 1
                            if w2 < NW:
                                for g in range(NGRP):
                                    G = w2 * NGRP + g
                                    wmark(scalar, "sp", ("projg", w2, g))
                                    if G >= 2:
                                        wtag2(scalar, ("p2s", G - 2))
                                    ps = psl(G % 4)
                                    eo = (G % 2) * WIN
                                    if g < 16:
                                        ins = scalar.copy(out=p2_ev[:, eo:eo + WIN], in_=ps)
                                    elif g < 32:
                                        ins = scalar.activation(out=p2_evb[:, eo:eo + WIN],
                                                                in_=ps, func=AF.Silu)
                                    elif g in (32, 33):
                                        ins = scalar.copy(out=p2_evb[0:16, eo:eo + WIN],
                                                          in_=ps[0:16, :])
                                    else:
                                        ins = scalar.copy(out=p2_evb[0:64, eo:eo + WIN],
                                                          in_=ps[0:64, :])
                                    inc(ins, "sa", 1)
                                    mk_mark(("p2ev", G))
                            if 0 <= w3 < NW:
                                for j in range(NJ):
                                    i3 = w3 * NJ + j
                                    wmark(scalar, "sv", ("p3conv", i3))
                                    if i3 >= 4:
                                        wtag2(scalar, ("p3s", i3 - 4))
                                    ins = scalar.activation(out=p3_u[:, (i3 % 4) * WIN:(i3 % 4 + 1) * WIN],
                                                            in_=p3_acc[:, (i3 % 4) * WIN:(i3 % 4 + 1) * WIN],
                                                            func=AF.Silu, bias=convb[:, j:j + 1])
                                    inc(ins, "sa", 1)
                                    mk_mark(("p3silu", i3))
                            if 0 <= w4 < NW:
                                for jh in range(2):
                                    for jj in range(8):
                                        j = jh * 8 + jj
                                        i4 = w4 * NJ + j
                                        wmark(scalar, "sp", ("p4mm", i4))
                                        ins = scalar.activation(out=p4_sig[:, jj * WIN:(jj + 1) * WIN],
                                                                in_=psl(4 + (i4 % 2)), func=AF.Sigmoid,
                                                                scale=-1.0, bias=negb[:, j:j + 1])
                                        inc(ins, "sa", 1)
                                        mk_mark(("p4sig", i4))
                                    for jj in range(8):
                                        j = jh * 8 + jj
                                        i4 = w4 * NJ + j
                                        if i4 >= 4:
                                            wtag2(scalar, ("p4s", i4 - 4))
                                        ins = scalar.activation(out=p4_dtn[:, (i4 % 4) * WIN:(i4 % 4 + 1) * WIN],
                                                                in_=p4_sig[:, jj * WIN:(jj + 1) * WIN],
                                                                func=AF.Ln, bias=cns[:, 1:2])
                                        inc(ins, "sa", 1)
                                        mk_mark(("p4ln", i4))
                            if 0 <= w5 < NW:
                                for g in range(32):
                                    G5 = w5 * 32 + g
                                    wmark(scalar, "sp", ("p5mm", G5))
                                    if G5 >= 4:
                                        wtag2(scalar, ("p5s", G5 - 4))
                                    ins = scalar.copy(out=p5_bc[:, (G5 % 4) * WIN:(G5 % 4 + 1) * WIN],
                                                      in_=psl(6 + (G5 % 2)))
                                    inc(ins, "sa", 1)
                                    mk_mark(("p5ev", G5))

                    @blk.vector
                    def _(vector):
                        if not counting:
                            wait_barrier(vector, "p1", ())
                        for half in range(2):
                            wtag2(vector, ("selst", half))
                            ins = vector.tensor_copy(out=p5_selb[0:NS, half * SEL:(half + 1) * SEL],
                                                     in_=p3_acc[0:NS, 0:SEL])
                            inc(ins, "sv", 1)
                            mk_mark(("selcast", half))
                        mk_mark("selcast2")
                        for W in range(NW + 2):
                            w2, w3, w4, w5 = W, W - 1, W - 2, W - 1
                            if 0 <= w3 < NW:
                                for j in range(NJ):
                                    i3 = w3 * NJ + j
                                    off = (i3 % 4) * 520
                                    ao = (i3 % 4) * WIN
                                    wtag2(vector, ("p3x", i3))
                                    if w3 == 0:
                                        vector.memset(p3_xs[:, off:off + 3], 0.0)
                                    if i3 >= 4:
                                        wmark(vector, "sa", ("p3silu", i3 - 4))
                                    vector.tensor_scalar(out=p3_acc[:, ao:ao + WIN],
                                                         in0=p3_xs[:, off + 3:off + 3 + WIN],
                                                         scalar1=convw[:, j * 4 + 3:j * 4 + 4],
                                                         scalar2=None, op0=ALU.mult)
                                    for k in (2, 1, 0):
                                        ins = vector.scalar_tensor_tensor(
                                            out=p3_acc[:, ao:ao + WIN],
                                            in0=p3_xs[:, off + k:off + k + WIN],
                                            scalar=convw[:, j * 4 + k:j * 4 + k + 1],
                                            in1=p3_acc[:, ao:ao + WIN],
                                            op0=ALU.mult, op1=ALU.add)
                                    inc(ins, "sv", 1)
                                    mk_mark(("p3conv", i3))
                            if 0 <= w4 < NW:
                                for j in range(NJ):
                                    i4 = w4 * NJ + j
                                    wmark(vector, "sa", ("p4ln", i4))
                                    wtag2(vector, ("p4u", i4))
                                    ins = vector.tensor_tensor(
                                        out=p4_dtu[:, (i4 % 4) * WIN:(i4 % 4 + 1) * WIN],
                                        in0=p4_dtn[:, (i4 % 4) * WIN:(i4 % 4 + 1) * WIN],
                                        in1=p4_u[:, (i4 % 4) * WIN:(i4 % 4 + 1) * WIN],
                                        op=ALU.mult)
                                    inc(ins, "sv", 1)
                                    mk_mark(("p4dtu", i4))

                emit_prod(_StubBlock(), True)
                S.clear()
                S.update(b0)
                emit_prod(block, False)
            _ps.close()
        barrier("p5")

        if 6 in phases:
            _ps = ExitStack()
            p6_dtn = _ps.enter_context(nc.sbuf_tensor("p6_dtn", [128, 2 * L], BF16))
            p6_dA = _ps.enter_context(nc.sbuf_tensor("p6_dA", [128, 2 * L], F32))
            p6_dtu = _ps.enter_context(nc.sbuf_tensor("p6_dtu", [128, 2 * L], BF16))
            p6_u = _ps.enter_context(nc.sbuf_tensor("p6_u", [128, 2 * L], BF16))
            p6_sz = _ps.enter_context(nc.sbuf_tensor("p6_sz", [128, 2 * L], BF16))
            p6_bcB = _ps.enter_context(nc.sbuf_tensor("p6_bcB", [128, 2 * L], BF16))
            p6_bcC = _ps.enter_context(nc.sbuf_tensor("p6_bcC", [128, 2 * L], BF16))
            p6_X = _ps.enter_context(nc.sbuf_tensor("p6_X", [128, L], BF16))
            p6_h = _ps.enter_context(nc.sbuf_tensor("p6_h", [128, L], BF16))
            p6_hC = _ps.enter_context(nc.sbuf_tensor("p6_hC", [128, 2 * L], BF16))
            p6_yg = _ps.enter_context(nc.sbuf_tensor("p6_yg", [128, L], BF16))
            if True:
                b0 = dict(S)
                PJ = {"sa": 16, "sv": 51, "sp": 16}

                def jb(j, name):
                    return b0[name] + j * PJ[name]

                @block.sync
                def _(sync):
                    wait_barrier(sync, "p5")
                    for j in range(NJ):
                        r = slice(j * 128, (j + 1) * 128)
                        jo = (j % 2) * L
                        if j >= 2:
                            sync.wait_ge(sems["sv"], jb(j - 1, "sv"))
                            sync.wait_ge(sems["sa"], jb(j - 1, "sa"))
                        dma(sync, "qH", p6_dtn[:, jo:jo + L], h_dtneg[r, :])
                        dma(sync, "qH", p6_dtu[:, jo:jo + L], h_dtu[r, :])
                        dma(sync, "qH", p6_u[:, jo:jo + L], h_u[r, :])
                        dma(sync, "qH", p6_sz[:, jo:jo + L], h_sz[r, :], tag=("p6j", j))
                        for n in range(NS):
                            if n >= 2:
                                sync.wait_ge(sems["sv"], jb(j, "sv") + 3 * n - 3)
                            elif j >= 1:
                                sync.wait_ge(sems["sv"], jb(j - 1, "sv") + 3 * (14 + n) + 3)
                            nb = (n % 2) * L
                            qq = "qP0" if n % 2 == 0 else "qP1"
                            dma(sync, qq, p6_bcB[:, nb:nb + L], h_Bbc[n * 128:(n + 1) * 128, :])
                            dma(sync, qq, p6_bcC[:, nb:nb + L], h_Cbc[n * 128:(n + 1) * 128, :],
                                tag=("p6bc", j, n))

                @block.gpsimd
                def _(gpsimd):
                    wait_barrier(gpsimd, "p5", ())
                    for j in range(NJ):
                        gpsimd.wait_ge(sems["sv"], jb(j, "sv") + PJ["sv"])
                        dma(gpsimd, "qG4", h_yg[j * 128:(j + 1) * 128, :], p6_yg[:],
                            tag=("p6s", j))

                @block.scalar
                def _(scalar):
                    wait_barrier(scalar, "p5", ())
                    for j in range(NJ):
                        jo = (j % 2) * L
                        wtag(scalar, ("p6j", j))
                        for n in range(NS):
                            if n >= 2:
                                scalar.wait_ge(sems["sv"], jb(j, "sv") + 3 * n - 4)
                            elif j >= 1:
                                if n == 0:
                                    scalar.wait_ge(sems["sv"], jb(j, "sv"))
                                else:
                                    scalar.wait_ge(sems["sv"], jb(j - 1, "sv") + 47)
                            ins = scalar.activation(out=p6_dA[:, (n % 2) * L:(n % 2 + 1) * L],
                                                    in_=p6_dtn[:, jo:jo + L], func=AF.Exp,
                                                    scale=A_pos[:, j * NS + n:j * NS + n + 1])
                            inc(ins, "sa", 1)

                @block.vector
                def _(vector):
                    wait_barrier(vector, "p5", ())
                    for j in range(NJ):
                        jo = (j % 2) * L
                        for n in range(NS):
                            nb = (n % 2) * L
                            if n == 0:
                                wtag(vector, ("p6j", j))
                            wtag(vector, ("p6bc", j, n))
                            ins = vector.tensor_tensor(out=p6_X[:],
                                                       in0=p6_dtu[:, jo:jo + L],
                                                       in1=p6_bcB[:, nb:nb + L],
                                                       op=ALU.mult)
                            inc(ins, "sv", 1)
                            vector.wait_ge(sems["sa"], jb(j, "sa") + n + 1)
                            ins = vector.tensor_tensor_scan(out=p6_h[:],
                                                            data0=p6_dA[:, nb:nb + L],
                                                            data1=p6_X[:],
                                                            initial=0.0,
                                                            op0=ALU.mult, op1=ALU.add)
                            inc(ins, "sv", 1)
                            if n >= 2:
                                vector.wait_ge(sems["sp"], jb(j, "sp") + n - 1)
                            elif j >= 1:
                                vector.wait_ge(sems["sp"], jb(j - 1, "sp") + 15 + n)
                            ins = vector.tensor_tensor(out=p6_hC[:, nb:nb + L],
                                                       in0=p6_h[:],
                                                       in1=p6_bcC[:, nb:nb + L],
                                                       op=ALU.mult)
                            inc(ins, "sv", 1)
                        vector.wait_ge(sems["sp"], jb(j, "sp") + 16)
                        ins = vector.scalar_tensor_tensor(out=p6_dA[:, 0:2048],
                                                          in0=p6_u[:, jo:jo + 2048],
                                                          scalar=D_col[:, j:j + 1],
                                                          in1=pY1[:],
                                                          op0=ALU.mult, op1=ALU.add)
                        inc(ins, "sv", 1)
                        ins = vector.scalar_tensor_tensor(out=p6_dA[:, 2048:4096],
                                                          in0=p6_u[:, jo + 2048:jo + 4096],
                                                          scalar=D_col[:, j:j + 1],
                                                          in1=pY2[:],
                                                          op0=ALU.mult, op1=ALU.add)
                        inc(ins, "sv", 1)
                        if j >= 1:
                            wtag(vector, ("p6s", j - 1))
                        ins = vector.tensor_tensor(out=p6_yg[:], in0=p6_dA[:, 0:4096],
                                                   in1=p6_sz[:, jo:jo + L], op=ALU.mult)
                        inc(ins, "sv", 1)

                @block.tensor
                def _(tensor):
                    wait_barrier(tensor, "p5", ())
                    for j in range(NJ):
                        for n in range(NS):
                            nb = (n % 2) * L
                            tensor.wait_ge(sems["sv"], jb(j, "sv") + 3 * n + 3)
                            if n == 0 and j >= 1:
                                tensor.wait_ge(sems["sv"], jb(j, "sv") - 1)
                            for s in range(8):
                                ins = tensor.matmul(psl(s), identb[:],
                                                    p6_hC[:, nb + s * 512:nb + (s + 1) * 512],
                                                    start=(n == 0), stop=(n == NS - 1))
                            inc(ins, "sp", 1)
            _ps.close()
        barrier("p6")

        if 7 in phases:
            _ps = ExitStack()
            p7_wout = _ps.enter_context(nc.sbuf_tensor("p7_wout", [128, NJ * D], BF16))
            p7_cbt = _ps.enter_context(nc.sbuf_tensor("p7_cbt", [128, NK * D], BF16))
            p7_yg = _ps.enter_context(nc.sbuf_tensor("p7_yg", [128, 2 * NJ * 128], BF16))
            p7_xt = _ps.enter_context(nc.sbuf_tensor("p7_xt", [128, 2 * NK * 128], BF16))
            p7_xn = _ps.enter_context(nc.sbuf_tensor("p7_xn", [128, 2 * D], F32))
            p7_x = _ps.enter_context(nc.sbuf_tensor("p7_x", [128, 2 * D], F32))
            p7_fa = _ps.enter_context(nc.sbuf_tensor("p7_fa", [128, D], F32))
            p7_sq = _ps.enter_context(nc.sbuf_tensor("p7_sq", [128, D], F32))
            p7_fb = _ps.enter_context(nc.sbuf_tensor("p7_fb", [128, D], F32))
            p7_out = _ps.enter_context(nc.sbuf_tensor("p7_out", [128, 2 * D], F32))
            p7_st = _ps.enter_context(nc.sbuf_tensor("p7_st", [128, 16], F32))
            p7_row = _ps.enter_context(nc.sbuf_tensor("p7_row", [128, 4 * D], F32))
            p7_gbc = _ps.enter_context(nc.sbuf_tensor("p7_gbc", [128, D], F32))
            p7_sbc = _ps.enter_context(nc.sbuf_tensor("p7_sbc", [128, D], F32))
            p7_bbc = _ps.enter_context(nc.sbuf_tensor("p7_bbc", [128, D], F32))
            p7_tbc = _ps.enter_context(nc.sbuf_tensor("p7_tbc", [128, D], F32))
            if True:
                b0 = dict(S)

                @block.sync
                def _(sync):
                    wait_barrier(sync, "p6")
                    dma(sync, "qH", p7_row[0:1, :], lnrow, tag="p7row")
                    for j in range(NJ):
                        dma(sync, "qH", p7_wout[:, j * D:(j + 1) * D],
                            h_WoutTb[j * 128:(j + 1) * 128, :])
                    for k in range(NK):
                        dma(sync, "qH", p7_cbt[:, k * D:(k + 1) * D],
                            h_CBTb[k * 128:(k + 1) * 128, :], tag="p7h")
                    for tt in range(NTT):
                        if tt >= 2:
                            sync.wait_ge(sems["sv"], b0["sv"] + 1 + (tt - 1) * 9)
                        off = tt % 2
                        qq = "qP0" if tt % 2 == 0 else "qP1"
                        r = slice(tt * 128, (tt + 1) * 128)
                        for j in range(NJ):
                            dma(sync, qq,
                                p7_yg[:, off * NJ * 128 + j * 128:off * NJ * 128 + (j + 1) * 128],
                                h_yg[j * 128:(j + 1) * 128, r])
                        for k in range(NK):
                            dmat(sync, "qT0" if tt % 2 == 0 else "qT1",
                                 p7_xt[:, off * NK * 128 + k * 128:off * NK * 128 + (k + 1) * 128],
                                 h_xnb[r, k * 128:(k + 1) * 128], tag=("p7xt", tt))
                        dma(sync, qq, p7_xn[:, off * D:(off + 1) * D], h_xn[r, :])
                        dma(sync, qq, p7_x[:, off * D:(off + 1) * D], x[r, :], tag=("p7l", tt))

                @block.tensor
                def _(tensor):
                    wait_barrier(tensor, "p6")
                    wtag(tensor, "p7row")
                    for i in range(4):
                        for half in range(2):
                            ins = tensor.matmul(psl(i * 2 + half), onesf[0:1, :],
                                                p7_row[0:1, i * D + half * 512:i * D + half * 512 + 512],
                                                start=True, stop=True)
                        inc(ins, "sp", 1)

                @block.scalar
                def _(scalar):
                    wait_barrier(scalar, "p6", ())
                    scalar.wait_ge(sems["sp"], b0["sp"] + 4)
                    for half in range(2):
                        scalar.copy(out=p7_gbc[:, half * 512:(half + 1) * 512], in_=psl(0 + half))
                        scalar.copy(out=p7_bbc[:, half * 512:(half + 1) * 512], in_=psl(2 + half))
                        scalar.copy(out=p7_sbc[:, half * 512:(half + 1) * 512], in_=psl(4 + half))
                        ins = scalar.copy(out=p7_tbc[:, half * 512:(half + 1) * 512], in_=psl(6 + half))
                    inc(ins, "sa", 1)

                @block.vector
                def _(vector):
                    wait_barrier(vector, "p6", ())
                    vector.wait_ge(sems["sa"], b0["sa"] + 1)
                    vector.tensor_scalar(out=p7_sbc[:], in0=p7_sbc[:],
                                         scalar1=1.0 / 127.0, scalar2=None, op0=ALU.mult)
                    ins = vector.tensor_tensor(out=p7_bbc[:], in0=p7_bbc[:],
                                               in1=p7_tbc[:], op=ALU.add)
                    inc(ins, "sv", 1)

                bp = dict(S)

                @block.gpsimd
                def _(gpsimd):
                    for tt in range(NTT):
                        off = tt % 2
                        gpsimd.wait_ge(sems["sv"], bp["sv"] + (tt + 1) * 9)
                        dma(gpsimd, "qG0" if tt % 2 == 0 else "qG1",
                            out[tt * 128:(tt + 1) * 128, :],
                            p7_out[:, off * D:(off + 1) * D], tag=("p7o", tt))

                @block.tensor
                def _(tensor):
                    tensor.wait_ge(sems["sv"], bp["sv"])
                    for tt in range(NTT):
                        off = tt % 2
                        wtag(tensor, ("p7l", tt))
                        wtag(tensor, ("p7xt", tt))
                        if tt >= 2:
                            tensor.wait_ge(sems["sv"], bp["sv"] + (tt - 1) * 9)
                        for half in range(2):
                            for j in range(NJ):
                                ins = tensor.matmul(
                                    psl(half),
                                    p7_yg[:, off * NJ * 128 + j * 128:off * NJ * 128 + (j + 1) * 128],
                                    p7_wout[:, j * D + half * 512:j * D + half * 512 + 512],
                                    start=(j == 0), stop=(j == NJ - 1))
                            inc(ins, "sp", 1)
                        for half in range(2):
                            for k in range(NK):
                                ins = tensor.matmul(
                                    psl(4 + half),
                                    p7_xt[:, off * NK * 128 + k * 128:off * NK * 128 + (k + 1) * 128],
                                    p7_cbt[:, k * D + half * 512:k * D + half * 512 + 512],
                                    start=(k == 0), stop=(k == NK - 1))
                            inc(ins, "sp", 1)

                @block.scalar
                def _(scalar):
                    for tt in range(NTT):
                        scalar.wait_ge(sems["sv"], bp["sv"] + tt * 9 + 1)
                        ins = scalar.activation(out=p7_sq[:], in_=p7_fa[:], func=AF.Square)
                        inc(ins, "sa", 1)
                        scalar.wait_ge(sems["sv"], bp["sv"] + tt * 9 + 2)
                        scalar.activation(out=p7_st[:, 5:6], in_=p7_st[:, 3:4],
                                          func=AF.Ln, scale=cns[:, 3:4], bias=cns[:, 0:1])
                        scalar.activation(out=p7_st[:, 6:7], in_=p7_st[:, 5:6],
                                          func=AF.Exp, scale=cns[:, 2:3])
                        ins = scalar.copy(out=jnk[:], in_=p7_sq[:, 0:512])
                        inc(ins, "sa", 1)

                @block.vector
                def _(vector):
                    for tt in range(NTT):
                        off = tt % 2
                        sl = lambda c: p7_st[:, c:c + 1]
                        vector.wait_ge(sems["sp"], bp["sp"] + tt * 4 + 2)
                        wtag(vector, ("p7l", tt))
                        vector.tensor_tensor(out=p7_fa[:, 0:512], in0=pY1[:, 0:512],
                                             in1=p7_xn[:, off * D:off * D + 512], op=ALU.add)
                        ins = vector.tensor_tensor(out=p7_fa[:, 512:1024], in0=pY1[:, 512:1024],
                                                   in1=p7_xn[:, off * D + 512:off * D + 1024],
                                                   op=ALU.add)
                        inc(ins, "sv", 1)
                        vector.tensor_reduce(out=sl(0), in_=p7_fa[:], axis=AX, op=ALU.add)
                        vector.wait_ge(sems["sa"], bp["sa"] + tt * 2 + 1)
                        vector.tensor_reduce(out=sl(1), in_=p7_sq[:], axis=AX, op=ALU.add)
                        vector.tensor_scalar(out=sl(4), in0=sl(0), scalar1=-1.0 / D,
                                             scalar2=None, op0=ALU.mult)
                        vector.tensor_copy(out=jnk[:], in_=p7_fa[:, 0:512])
                        vector.scalar_tensor_tensor(out=sl(3), in0=sl(0), scalar=sl(4),
                                                    in1=sl(1), op0=ALU.mult, op1=ALU.add)
                        ins = vector.tensor_copy(out=jnk[:], in_=p7_fa[:, 0:512])
                        inc(ins, "sv", 1)
                        vector.wait_ge(sems["sa"], bp["sa"] + tt * 2 + 2)
                        ins = vector.tensor_scalar(out=p7_fb[:], in0=p7_fa[:], scalar1=sl(4),
                                                   scalar2=sl(6), op0=ALU.add, op1=ALU.mult)
                        inc(ins, "sv", 1)
                        ins = vector.tensor_tensor(out=p7_fb[:], in0=p7_fb[:], in1=p7_gbc[:],
                                                   op=ALU.mult)
                        inc(ins, "sv", 1)
                        vector.wait_ge(sems["sp"], bp["sp"] + tt * 4 + 4)
                        vector.tensor_tensor(out=p7_fa[:, 0:512], in0=pY2[:, 0:512],
                                             in1=p7_sbc[:, 0:512], op=ALU.mult)
                        ins = vector.tensor_tensor(out=p7_fa[:, 512:1024], in0=pY2[:, 512:1024],
                                                   in1=p7_sbc[:, 512:1024], op=ALU.mult)
                        inc(ins, "sv", 1)
                        ins = vector.tensor_tensor(out=p7_fb[:], in0=p7_fb[:], in1=p7_fa[:],
                                                   op=ALU.add)
                        inc(ins, "sv", 1)
                        ins = vector.tensor_tensor(out=p7_fb[:], in0=p7_fb[:], in1=p7_bbc[:],
                                                   op=ALU.add)
                        inc(ins, "sv", 1)
                        if tt >= 2:
                            wtag(vector, ("p7o", tt - 2))
                        ins = vector.tensor_tensor(out=p7_out[:, off * D:(off + 1) * D],
                                                   in0=p7_fb[:],
                                                   in1=p7_x[:, off * D:(off + 1) * D],
                                                   op=ALU.add)
                        inc(ins, "sv", 2)

            _ps.close()
        barrier("p7")

        @block.gpsimd
        def _(gpsimd):
            for n in DMA_SEMS:
                if S[n] > 0:
                    gpsimd.wait_ge(sems[n], S[n])

    import os
    if os.environ.get("MK_PRINT_S"):
        print("FINAL S:", S)
        print("n instructions:", sum(1 for _ in nc.all_instructions()))
    return nc


def _prep(inputs):
    return {
        "WinT": np.ascontiguousarray(inputs["L0_attn_W_in"].T),
        "WdtT": np.ascontiguousarray(inputs["L0_attn_W_dt"].T),
        "WoutT": np.ascontiguousarray(inputs["L0_attn_W_out"].T),
        "CBT": np.ascontiguousarray(inputs["L0_ffn_CB"].T),
        "A_logr": np.ascontiguousarray(inputs["L0_attn_A_log"]),
        "bdt_c": np.ascontiguousarray(inputs["L0_attn_b_dt"].reshape(NJ, 128).T),
        "convw_c": np.ascontiguousarray(
            inputs["L0_attn_conv_w"][:, 0, :].reshape(NJ, 128, 4).transpose(1, 0, 2).reshape(128, NJ * 4)),
        "convb_c": np.ascontiguousarray(inputs["L0_attn_conv_b"].reshape(NJ, 128).T),
        "D_c": np.ascontiguousarray(inputs["L0_attn_D"].reshape(NJ, 128).T),
        "selPC": np.kron(np.eye(NS, dtype=np.float32), np.ones((1, 128), np.float32)),
        "selNC": -np.kron(np.eye(NS, dtype=np.float32), np.ones((1, 128), np.float32)),
        "ident": np.eye(128, dtype=np.float32),
        "ones1": np.ones((1, 128), np.float32),
        "lnrow": np.concatenate([inputs["L0_attn_ln_g"], inputs["L0_attn_ln_b"],
                                 inputs["L0_ffn_SCB"], inputs["L0_ffn_bias"]])[None, :].astype(np.float32),
    }


def kernel(**inputs):
    shared = _prep(inputs)
    xfull = np.ascontiguousarray(np.asarray(inputs["L0_attn_x"], dtype=np.float32))
    nc = build_nc()
    in_maps = []
    for b in range(8):
        m = dict(shared)
        m["x"] = np.ascontiguousarray(xfull[b])
        in_maps.append(m)
    res = run_bass_kernel_spmd(nc, in_maps, core_ids=list(range(8)))
    return np.stack([r["out"] for r in res.results], axis=0)
